# version 02
# baseline (speedup 1.0000x reference)
import numpy as np
from contextlib import ExitStack
import concourse.bass as bass
import concourse.mybir as mybir
from concourse.bass_utils import run_bass_kernel_spmd
import ml_dtypes

F32 = mybir.dt.float32
BF16 = mybir.dt.bfloat16
AF = mybir.ActivationFunctionType
ALU = mybir.AluOpType
AX = mybir.AxisListType

B = 4
D = 1024
H = 16
HD = 64
EPS = 1e-5


def my_chunks(par, nchunk):
    return [i for i in range(nchunk) if (i % 4 in ((0, 3) if par == 0 else (1, 2)))]


def np_bf16(x):
    return np.ascontiguousarray(x).astype(ml_dtypes.bfloat16)


class _Auto:

    def __init__(self, ctx, e, sem):
        object.__setattr__(self, "_ctx", ctx)
        object.__setattr__(self, "_e", e)
        object.__setattr__(self, "_sem", sem)

    def __getattr__(self, name):
        attr = getattr(self._e, name)
        if name.startswith("wait") or not callable(attr):
            return attr

        def wrapped(*a, **k):
            c = self._ctx
            if c.cnt[self._sem] > 0:
                self._e.wait_ge(c.sems[self._sem], c.cnt[self._sem])
            ins = attr(*a, **k)
            c.inc(ins, self._sem)
            return ins
        return wrapped


class Ctx:
    def __init__(self, name):
        self.nc = bass.Bass("TRN2", target_bir_lowering=False, debug=False,
                            detect_race_conditions=False)
        self.es = ExitStack()
        self.cnt = {}
        self.sems = {}

    def dram_in(self, name, shape, dtype=F32):
        return self.nc.dram_tensor(name, list(shape), dtype, kind="ExternalInput").ap()

    def dram_out(self, name, shape, dtype=F32):
        return self.nc.dram_tensor(name, list(shape), dtype, kind="ExternalOutput").ap()

    def dram_tmp(self, name, shape, dtype=F32):
        return self.nc.dram_tensor(name, list(shape), dtype, kind="Internal").ap()

    def sb(self, name, shape, dtype=F32):
        return self.es.enter_context(self.nc.sbuf_tensor(name, list(shape), dtype))

    def ps(self, name, shape, dtype=F32):
        return self.es.enter_context(self.nc.psum_tensor(name, list(shape), dtype))

    def sem(self, name):
        h = self.es.enter_context(self.nc.semaphore(name))
        self.cnt[name] = 0
        self.sems[name] = h
        return h

    def start_block(self, consts=()):
        self.block = self.es.enter_context(self.nc.Block())
        if consts:
            consts = list(dict.fromkeys(consts))
            self.const_sb = self.sb("const_sb", (128, len(consts)))
            for k, v in enumerate(consts):
                self.nc.const_aps.register(F32, float(v), self.const_sb[:, k:k + 1])

            def f_init(e):
                for k, v in enumerate(consts):
                    e.memset(self.const_sb[:, k:k + 1], float(v))
            self.vec(f_init)

    def inc(self, instr, name, k=1):
        instr.then_inc(self.sems[name], k)
        self.cnt[name] += k
        return self.cnt[name]

    def n(self, name):
        return self.cnt[name]

    def close(self):
        self.es.close()

    def wait(self, e, name, val=None):
        e.wait_ge(self.sems[name], self.cnt[name] if val is None else val)

    def vec(self, f):
        self.block.vector(lambda e: f(_Auto(self, e, "dve")))

    def scal(self, f):
        self.block.scalar(lambda e: f(_Auto(self, e, "act")))

    def psum_guard(self, e):
        self.wait(e, "dve")
        self.wait(e, "act")


def emit_ln(ctx, x_ap, out_ap, width=D, gamma_beta=None):
    blk = ctx.block
    sm = ctx.ln_sm
    sq = ctx.ln_sq
    inv = 1.0 / width

    def f_sc1(e):
        e.activation(out=sq[:, 0:width], in_=x_ap, func=AF.Square)
    ctx.scal(f_sc1)
    a_sq = ctx.n("act")

    def f_v1(e):
        e.wait_ge(ctx.sems["act"], a_sq)
        e.reduce_sum(out=sm[:, 0:1], in_=x_ap, axis=AX.X)
        e.tensor_scalar(out=sm[:, 1:2], in0=sm[:, 0:1], scalar1=inv, scalar2=None, op0=ALU.mult)
        e.reduce_sum(out=sm[:, 2:3], in_=sq[:, 0:width], axis=AX.X)
        e.tensor_scalar(out=sm[:, 3:4], in0=sm[:, 2:3], scalar1=inv, scalar2=None, op0=ALU.mult)
        e.tensor_mul(out=sm[:, 4:5], in0=sm[:, 1:2], in1=sm[:, 1:2])
        e.tensor_sub(out=sm[:, 5:6], in0=sm[:, 3:4], in1=sm[:, 4:5])
    ctx.vec(f_v1)
    v_var = ctx.n("dve")

    def f_sc2(e):
        e.wait_ge(ctx.sems["dve"], v_var)
        e.activation(out=sm[:, 2:3], in_=sm[:, 5:6], func=AF.Ln, bias=EPS)
        e.activation(out=sm[:, 6:7], in_=sm[:, 2:3], func=AF.Exp, scale=-0.5)
    ctx.scal(f_sc2)
    a_rstd = ctx.n("act")

    def f_v2(e):
        e.wait_ge(ctx.sems["act"], a_rstd)
        e.scalar_tensor_tensor(out=sm[:, 7:8], in0=sm[:, 1:2], scalar=-1.0,
                                       in1=sm[:, 6:7], op0=ALU.mult, op1=ALU.mult)
    ctx.vec(f_v2)
    v_nmr = ctx.n("dve")

    if gamma_beta is None:
        def f_v3(e):
            e.tensor_scalar(out=out_ap, in0=x_ap, scalar1=sm[:, 6:7],
                                    scalar2=sm[:, 7:8], op0=ALU.mult, op1=ALU.add)
        ctx.vec(f_v3)
    else:
        g_ap, b_ap = gamma_beta

        def f_v3(e):
            e.tensor_scalar(out=sq[:, 0:width], in0=x_ap, scalar1=sm[:, 6:7],
                            scalar2=sm[:, 7:8], op0=ALU.mult, op1=ALU.add)
            e.tensor_mul(out=ctx.ln_sq2[:, 0:width], in0=sq[:, 0:width], in1=g_ap)
            e.tensor_add(out=out_ap, in0=ctx.ln_sq2[:, 0:width], in1=b_ap)
        ctx.vec(f_v3)


def ln_scratch(ctx, affine=False):
    ctx.ln_sq = ctx.sb("ln_sq", (128, D))
    ctx.ln_sm = ctx.sb("ln_sm", (128, 8))
    if affine:
        ctx.ln_sq2 = ctx.sb("ln_sq2", (128, D))


def emit_fence(ctx, engines=("scalar", "vector", "tensor")):
    blk = ctx.block
    tgts = [("dma", ctx.n("dma")), ("pe", ctx.n("pe")),
            ("dve", ctx.n("dve")), ("act", ctx.n("act"))]

    def f(e):
        for s, v in tgts:
            if v > 0:
                e.wait_ge(ctx.sems[s], v)
    for eng in engines:
        if eng == "vector":
            ctx.vec(f)
        elif eng == "scalar":
            ctx.scal(f)
        else:
            getattr(blk, eng)(f)


def emit_load_cast(ctx, dst_bf_ap, src_dram_ap, stage_ap):
    blk = ctx.block
    v0 = ctx.n("dve")

    def f_s(e):
        e.wait_ge(ctx.sems["dve"], v0)
        ctx.wait(e, "pe")
        e.dma_start(out=stage_ap, in_=src_dram_ap).then_inc(ctx.sems["dma"], 16)
        ctx.cnt["dma"] += 16
    blk.sync(f_s)
    dn = ctx.n("dma")

    def f_v(e):
        e.wait_ge(ctx.sems["dma"], dn)
        e.tensor_copy(out=dst_bf_ap, in_=stage_ap)
    ctx.vec(f_v)



def build_l1f(par, S, dbg=False):
    nchunk = S // 512
    chunks = my_chunks(par, nchunk)
    NCH = len(chunks)

    ctx = Ctx("l1f")
    x_in = ctx.dram_in("x_in", (S, D))
    m1_in = ctx.dram_in("m1", (D, 64))
    m2_in = ctx.dram_in("m2f", (64, 4096))
    mb_in = ctx.dram_in("mbf", (1, 4096))
    w1_in = ctx.dram_in("w1f", (D, 4096))
    w2_in = ctx.dram_in("w2pf", (4096, D))
    b2_in = ctx.dram_in("b2r", (1, D))
    ones_in = ctx.dram_in("ones_f", (1, 128))
    idbf_in = ctx.dram_in("id_f", (128, 128))
    x_out = ctx.dram_out("x_out", (NCH * 512, D))
    if dbg:
        dbg_out = ctx.dram_out("dbg", (128, 8 * 4096))

    wstage = ctx.sb("wstage", (128, 2048))
    w1_sb = ctx.sb("w1_sb", (128, 8 * 4096), BF16)
    m1_sb = ctx.sb("m1_sb", (128, 8 * 64), BF16)
    m2_sb = ctx.sb("m2_sb", (64, 4096), BF16)
    mb_sb = ctx.sb("mb_sb", (1, 4096), BF16)
    b2_sb = ctx.sb("b2_sb", (1, D), BF16)
    ones_sb = ctx.sb("ones_sb", (1, 128), BF16)
    idbf_sb = ctx.sb("idbf_sb", (128, 128), BF16)
    xc_sb = ctx.sb("xc_sb", (128, 4 * D))
    xnfm = ctx.sb("xnfm", (128, 8 * 512), BF16)
    xn_t = ctx.sb("xn_t", (128, D), BF16)
    lr_sb = ctx.sb("lr_sb", (64, 512), BF16)
    logit_sb = ctx.sb("logit_sb", (128, 512))
    rmax_sb = ctx.sb("rmax_sb", (128, 8))
    oh_sb = ctx.sb("oh_sb", (128, 4 * 512), BF16)
    midr_sb = ctx.sb("midr_sb", (128, 512), BF16)
    s_sb = ctx.sb("s_sb", (128, 32 * 512), BF16)
    w2bf = ctx.sb("w2bf", (128, 4 * D), BF16)
    resA = ctx.sb("resA", (128, 4 * D))
    resB = ctx.sb("resB", (128, 4 * D))
    outt = ctx.sb("outt", (128, D))
    ln_scratch(ctx)

    psA = ctx.ps("psA", (128, 512))
    psB = ctx.ps("psB", (128, 512))
    psT = ctx.ps("psT", (128, 1024), BF16)
    psO = ctx.ps("psO", (128, 512), BF16)

    for s in ("dma", "pe", "dve", "act", "st", "gp"):
        ctx.sem(s)
    dma, pe, dve, act, st = (ctx.sems[s] for s in ("dma", "pe", "dve", "act", "st"))
    ctx.start_block(consts=[EPS, -0.5])
    blk = ctx.block

    def dump(col, ap, w):
        if not dbg:
            return
        def f(e):
            ctx.wait(e, "dve"); ctx.wait(e, "act"); ctx.wait(e, "pe")
            e.dma_start(out=dbg_out[0:ap.shape[0], int(col * 4096): int(col * 4096) + w], in_=ap).then_inc(
                ctx.sems["gp"], 16)
            ctx.cnt["gp"] += 16
            e.wait_ge(ctx.sems["gp"], ctx.cnt["gp"])
        blk.gpsimd(f)

        def fw(e):
            e.wait_ge(ctx.sems["gp"], ctx.cnt["gp"])
        blk.sync(fw)
        blk.tensor(fw)
        ctx.vec(fw)
        ctx.scal(fw)

    def f_s0(e):
        e.dma_start(out=wstage[0:1, 0:128], in_=ones_in).then_inc(dma, 16)
        e.dma_start(out=wstage[0:1, 128:1152], in_=b2_in).then_inc(dma, 16)
        e.dma_start(out=wstage[:, 1280:1408], in_=idbf_in).then_inc(dma, 16)
        e.dma_start(out=wstage[64:65, 0:1024], in_=mb_in[:, 0:1024]).then_inc(dma, 16)
        e.dma_start(out=wstage[32:33, 0:1024], in_=mb_in[:, 1024:2048]).then_inc(dma, 16)
        ctx.cnt["dma"] += 16 * 5
    blk.sync(f_s0)
    d0 = ctx.n("dma")

    def f_v0(e):
        e.wait_ge(dma, d0)
        e.tensor_copy(out=ones_sb[:], in_=wstage[0:1, 0:128])
        e.tensor_copy(out=b2_sb[:], in_=wstage[0:1, 128:1152])
        e.tensor_copy(out=idbf_sb[:], in_=wstage[:, 1280:1408])
        e.tensor_copy(out=mb_sb[:, 0:1024], in_=wstage[64:65, 0:1024])
        e.tensor_copy(out=mb_sb[:, 1024:2048], in_=wstage[32:33, 0:1024])
    ctx.vec(f_v0)
    v0 = ctx.n("dve")

    def f_s0b(e):
        e.wait_ge(dve, v0)
        e.dma_start(out=wstage[64:65, 0:1024], in_=mb_in[:, 2048:3072]).then_inc(dma, 16)
        e.dma_start(out=wstage[32:33, 0:1024], in_=mb_in[:, 3072:4096]).then_inc(dma, 16)
        ctx.cnt["dma"] += 32
    blk.sync(f_s0b)
    d0b = ctx.n("dma")

    def f_v0b(e):
        e.wait_ge(dma, d0b)
        e.tensor_copy(out=mb_sb[:, 2048:3072], in_=wstage[64:65, 0:1024])
        e.tensor_copy(out=mb_sb[:, 3072:4096], in_=wstage[32:33, 0:1024])
    ctx.vec(f_v0b)
    v0 = ctx.n("dve")

    def f_s1(e):
        e.wait_ge(dve, v0)
        for dc in range(8):
            e.dma_start(out=wstage[:, dc * 64:(dc + 1) * 64],
                        in_=m1_in[dc * 128:(dc + 1) * 128, :]).then_inc(dma, 16)
        e.dma_start(out=wstage[0:64, 512:512 + 1536], in_=m2_in[:, 0:1536]).then_inc(dma, 16)
        e.dma_start(out=wstage[64:128, 512:512 + 1536], in_=m2_in[:, 1536:3072]).then_inc(dma, 16)
        ctx.cnt["dma"] += 16 * 10
    blk.sync(f_s1)
    d1 = ctx.n("dma")

    def f_v1(e):
        e.wait_ge(dma, d1)
        e.tensor_copy(out=m1_sb[:], in_=wstage[:, 0:512])
        e.tensor_copy(out=m2_sb[:, 0:1536], in_=wstage[0:64, 512:2048])
        e.tensor_copy(out=m2_sb[:, 1536:3072], in_=wstage[64:128, 512:2048])
    ctx.vec(f_v1)
    v1b = ctx.n("dve")

    def f_s1b(e):
        e.wait_ge(dve, v1b)
        e.dma_start(out=wstage[0:64, 0:1024], in_=m2_in[:, 3072:4096]).then_inc(dma, 16)
        ctx.cnt["dma"] += 16
    blk.sync(f_s1b)
    d1b = ctx.n("dma")

    def f_v1b(e):
        e.wait_ge(dma, d1b)
        e.tensor_copy(out=m2_sb[:, 3072:4096], in_=wstage[0:64, 0:1024])
    ctx.vec(f_v1b)

    for dc in range(8):
        for half in range(2):
            emit_load_cast(ctx, w1_sb[:, dc * 4096 + half * 2048: dc * 4096 + (half + 1) * 2048],
                           w1_in[dc * 128:(dc + 1) * 128, half * 2048:(half + 1) * 2048],
                           wstage[:])

    for ci, g in enumerate(chunks):
        for tt in range(4):
            pe_prev = ctx.n("pe")

            def f_s(e, g=g, tt=tt):
                ctx.wait(e, "dve")
                ctx.wait(e, "act")
                e.dma_start(out=xc_sb[:, tt * D:(tt + 1) * D],
                            in_=x_in[g * 512 + tt * 128: g * 512 + (tt + 1) * 128, :]
                            ).then_inc(dma, 16)
                ctx.cnt["dma"] += 16
            blk.sync(f_s)
            dn = ctx.n("dma")

            def f_fence(e, dn=dn, pe_prev=pe_prev):
                e.wait_ge(dma, dn)
                e.wait_ge(pe, pe_prev)
            ctx.scal(f_fence)
            ctx.vec(f_fence)

            emit_ln(ctx, xc_sb[:, tt * D:(tt + 1) * D], xn_t[:])
            a_ln = ctx.n("act")
            v_before = ctx.n("dve")
            if ci == 0 and tt == 0:
                dump(7, xn_t[:], 1024)
                dump(7.25, ctx.ln_sm[:], 8)
                dump(7.5, xc_sb[:, 0:D], 1024)
                dump(7.75, ctx.ln_sq[:, 0:D], 1024)

            def f_t(e, tt=tt, a_ln=a_ln):
                e.wait_ge(act, a_ln)
                e.wait_ge(dve, v_before)
                for k in range(8):
                    ins = e.transpose(psT[:, k * 128:(k + 1) * 128],
                                      xn_t[:, k * 128:(k + 1) * 128], idbf_sb[:])
                ctx.inc(ins, "pe")
            blk.tensor(f_t)
            p_tr = ctx.n("pe")

            def f_v(e, tt=tt, p_tr=p_tr):
                e.wait_ge(pe, p_tr)
                e.tensor_copy(
                    out=xnfm[:].rearrange("p (k t) -> p k t", t=512)[:, :, tt * 128:(tt + 1) * 128],
                    in_=psT[:].rearrange("p (k t) -> p k t", t=128))
            ctx.vec(f_v)

        v_fm = ctx.n("dve")
        if ci == 0:
            dump(0, xnfm[:], 4096)

        def f_t(e):
            ctx.psum_guard(e)
            for dc in range(8):
                ins = e.matmul(psA[0:64, :], m1_sb[:, dc * 64:(dc + 1) * 64],
                               xnfm[:, dc * 512:(dc + 1) * 512],
                               start=(dc == 0), stop=(dc == 7))
            ctx.inc(ins, "pe")
        blk.tensor(f_t)
        p_lr = ctx.n("pe")

        def f_v(e, p_lr=p_lr):
            e.wait_ge(pe, p_lr)
            e.tensor_copy(out=lr_sb[:], in_=psA[0:64, :])
        ctx.vec(f_v)
        v_lr = ctx.n("dve")
        if ci == 0:
            dump(1, lr_sb[:], 512)

        for jmc in range(8):
            for tt in range(4):
                ps_l = psA if tt % 2 == 0 else psB

                def f_t(e, jmc=jmc, tt=tt, ps_l=ps_l):
                    ctx.psum_guard(e)
                    e.matmul(ps_l[:], lr_sb[:, tt * 128:(tt + 1) * 128],
                             m2_sb[:, jmc * 512:(jmc + 1) * 512], start=True, stop=False)
                    ins = e.matmul(ps_l[:], ones_sb[:], mb_sb[:, jmc * 512:(jmc + 1) * 512],
                                   start=False, stop=True)
                    ctx.inc(ins, "pe")
                blk.tensor(f_t)
                p_l = ctx.n("pe")

                def f_v(e, tt=tt, ps_l=ps_l, p_l=p_l):
                    e.wait_ge(pe, p_l)
                    e.tensor_copy(out=logit_sb[:], in_=ps_l[:])
                    e.tensor_reduce(out=rmax_sb[:],
                                    in_=logit_sb[:].rearrange("p (j m) -> p j m", m=64),
                                    axis=AX.X, op=ALU.max)
                    for j in range(8):
                        ins = e.tensor_single_scalar(
                            out=oh_sb[:, tt * 512 + j * 64: tt * 512 + (j + 1) * 64],
                            in_=logit_sb[:, j * 64:(j + 1) * 64],
                            scalar=rmax_sb[:, j:j + 1], op=ALU.is_equal)
                ctx.vec(f_v)
            v_oh = ctx.n("dve")
            if ci == 0 and jmc == 0:
                dump(2, oh_sb[:], 2048)
                dump(3, logit_sb[:], 512)

            for jj in range(4):
                jc = jmc * 4 + jj

                def f_t1(e, jc=jc):
                    ctx.psum_guard(e)
                    for dc in range(8):
                        ins = e.matmul(psA[:],
                                       w1_sb[:, dc * 4096 + jc * 128: dc * 4096 + (jc + 1) * 128],
                                       xnfm[:, dc * 512:(dc + 1) * 512],
                                       start=(dc == 0), stop=(dc == 7))
                    ctx.inc(ins, "pe")
                blk.tensor(f_t1)
                p_mid = ctx.n("pe")

                def f_sc(e, p_mid=p_mid):
                    e.wait_ge(pe, p_mid)
                    e.activation(out=midr_sb[:], in_=psA[:], func=AF.Relu)
                ctx.scal(f_sc)
                a_midr = ctx.n("act")

                def f_t2(e, jj=jj, v_oh=v_oh):
                    e.wait_ge(dve, max(v_oh, ctx.n("dve")))
                    for tt in range(4):
                        ins = e.transpose(psO[:, tt * 128:(tt + 1) * 128],
                                          oh_sb[:, tt * 512 + jj * 128: tt * 512 + (jj + 1) * 128],
                                          idbf_sb[:])
                    ctx.inc(ins, "pe")
                blk.tensor(f_t2)
                p_oht = ctx.n("pe")

                def f_v(e, jc=jc, p_oht=p_oht, a_midr=a_midr):
                    e.wait_ge(pe, p_oht)
                    e.wait_ge(act, a_midr)
                    e.tensor_mul(out=s_sb[:, jc * 512:(jc + 1) * 512],
                                         in0=psO[:], in1=midr_sb[:])
                ctx.vec(f_v)

        v_s = ctx.n("dve")
        if ci == 0:
            dump(4, s_sb[:, 0:4096], 4096)
            dump(5, midr_sb[:], 512)

        for jcb in range(8):
            res_dst = resA if jcb % 2 == 0 else resB
            res_src = resB if jcb % 2 == 0 else resA
            for jj in range(4):
                jc = jcb * 4 + jj
                emit_load_cast(ctx, w2bf[:, jj * D:(jj + 1) * D],
                               w2_in[jc * 128:(jc + 1) * 128, :], wstage[:, 0:D])
            v_w2 = ctx.n("dve")

            for u in range(8):
                tt, dh = u // 2, u % 2
                ps_r = psA if u % 2 == 0 else psB

                def f_t(e, jcb=jcb, tt=tt, dh=dh, ps_r=ps_r, v_w2=v_w2):
                    e.wait_ge(dve, v_w2)
                    ctx.psum_guard(e)
                    first = True
                    if jcb == 0:
                        e.matmul(ps_r[:], ones_sb[:], b2_sb[:, dh * 512:(dh + 1) * 512],
                                 start=True, stop=False)
                        first = False
                    for jj in range(4):
                        jc = jcb * 4 + jj
                        ins = e.matmul(ps_r[:],
                                       s_sb[:, jc * 512 + tt * 128: jc * 512 + (tt + 1) * 128],
                                       w2bf[:, jj * D + dh * 512: jj * D + (dh + 1) * 512],
                                       start=first, stop=(jj == 3))
                        first = False
                    ctx.inc(ins, "pe")
                blk.tensor(f_t)
                p_r = ctx.n("pe")

                def f_v(e, jcb=jcb, tt=tt, dh=dh, ps_r=ps_r, p_r=p_r,
                        res_dst=res_dst, res_src=res_src):
                    e.wait_ge(pe, p_r)
                    dst = res_dst[:, tt * D + dh * 512: tt * D + (dh + 1) * 512]
                    if jcb == 0:
                        e.tensor_copy(out=dst, in_=ps_r[:])
                    else:
                        src = res_src[:, tt * D + dh * 512: tt * D + (dh + 1) * 512]
                        e.tensor_add(out=dst, in0=ps_r[:], in1=src)
                ctx.vec(f_v)

        if ci == 0:
            dump(6, resB[:, 0:4096], 4096)
        for tt in range(4):
            emit_fence(ctx, ("scalar", "vector"))
            emit_ln(ctx, resB[:, tt * D:(tt + 1) * D], ctx.ln_sq[:, 0:D])
            a_l = ctx.n("act")

            def f_v(e, tt=tt, a_l=a_l):
                e.wait_ge(act, a_l)
                e.wait_ge(st, ctx.cnt["st"])
                e.tensor_add(out=outt[:], in0=xc_sb[:, tt * D:(tt + 1) * D],
                                     in1=ctx.ln_sq[:, 0:D])
            ctx.vec(f_v)
            v_o = ctx.n("dve")

            def f_s(e, ci=ci, tt=tt, v_o=v_o):
                e.wait_ge(dve, v_o)
                e.dma_start(out=x_out[ci * 512 + tt * 128: ci * 512 + (tt + 1) * 128, :],
                            in_=outt[:]).then_inc(st, 16)
                ctx.cnt["st"] += 16
                if not (ci == len(chunks) - 1 and tt == 3):
                    pass
                e.wait_ge(st, ctx.cnt["st"])
            blk.sync(f_s)

    return ctx




def build_l0fa(par, S, upto=9, dbg=False):
    nchunk = S // 512
    chunks = my_chunks(par, nchunk)
    NCH = len(chunks)
    NT = S // 512
    SK = S // 128

    ctx = Ctx("l0fa")
    x_in = ctx.dram_in("x_in", (S, D))
    cwT_in = ctx.dram_in("cwT", (3 * D, D))
    cbP_in = ctx.dram_in("cbP", (128, 8))
    wqkvT_in = ctx.dram_in("wqkvT", (D, 3 * D))
    bqkvP_in = ctx.dram_in("bqkvP", (128, 24))
    woT_in = ctx.dram_in("woT", (D, D))
    bo_in = ctx.dram_in("bo_r", (1, D))
    bv_in = ctx.dram_in("bv_r", (1, D))
    g1_in = ctx.dram_in("g1t", (128, D))
    b1_in = ctx.dram_in("b1t", (128, D))
    ones_in = ctx.dram_in("ones_f", (1, 128))
    idbf_in = ctx.dram_in("id_f", (128, 128))
    x2p_out = ctx.dram_out("x_out", (NCH * 512, D))
    ctm_hbm = ctx.dram_tmp("ctm_hbm", (NCH * 512, D))
    if dbg:
        dbg_out = ctx.dram_out("dbg", (128, 10 * 4096))

    c_xln = 0
    c_vall = 0
    c_conv = max(8 * S, SK * 1040)
    c_qk = c_conv + 8 * S
    c_end = c_qk + max(2 * NCH * 512 + 2 * S, 8 * D)
    arena = ctx.sb("arena", (128, c_end), BF16)
    xlnfm = arena[:, c_xln:c_xln + 8 * S]
    v_all = arena[:, c_vall:c_vall + SK * 1040]
    conv_fm = arena[:, c_conv:c_conv + 8 * S]
    q_sb = arena[0:64, c_qk:c_qk + 2 * NCH * 512]
    k_sb = arena[0:64, c_qk + 2 * NCH * 512:c_qk + 2 * NCH * 512 + 2 * S]
    ydm_hbm = ctx.dram_tmp("ydm_hbm", (128, 8 * NCH * 512), BF16)

    wstage = ctx.sb("wstage", (128, 1280))
    xn_t = ctx.sb("xn_t", (128, D), BF16)
    vw_sb = arena[:, c_qk:c_qk + 8 * D]
    qkw_sb = ctx.sb("qkw_sb", (128, 2048), BF16)
    cw_sb = ctx.sb("cw_sb", (128, 256), BF16)
    woT_sb = arena[:, c_vall:c_vall + 8 * D]
    ones_sb = ctx.sb("ones_sb", (1, 128), BF16)
    ones32_sb = ctx.sb("ones32_sb", (1, 128))
    onescol = ctx.sb("onescol", (128, 1), BF16)
    idbf_sb = ctx.sb("idbf_sb", (128, 128), BF16)
    cbP_sb = ctx.sb("cbP_sb", (128, 8))
    bqkvP_sb = ctx.sb("bqkvP_sb", (128, 24))
    g1_sb = ctx.sb("g1_sb", (128, D))
    b1_sb = ctx.sb("b1_sb", (128, D))
    bo_sb = ctx.sb("bo_sb", (1, D), BF16)
    bv_sb = ctx.sb("bv_sb", (1, D), BF16)
    xt_sb = ctx.sb("xt_sb", (128, D))
    p_sb = ctx.sb("p_sb", (128, 2 * 512), BF16)
    sqt_sb = ctx.sb("sqt_sb", (64, 512), BF16)
    nrm_sb = ctx.sb("nrm_sb", (128, 8))
    rc_sb = ctx.sb("rc_sb", (1, 512))
    attn_sb = ctx.sb("attn_sb", (128, D))
    outt = ctx.sb("outt", (128, D))
    negb_sb = ctx.sb("negb_sb", (128, 1))
    yt_sb = ctx.sb("yt_sb", (64, 512), BF16)
    ydm_t = ctx.sb("ydm_t", (128, 1024), BF16)
    ln_scratch(ctx, affine=True)
    rcb_sb = ctx.ln_sq[0:64, 0:512]
    ctm_sb = ctx.ln_sq2[:, 0:D]

    psA = ctx.ps("psA", (128, 512))
    psB = ctx.ps("psB", (128, 512))
    psC = ctx.ps("psC", (128, 512))
    psD = ctx.ps("psD", (128, 512))
    psCtx = ctx.ps("psCtx", (128, 512))
    psT = ctx.ps("psT", (128, 1024), BF16)

    for sname in ("dma", "pe", "dve", "act", "st", "gp"):
        ctx.sem(sname)
    dma, pe, dve, act, st = (ctx.sems[sn] for sn in ("dma", "pe", "dve", "act", "st"))
    ctx.start_block(consts=[EPS, -0.5, 0.125, 0.5])
    blk = ctx.block

    def dump(col, ap, w):
        if not dbg:
            return
        def f(e):
            ctx.wait(e, "dve"); ctx.wait(e, "act"); ctx.wait(e, "pe")
            e.dma_start(out=dbg_out[0:ap.shape[0], int(col * 4096): int(col * 4096) + w],
                        in_=ap).then_inc(ctx.sems["gp"], 16)
            ctx.cnt["gp"] += 16
            e.wait_ge(ctx.sems["gp"], ctx.cnt["gp"])
        blk.gpsimd(f)

        def fw(e):
            e.wait_ge(ctx.sems["gp"], ctx.cnt["gp"])
        blk.sync(fw)
        blk.tensor(fw)
        ctx.vec(fw)
        ctx.scal(fw)

    def f_s0(e):
        e.dma_start(out=wstage[0:1, 0:128], in_=ones_in).then_inc(dma, 16)
        e.dma_start(out=wstage[0:1, 128:1152], in_=bo_in).then_inc(dma, 16)
        e.dma_start(out=wstage[:, 1152:1280], in_=idbf_in).then_inc(dma, 16)
        e.dma_start(out=cbP_sb[:], in_=cbP_in).then_inc(dma, 16)
        e.dma_start(out=bqkvP_sb[:], in_=bqkvP_in).then_inc(dma, 16)
        e.dma_start(out=g1_sb[:], in_=g1_in).then_inc(dma, 16)
        e.dma_start(out=b1_sb[:], in_=b1_in).then_inc(dma, 16)
        e.dma_start(out=wstage[64:65, 0:1024], in_=bv_in).then_inc(dma, 16)
        ctx.cnt["dma"] += 16 * 8
    blk.sync(f_s0)
    d0 = ctx.n("dma")

    def f_v0(e):
        e.wait_ge(dma, d0)
        e.tensor_copy(out=ones_sb[:], in_=wstage[0:1, 0:128])
        e.tensor_copy(out=ones32_sb[:], in_=wstage[0:1, 0:128])
        e.tensor_copy(out=bo_sb[:], in_=wstage[0:1, 128:1152])
        e.tensor_copy(out=idbf_sb[:], in_=wstage[:, 1152:1280])
        e.tensor_copy(out=bv_sb[:], in_=wstage[64:65, 0:1024])
        e.memset(onescol[:], 1.0)
    ctx.vec(f_v0)

    for dc in range(8):
        emit_load_cast(ctx, vw_sb[:, dc * D:(dc + 1) * D],
                       wqkvT_in[dc * 128:(dc + 1) * 128, 2048:3072], wstage[:, 0:D])

    if upto < 1:
        return ctx
    for tc in range(NT):
        for tt in range(4):
            def f_s(e, tc=tc, tt=tt):
                ctx.wait(e, "dve")
                ctx.wait(e, "act")
                e.dma_start(out=xt_sb[:],
                            in_=x_in[tc * 512 + tt * 128: tc * 512 + (tt + 1) * 128, :]
                            ).then_inc(dma, 16)
                ctx.cnt["dma"] += 16
            blk.sync(f_s)
            dn = ctx.n("dma")
            pe_prev = ctx.n("pe")

            def f_fence(e, dn=dn, pe_prev=pe_prev):
                e.wait_ge(dma, dn)
                e.wait_ge(pe, pe_prev)
            ctx.scal(f_fence)
            ctx.vec(f_fence)
            emit_ln(ctx, xt_sb[:], xn_t[:])
            a_ln = ctx.n("act")
            v_before = ctx.n("dve")

            def f_t(e, a_ln=a_ln, v_before=v_before):
                e.wait_ge(act, a_ln)
                e.wait_ge(dve, v_before)
                for k in range(8):
                    ins = e.transpose(psT[:, k * 128:(k + 1) * 128],
                                      xn_t[:, k * 128:(k + 1) * 128], idbf_sb[:])
                ctx.inc(ins, "pe")
            blk.tensor(f_t)
            p_tr = ctx.n("pe")

            def f_v(e, tc=tc, tt=tt, p_tr=p_tr):
                e.wait_ge(pe, p_tr)
                e.tensor_copy(
                    out=xlnfm.rearrange("p (k t) -> p k t", t=S)[:, :, tc * 512 + tt * 128: tc * 512 + (tt + 1) * 128],
                    in_=psT[:].rearrange("p (k t) -> p k t", t=128))
            ctx.vec(f_v)

    dump(0, xlnfm[:, 0:4096], 4096)
    dump(1, xlnfm[:, 4096:8192], 4096)
    if upto < 2:
        return ctx
    order = [(0, 1)] + [(ic, tau) for ic in range(8) for tau in (0, 1, 2) if not (ic == 0 and tau == 1)]
    NGRP = max(1, NT // 2)
    pslist = [psA, psB, psC, psD]
    for oc in range(8):
        for grp in range(0, NT, 4):
            tcs = list(range(grp, min(grp + 4, NT)))
            for idx, (ic, tau) in enumerate(order):
                slot = idx % 2
                emit_load_cast(ctx, cw_sb[:, slot * 128:(slot + 1) * 128],
                               cwT_in[tau * D + ic * 128: tau * D + (ic + 1) * 128,
                                      oc * 128:(oc + 1) * 128], wstage[:, 0:128])
                v_w = ctx.n("dve")

                def f_t(e, idx=idx, ic=ic, tau=tau, slot=slot, tcs=tcs, v_w=v_w):
                    e.wait_ge(dve, v_w)
                    if idx == 0:
                        ctx.psum_guard(e)
                    for ti, tc in enumerate(tcs):
                        ps = pslist[ti]
                        sh = tau - 1
                        lo = tc * 512 + sh
                        hi = lo + 512
                        po, qo = 0, 512
                        if lo < 0:
                            po = -lo; lo = 0
                        if hi > S:
                            qo = 512 - (hi - S); hi = S
                        ins = e.matmul(ps[:, po:qo],
                                       cw_sb[:, slot * 128:(slot + 1) * 128],
                                       xlnfm[:, ic * S + lo: ic * S + hi],
                                       start=(idx == 0), stop=(idx == len(order) - 1))
                    ctx.inc(ins, "pe")
                blk.tensor(f_t)
            p_c = ctx.n("pe")

            def f_v(e, oc=oc, tcs=tcs, p_c=p_c):
                e.wait_ge(pe, p_c)
                for ti, tc in enumerate(tcs):
                    e.tensor_scalar(out=conv_fm[:, oc * S + tc * 512: oc * S + (tc + 1) * 512],
                                    in0=pslist[ti][:], scalar1=cbP_sb[:, oc:oc + 1],
                                    scalar2=None, op0=ALU.add)
            ctx.vec(f_v)

    for ci, g in enumerate(chunks):
        for tt in range(4):
            v_prev = ctx.n("dve")

            def f_t(e, g=g, tt=tt, v_prev=v_prev):
                e.wait_ge(dve, v_prev)
                ctx.psum_guard(e)
                for k in range(8):
                    ins = e.transpose(psT[:, k * 128:(k + 1) * 128],
                                      conv_fm[:, k * S + g * 512 + tt * 128: k * S + g * 512 + (tt + 1) * 128],
                                      idbf_sb[:])
                ctx.inc(ins, "pe")
            blk.tensor(f_t)
            p_tr = ctx.n("pe")

            def f_v(e, p_tr=p_tr):
                e.wait_ge(pe, p_tr)
                e.wait_ge(ctx.sems["st"], ctx.cnt["st"])
                e.tensor_copy(out=ctm_sb[:], in_=psT[:])
            ctx.vec(f_v)
            v_ct = ctx.n("dve")

            def f_s(e, ci=ci, tt=tt, v_ct=v_ct):
                e.wait_ge(dve, v_ct)
                e.dma_start(out=ctm_hbm[ci * 512 + tt * 128: ci * 512 + (tt + 1) * 128, :],
                            in_=ctm_sb[:]).then_inc(st, 16)
                ctx.cnt["st"] += 16
                e.wait_ge(st, ctx.cnt["st"])
            blk.sync(f_s)

    dump(2, conv_fm[:, 0:4096], 4096)
    dump(3, conv_fm[:, 4096:8192], 4096)
    if upto < 3:
        return ctx
    def f_vm(e):
        ctx.wait(e, "pe")
        e.memset(v_all, 1.0)
    ctx.vec(f_vm)
    for kt in range(SK):
        for vh in range(2):
            ps = psA if (kt * 2 + vh) % 2 == 0 else psB

            def f_t(e, kt=kt, vh=vh, ps=ps):
                ctx.psum_guard(e)
                e.matmul(ps[:], ones_sb[:], bv_sb[:, vh * 512:(vh + 1) * 512],
                         start=True, stop=False)
                for dc in range(8):
                    ins = e.matmul(ps[:], conv_fm[:, dc * S + kt * 128: dc * S + (kt + 1) * 128],
                                   vw_sb[:, dc * D + vh * 512: dc * D + (vh + 1) * 512],
                                   start=False, stop=(dc == 7))
                ctx.inc(ins, "pe")
            blk.tensor(f_t)
            p_v = ctx.n("pe")

            def f_v(e, kt=kt, vh=vh, ps=ps, p_v=p_v):
                e.wait_ge(pe, p_v)
                e.tensor_copy(
                    out=v_all.rearrange("p (a b) -> p a b", b=65)[:, kt * 16 + vh * 8: kt * 16 + (vh + 1) * 8, 0:64],
                    in_=ps[:].rearrange("p (a b) -> p a b", b=64))
            ctx.vec(f_v)

    dump(4, v_all[:, 0:4096], 4096)
    if upto < 4:
        return ctx
    for hp in range(8):
        for dc in range(8):
            emit_load_cast(ctx, qkw_sb[:, dc * 128:(dc + 1) * 128],
                           wqkvT_in[dc * 128:(dc + 1) * 128, hp * 128:(hp + 1) * 128],
                           wstage[:, 0:128])
            emit_load_cast(ctx, qkw_sb[:, 1024 + dc * 128: 1024 + (dc + 1) * 128],
                           wqkvT_in[dc * 128:(dc + 1) * 128, 1024 + hp * 128: 1024 + (hp + 1) * 128],
                           wstage[:, 0:128])
        v_w = ctx.n("dve")

        for qc in range(NCH):
            g = chunks[qc]
            ps = psA if qc % 2 == 0 else psB

            def f_t(e, g=g, ps=ps, v_w=v_w):
                e.wait_ge(dve, v_w)
                ctx.psum_guard(e)
                for dc in range(8):
                    ins = e.matmul(ps[:], qkw_sb[:, dc * 128:(dc + 1) * 128],
                                   conv_fm[:, dc * S + g * 512: dc * S + (g + 1) * 512],
                                   start=(dc == 0), stop=(dc == 7))
                ctx.inc(ins, "pe")
            blk.tensor(f_t)
            p_q = ctx.n("pe")

            def f_v(e, hp=hp, qc=qc, ps=ps, p_q=p_q):
                e.wait_ge(pe, p_q)
                e.tensor_scalar(out=q_sb[:, qc * 512:(qc + 1) * 512], in0=ps[0:64, :],
                                scalar1=bqkvP_sb[0:64, hp:hp + 1], scalar2=None, op0=ALU.add)
                e.tensor_scalar(out=q_sb[:, NCH * 512 + qc * 512: NCH * 512 + (qc + 1) * 512],
                                in0=ps[64:128, :],
                                scalar1=bqkvP_sb[64:128, hp:hp + 1], scalar2=None, op0=ALU.add)
            ctx.vec(f_v)
        for tc in range(NT):
            ps = psA if tc % 2 == 0 else psB

            def f_t(e, tc=tc, ps=ps, v_w=v_w):
                e.wait_ge(dve, v_w)
                ctx.psum_guard(e)
                for dc in range(8):
                    ins = e.matmul(ps[:], qkw_sb[:, 1024 + dc * 128: 1024 + (dc + 1) * 128],
                                   conv_fm[:, dc * S + tc * 512: dc * S + (tc + 1) * 512],
                                   start=(dc == 0), stop=(dc == 7))
                ctx.inc(ins, "pe")
            blk.tensor(f_t)
            p_k = ctx.n("pe")

            def f_v(e, hp=hp, tc=tc, ps=ps, p_k=p_k):
                e.wait_ge(pe, p_k)
                e.tensor_scalar(out=k_sb[:, tc * 512:(tc + 1) * 512], in0=ps[0:64, :],
                                scalar1=bqkvP_sb[0:64, 8 + hp:8 + hp + 1], scalar2=None, op0=ALU.add)
                e.tensor_scalar(out=k_sb[:, S + tc * 512: S + (tc + 1) * 512], in0=ps[64:128, :],
                                scalar1=bqkvP_sb[64:128, 8 + hp:8 + hp + 1], scalar2=None, op0=ALU.add)
            ctx.vec(f_v)

        if hp == 0:
            dump(5, q_sb[:, 0:2 * NCH * 512], 2 * NCH * 512)
            dump(6, k_sb[:, 0:2 * S], 2 * S)
        for hh in range(2):
            qoff = hh * NCH * 512
            koff = hh * S
            h_glob = hp * 2 + hh
            def f_nrm_init(e):
                e.memset(nrm_sb[:, 0:2], 0.0)
            ctx.vec(f_nrm_init)
            for (src, ncols, col, off) in ((q_sb, NCH, 0, qoff), (k_sb, NT, 1, koff)):
                for cc in range(ncols):
                    a_prev = ctx.n("act")

                    def f_sc(e, src=src, cc=cc, off=off):
                        ctx.wait(e, "dve")
                        ctx.wait(e, "pe")
                        e.activation(out=sqt_sb[:], in_=src[:, off + cc * 512: off + (cc + 1) * 512],
                                     func=AF.Square)
                    ctx.scal(f_sc)
                    a_sq = ctx.n("act")

                    def f_t(e, a_sq=a_sq):
                        e.wait_ge(act, a_sq)
                        ctx.psum_guard(e)
                        ins = e.matmul(psD[0:1, :], onescol[0:64, :], sqt_sb[:],
                                       start=True, stop=True)
                        ctx.inc(ins, "pe")
                    blk.tensor(f_t)
                    p_n = ctx.n("pe")

                    def f_v(e, col=col, p_n=p_n):
                        e.wait_ge(pe, p_n)
                        e.reduce_max(out=nrm_sb[0:1, 4:5], in_=psD[0:1, :], axis=AX.X)
                        e.tensor_max(out=nrm_sb[0:1, col:col + 1], in0=nrm_sb[0:1, col:col + 1],
                                     in1=nrm_sb[0:1, 4:5])
                    ctx.vec(f_v)
            def f_v(e):
                e.tensor_mul(out=nrm_sb[0:1, 5:6], in0=nrm_sb[0:1, 0:1], in1=nrm_sb[0:1, 1:2])
            ctx.vec(f_v)
            v_n = ctx.n("dve")

            def f_sc(e, v_n=v_n):
                e.wait_ge(dve, v_n)
                e.activation(out=nrm_sb[0:1, 6:7], in_=nrm_sb[0:1, 5:6], func=AF.Ln, bias=EPS)
                e.activation(out=nrm_sb[0:1, 7:8], in_=nrm_sb[0:1, 6:7], func=AF.Exp, scale=0.5)
            ctx.scal(f_sc)
            a_b = ctx.n("act")

            def f_v(e, a_b=a_b):
                e.wait_ge(act, a_b)
                e.tensor_scalar(out=nrm_sb[0:1, 3:4], in0=nrm_sb[0:1, 7:8], scalar1=-0.125,
                                scalar2=None, op0=ALU.mult)
            ctx.vec(f_v)
            v_nb = ctx.n("dve")

            def f_t(e, v_nb=v_nb):
                e.wait_ge(dve, v_nb)
                ctx.psum_guard(e)
                ins = e.matmul(psD[:, 0:1], ones32_sb[:], nrm_sb[0:1, 3:4], start=True, stop=True)
                ctx.inc(ins, "pe")
            blk.tensor(f_t)
            p_nb = ctx.n("pe")

            def f_v(e, p_nb=p_nb):
                e.wait_ge(pe, p_nb)
                e.tensor_copy(out=negb_sb[:], in_=psD[:, 0:1])
            ctx.vec(f_v)
            v_negb = ctx.n("dve")

            for qc in range(NCH):
                for kt in range(SK):
                    psS = psA if kt % 2 == 0 else psB
                    slot = kt % 2
                    a_prev_exp = ctx.n("act")
                    p_prev_ctx = ctx.n("pe")

                    def f_t1(e, qoff=qoff, koff=koff, kt=kt, qc=qc, psS=psS):
                        ctx.psum_guard(e)
                        ins = e.matmul(psS[:], k_sb[:, koff + kt * 128: koff + (kt + 1) * 128],
                                       q_sb[:, qoff + qc * 512: qoff + (qc + 1) * 512],
                                       start=True, stop=True)
                        ctx.inc(ins, "pe")
                    blk.tensor(f_t1)
                    p_s = ctx.n("pe")

                    def f_sc(e, psS=psS, slot=slot, p_s=p_s, v_negb=v_negb):
                        e.wait_ge(pe, p_s)
                        e.wait_ge(dve, v_negb)
                        e.activation(out=p_sb[:, slot * 512:(slot + 1) * 512], in_=psS[:],
                                     func=AF.Exp, bias=negb_sb[:], scale=0.125)
                    ctx.scal(f_sc)
                    a_exp = ctx.n("act")

                    def f_t2(e, kt=kt, slot=slot, h_glob=h_glob, a_exp=a_exp):
                        e.wait_ge(act, a_exp)
                        if kt == 0:
                            ctx.wait(e, "dve")
                        ins = e.matmul(psCtx[0:65, :],
                                       v_all[:, kt * 1040 + h_glob * 65: kt * 1040 + (h_glob + 1) * 65],
                                       p_sb[:, slot * 512:(slot + 1) * 512],
                                       start=(kt == 0), stop=(kt == SK - 1))
                        ctx.inc(ins, "pe")
                    blk.tensor(f_t2)
                p_ctx = ctx.n("pe")

                def f_v(e, p_ctx=p_ctx):
                    e.wait_ge(pe, p_ctx)
                    e.reciprocal(out=rc_sb[:], in_=psCtx[64:65, :])
                ctx.vec(f_v)
                v_rc = ctx.n("dve")

                def f_t3(e, v_rc=v_rc):
                    e.wait_ge(dve, v_rc)
                    ctx.wait(e, "dve")
                    ins = e.matmul(psD[0:64, :], ones32_sb[:, 0:64], rc_sb[:], start=True, stop=True)
                    ctx.inc(ins, "pe")
                blk.tensor(f_t3)
                p_rcb = ctx.n("pe")

                def f_v2(e, p_rcb=p_rcb):
                    e.wait_ge(pe, p_rcb)
                    e.wait_ge(ctx.sems["st"], ctx.cnt["st"])
                    e.tensor_copy(out=rcb_sb[:], in_=psD[0:64, :])
                    e.tensor_mul(out=yt_sb[:], in0=psCtx[0:64, :], in1=rcb_sb[:])
                ctx.vec(f_v2)
                v_y = ctx.n("dve")

                def f_sty(e, hp=hp, hh=hh, qc=qc, v_y=v_y):
                    e.wait_ge(ctx.sems["dve"], v_y)
                    e.dma_start(out=ydm_hbm[hh * 64: hh * 64 + 64,
                                            hp * NCH * 512 + qc * 512: hp * NCH * 512 + (qc + 1) * 512],
                                in_=yt_sb[:]).then_inc(ctx.sems["st"], 16)
                    ctx.cnt["st"] += 16
                    e.wait_ge(ctx.sems["st"], ctx.cnt["st"])
                blk.sync(f_sty)

    if upto < 5:
        return ctx
    for rc in range(8):
        emit_load_cast(ctx, woT_sb[:, rc * D:(rc + 1) * D],
                       woT_in[rc * 128:(rc + 1) * 128, :], wstage[:, 0:D])
    for ci in range(NCH):
        for tt in range(4):
            def f_s(e, ci=ci, tt=tt):
                ctx.wait(e, "dve")
                ctx.wait(e, "act")
                e.dma_start(out=ctm_sb[:],
                            in_=ctm_hbm[ci * 512 + tt * 128: ci * 512 + (tt + 1) * 128, :]
                            ).then_inc(dma, 16)
                for hp in range(8):
                    e.dma_start(out=ydm_t[:, hp * 128:(hp + 1) * 128],
                                in_=ydm_hbm[:, hp * NCH * 512 + ci * 512 + tt * 128:
                                            hp * NCH * 512 + ci * 512 + (tt + 1) * 128]).then_inc(dma, 16)
                ctx.cnt["dma"] += 16 * 9
            blk.sync(f_s)
            d_ct = ctx.n("dma")

            for oc in range(2):
                ps = psA if oc == 0 else psB

                def f_t(e, ci=ci, tt=tt, oc=oc, ps=ps, d_ct=d_ct):
                    e.wait_ge(dma, d_ct)
                    ctx.psum_guard(e)
                    e.matmul(ps[:], ones_sb[:], bo_sb[:, oc * 512:(oc + 1) * 512],
                             start=True, stop=False)
                    for hp in range(8):
                        ins = e.matmul(
                            ps[:],
                            ydm_t[:, hp * 128:(hp + 1) * 128],
                            woT_sb[:, hp * 1024 + oc * 512: hp * 1024 + (oc + 1) * 512],
                            start=False, stop=(hp == 7))
                    ctx.inc(ins, "pe")
                blk.tensor(f_t)
            p_o = ctx.n("pe")

            def f_v(e, p_o=p_o, d_ct=d_ct):
                e.wait_ge(pe, p_o)
                e.wait_ge(dma, d_ct)
                e.wait_ge(ctx.sems["st"], ctx.cnt["st"])
                e.tensor_copy(out=attn_sb[:, 0:512], in_=psA[:])
                e.tensor_copy(out=attn_sb[:, 512:1024], in_=psB[:])
                e.tensor_add(out=outt[:], in0=attn_sb[:], in1=ctm_sb[:])
            ctx.vec(f_v)
            if ci == 0 and tt == 0:
                dump(8, attn_sb[:], 1024)
                dump(8.25, ctm_sb[:], 1024)

            if upto >= 6:
                emit_fence(ctx, ("scalar", "vector"))
                emit_ln(ctx, outt[:], attn_sb[:], gamma_beta=(g1_sb[:], b1_sb[:]))
            else:
                def f_cp(e):
                    e.tensor_copy(out=attn_sb[:], in_=outt[:])
                ctx.vec(f_cp)
            v_ln = ctx.n("dve")

            def f_st(e, ci=ci, tt=tt, v_ln=v_ln):
                e.wait_ge(dve, v_ln)
                e.dma_start(out=x2p_out[ci * 512 + tt * 128: ci * 512 + (tt + 1) * 128, :],
                            in_=attn_sb[:]).then_inc(st, 16)
                ctx.cnt["st"] += 16
                e.wait_ge(st, ctx.cnt["st"])
            blk.sync(f_st)

    return ctx




def build_l0fb(par, S, dbg=False):
    nchunk = S // 512
    chunks = my_chunks(par, nchunk)
    NCH = len(chunks)
    NTL = NCH * 4
    MY = NCH * 512

    ctx = Ctx("l0fb")
    x2p_in = ctx.dram_in("x2p", (MY, D))
    x1_in = ctx.dram_in("x1", (MY, D))
    gw_in = ctx.dram_in("gate_wP", (128, 64))
    gb_in = ctx.dram_in("gate_b", (1, 8))
    w1_in = ctx.dram_in("w1r", (8 * D, 4096))
    b1P_in = ctx.dram_in("b1P", (128, 256))
    w2_in = ctx.dram_in("w2r", (8 * 4096, D))
    b2_in = ctx.dram_in("b2r8", (8, D))
    g2_in = ctx.dram_in("g2t", (128, D))
    b2t_in = ctx.dram_in("b2t", (128, D))
    ones_in = ctx.dram_in("ones_f", (1, 128))
    idbf_in = ctx.dram_in("id_f", (128, 128))
    x_out = ctx.dram_out("x_out", (MY, D))
    if dbg:
        dbg_out = ctx.dram_out("dbg", (128, 8 * 4096))

    x2fm = ctx.sb("x2fm", (128, 8 * MY), BF16)
    acc = ctx.sb("acc", (128, NTL * D))
    probs_sb = ctx.sb("probs_sb", (128, NTL * 8))
    w1h_sb = ctx.sb("w1h_sb", (128, 8 * 1024), BF16)
    w2h_sb = ctx.sb("w2h_sb", (128, 8 * D), BF16)
    h_sb = ctx.sb("h_sb", (128, 8 * 512), BF16)
    wstage = ctx.sb("wstage", (128, 2048))
    gw_sb = ctx.sb("gw_sb", (128, 64), BF16)
    gb_sb = ctx.sb("gb_sb", (1, 8), BF16)
    b1P_sb = ctx.sb("b1P_sb", (128, 256))
    b2row_sb = ctx.sb("b2row_sb", (1, D), BF16)
    g2_sb = ctx.sb("g2_sb", (128, D))
    b2t_sb = ctx.sb("b2t_sb", (128, D))
    ones_sb = ctx.sb("ones_sb", (1, 128), BF16)
    idbf_sb = ctx.sb("idbf_sb", (128, 128), BF16)
    xt_sb = ctx.sb("xt_sb", (128, D))
    xn_t = ctx.sb("xn_t", (128, D), BF16)
    sm8 = ctx.sb("sm8", (128, 8))
    ffn_sb = ctx.sb("ffn_sb", (128, D))
    ln_scratch(ctx, affine=True)
    x1t_sb = wstage[:, 0:D]
    outt = wstage[:, D:2 * D]

    psA = ctx.ps("psA", (128, 512))
    psB = ctx.ps("psB", (128, 512))
    psC = ctx.ps("psC", (128, 512))
    psD = ctx.ps("psD", (128, 512))
    psT = ctx.ps("psT", (128, 1024), BF16)

    for sname in ("dma", "pe", "dve", "act", "st", "gp"):
        ctx.sem(sname)
    dma, pe, dve, act, st = (ctx.sems[sn] for sn in ("dma", "pe", "dve", "act", "st"))
    ctx.start_block(consts=[EPS, -0.5])
    blk = ctx.block

    def dump(col, ap, w):
        if not dbg:
            return
        def f(e):
            ctx.wait(e, "dve"); ctx.wait(e, "act"); ctx.wait(e, "pe")
            e.dma_start(out=dbg_out[0:ap.shape[0], int(col * 4096): int(col * 4096) + w],
                        in_=ap).then_inc(ctx.sems["gp"], 16)
            ctx.cnt["gp"] += 16
            e.wait_ge(ctx.sems["gp"], ctx.cnt["gp"])
        blk.gpsimd(f)

        def fw(e):
            e.wait_ge(ctx.sems["gp"], ctx.cnt["gp"])
        blk.sync(fw)
        blk.tensor(fw)
        ctx.vec(fw)
        ctx.scal(fw)

    def f_s0(e):
        e.dma_start(out=wstage[0:1, 0:128], in_=ones_in).then_inc(dma, 16)
        e.dma_start(out=wstage[0:1, 128:136], in_=gb_in).then_inc(dma, 16)
        e.dma_start(out=wstage[:, 256:384], in_=idbf_in).then_inc(dma, 16)
        e.dma_start(out=wstage[:, 384:448], in_=gw_in).then_inc(dma, 16)
        e.dma_start(out=b1P_sb[:], in_=b1P_in).then_inc(dma, 16)
        e.dma_start(out=g2_sb[:], in_=g2_in).then_inc(dma, 16)
        e.dma_start(out=b2t_sb[:], in_=b2t_in).then_inc(dma, 16)
        ctx.cnt["dma"] += 16 * 7
    blk.sync(f_s0)
    d0 = ctx.n("dma")

    def f_v0(e):
        e.wait_ge(dma, d0)
        e.tensor_copy(out=ones_sb[:], in_=wstage[0:1, 0:128])
        e.tensor_copy(out=gb_sb[:], in_=wstage[0:1, 128:136])
        e.tensor_copy(out=idbf_sb[:], in_=wstage[:, 256:384])
        e.tensor_copy(out=gw_sb[:], in_=wstage[:, 384:448])
        e.memset(acc[:], 0.0)
    ctx.vec(f_v0)

    for mt in range(NTL):
        def f_s(e, mt=mt):
            ctx.wait(e, "dve")
            ctx.wait(e, "act")
            e.dma_start(out=xt_sb[:], in_=x2p_in[mt * 128:(mt + 1) * 128, :]).then_inc(dma, 16)
            ctx.cnt["dma"] += 16
        blk.sync(f_s)
        dn = ctx.n("dma")
        pe_prev = ctx.n("pe")

        def f_fence(e, dn=dn, pe_prev=pe_prev):
            e.wait_ge(dma, dn)
            e.wait_ge(pe, pe_prev)
        ctx.vec(f_fence)

        def f_cast(e):
            e.tensor_copy(out=xn_t[:], in_=xt_sb[:])
        ctx.vec(f_cast)
        v_c = ctx.n("dve")

        def f_t(e, mt=mt, v_c=v_c):
            e.wait_ge(dve, v_c)
            ctx.psum_guard(e)
            for k in range(8):
                ins = e.transpose(psT[:, k * 128:(k + 1) * 128],
                                  xn_t[:, k * 128:(k + 1) * 128], idbf_sb[:])
            ctx.inc(ins, "pe")
        blk.tensor(f_t)
        p_tr = ctx.n("pe")

        def f_v(e, mt=mt, p_tr=p_tr):
            e.wait_ge(pe, p_tr)
            e.tensor_copy(
                out=x2fm[:].rearrange("p (k t) -> p k t", t=MY)[:, :, mt * 128:(mt + 1) * 128],
                in_=psT[:].rearrange("p (k t) -> p k t", t=128))
        ctx.vec(f_v)
        v_fm = ctx.n("dve")

        def f_tg(e, mt=mt, v_fm=v_fm):
            e.wait_ge(dve, v_fm)
            ctx.psum_guard(e)
            e.matmul(psA[:, 0:8], ones_sb[:], gb_sb[:], start=True, stop=False)
            for dc in range(8):
                ins = e.matmul(psA[:, 0:8], x2fm[:, dc * MY + mt * 128: dc * MY + (mt + 1) * 128],
                               gw_sb[:, dc * 8:(dc + 1) * 8], start=False, stop=(dc == 7))
            ctx.inc(ins, "pe")
        blk.tensor(f_tg)
        p_g = ctx.n("pe")

        def f_vg(e, mt=mt, p_g=p_g):
            e.wait_ge(pe, p_g)
            e.tensor_copy(out=sm8[:], in_=psA[:, 0:8])
            e.reduce_max(out=ctx.ln_sm[:, 0:1], in_=sm8[:], axis=AX.X, negate=True)
        ctx.vec(f_vg)
        v_m = ctx.n("dve")

        def f_sg(e, v_m=v_m):
            e.wait_ge(dve, v_m)
            e.activation(out=ctx.ln_sq[:, 0:8], in_=sm8[:], func=AF.Exp, bias=ctx.ln_sm[:, 0:1])
        ctx.scal(f_sg)
        a_e = ctx.n("act")

        def f_vg2(e, mt=mt, a_e=a_e):
            e.wait_ge(act, a_e)
            e.reduce_sum(out=ctx.ln_sm[:, 1:2], in_=ctx.ln_sq[:, 0:8], axis=AX.X)
            e.reciprocal(out=ctx.ln_sm[:, 2:3], in_=ctx.ln_sm[:, 1:2])
            e.tensor_scalar(out=probs_sb[:, mt * 8:(mt + 1) * 8], in0=ctx.ln_sq[:, 0:8],
                            scalar1=ctx.ln_sm[:, 2:3], scalar2=None, op0=ALU.mult)
        ctx.vec(f_vg2)

    if dbg:
        dump(0, x2fm[:, 0:4096], 4096)
        dump(1, probs_sb[:], NTL * 8)

    for e_i in range(8):
        def f_sb2(e, e_i=e_i):
            ctx.wait(e, "pe")
            ctx.wait(e, "dve")
            e.dma_start(out=wstage[0:1, 0:1024], in_=b2_in[e_i:e_i + 1, :]).then_inc(dma, 16)
            ctx.cnt["dma"] += 16
        blk.sync(f_sb2)
        d_b2 = ctx.n("dma")

        def f_vb2(e, d_b2=d_b2):
            e.wait_ge(dma, d_b2)
            e.tensor_copy(out=b2row_sb[:], in_=wstage[0:1, 0:1024])
        ctx.vec(f_vb2)

        for fh in range(4):
            for dc in range(8):
                emit_load_cast(ctx, w1h_sb[:, dc * 1024:(dc + 1) * 1024],
                               w1_in[e_i * D + dc * 128: e_i * D + (dc + 1) * 128,
                                     fh * 1024:(fh + 1) * 1024], wstage[:, 0:D])
            for fc in range(8):
                emit_load_cast(ctx, w2h_sb[:, fc * D:(fc + 1) * D],
                               w2_in[e_i * 4096 + fh * 1024 + fc * 128:
                                     e_i * 4096 + fh * 1024 + (fc + 1) * 128, :],
                               wstage[:, 0:D])
            v_w = ctx.n("dve")

            for tcg in range(NCH):
                for fc in range(8):
                    ps = psA if fc % 2 == 0 else psB

                    def f_t(e, e_i=e_i, fh=fh, tcg=tcg, fc=fc, ps=ps, v_w=v_w):
                        e.wait_ge(dve, v_w)
                        ctx.psum_guard(e)
                        for dc in range(8):
                            ins = e.matmul(ps[:],
                                           w1h_sb[:, dc * 1024 + fc * 128: dc * 1024 + (fc + 1) * 128],
                                           x2fm[:, dc * MY + tcg * 512: dc * MY + (tcg + 1) * 512],
                                           start=(dc == 0), stop=(dc == 7))
                        ctx.inc(ins, "pe")
                    blk.tensor(f_t)
                    p_h = ctx.n("pe")

                    def f_sc(e, e_i=e_i, fh=fh, fc=fc, ps=ps, p_h=p_h):
                        e.wait_ge(pe, p_h)
                        e.activation(out=h_sb[:, fc * 512:(fc + 1) * 512], in_=ps[:],
                                     func=AF.Relu, bias=b1P_sb[:, e_i * 32 + fh * 8 + fc: e_i * 32 + fh * 8 + fc + 1])
                    ctx.scal(f_sc)
                a_h = ctx.n("act")

                for ts in range(4):
                    for dh in range(2):
                        ps = psC if (ts * 2 + dh) % 2 == 0 else psD

                        def f_t(e, e_i=e_i, fh=fh, tcg=tcg, ts=ts, dh=dh, ps=ps, a_h=a_h):
                            e.wait_ge(act, a_h)
                            ctx.psum_guard(e)
                            first = True
                            if fh == 0:
                                e.matmul(ps[:], ones_sb[:], b2row_sb[:, dh * 512:(dh + 1) * 512],
                                         start=True, stop=False)
                                first = False
                            for fc in range(8):
                                ins = e.matmul(ps[:],
                                               h_sb[:, fc * 512 + ts * 128: fc * 512 + (ts + 1) * 128],
                                               w2h_sb[:, fc * D + dh * 512: fc * D + (dh + 1) * 512],
                                               start=first, stop=(fc == 7))
                                first = False
                            ctx.inc(ins, "pe")
                        blk.tensor(f_t)
                        p_y = ctx.n("pe")

                        def f_v(e, e_i=e_i, tcg=tcg, ts=ts, dh=dh, ps=ps, p_y=p_y):
                            e.wait_ge(pe, p_y)
                            mt = tcg * 4 + ts
                            sl = acc[:, mt * D + dh * 512: mt * D + (dh + 1) * 512]
                            e.scalar_tensor_tensor(out=sl, in0=ps[:],
                                                   scalar=probs_sb[:, mt * 8 + e_i: mt * 8 + e_i + 1],
                                                   in1=sl, op0=ALU.mult, op1=ALU.add)
                        ctx.vec(f_v)

    if dbg:
        dump(2, acc[:, 0:4096], 4096)

    for mt in range(NTL):
        def f_s(e, mt=mt):
            ctx.wait(e, "dve")
            ctx.wait(e, "act")
            e.dma_start(out=xt_sb[:], in_=x2p_in[mt * 128:(mt + 1) * 128, :]).then_inc(dma, 16)
            e.dma_start(out=x1t_sb[:], in_=x1_in[mt * 128:(mt + 1) * 128, :]).then_inc(dma, 16)
            ctx.cnt["dma"] += 32
        blk.sync(f_s)
        dn = ctx.n("dma")

        def f_fence(e, dn=dn):
            e.wait_ge(dma, dn)
        ctx.vec(f_fence)
        ctx.scal(f_fence)

        def f_v(e, mt=mt):
            e.tensor_add(out=outt[:], in0=xt_sb[:], in1=acc[:, mt * D:(mt + 1) * D])
        ctx.vec(f_v)
        emit_fence(ctx, ("scalar", "vector"))
        emit_ln(ctx, outt[:], ffn_sb[:], gamma_beta=(g2_sb[:], b2t_sb[:]))
        emit_fence(ctx, ("scalar", "vector"))
        emit_ln(ctx, ffn_sb[:], ctx.ln_sq[:, 0:D])

        def f_v2(e, mt=mt):
            e.wait_ge(ctx.sems["st"], ctx.cnt["st"])
            e.tensor_add(out=outt[:], in0=x1t_sb[:], in1=ctx.ln_sq[:, 0:D])
        ctx.vec(f_v2)
        v_o = ctx.n("dve")

        def f_st(e, mt=mt, v_o=v_o):
            e.wait_ge(dve, v_o)
            e.dma_start(out=x_out[mt * 128:(mt + 1) * 128, :], in_=outt[:]).then_inc(st, 16)
            ctx.cnt["st"] += 16
            e.wait_ge(st, ctx.cnt["st"])
        blk.sync(f_st)

    return ctx


def l0fb_inmaps(x2p_rows, x1_rows, inputs, par, S):
    b1 = inputs["L0_ffn_b1"]
    return {
        "x2p": np.ascontiguousarray(x2p_rows),
        "x1": np.ascontiguousarray(x1_rows),
        "gate_wP": np.ascontiguousarray(inputs["L0_ffn_gate_w"].reshape(8, 128, 8).transpose(1, 0, 2).reshape(128, 64)),
        "gate_b": np.ascontiguousarray(inputs["L0_ffn_gate_b"].reshape(1, 8)),
        "w1r": np.ascontiguousarray(inputs["L0_ffn_w1"].reshape(8 * D, 4096)),
        "b1P": np.ascontiguousarray(b1.reshape(8, 32, 128).transpose(2, 0, 1).reshape(128, 256)),
        "w2r": np.ascontiguousarray(inputs["L0_ffn_w2"].reshape(8 * 4096, D)),
        "b2r8": np.ascontiguousarray(inputs["L0_ffn_b2"]),
        "g2t": np.ascontiguousarray(np.tile(inputs["L0_ffn_ln2_g"].reshape(1, D), (128, 1))),
        "b2t": np.ascontiguousarray(np.tile(inputs["L0_ffn_ln2_b"].reshape(1, D), (128, 1))),
        "ones_f": np.ones((1, 128), dtype=np.float32),
        "id_f": np.eye(128, dtype=np.float32),
    }




def build_l0a1(par, S, dbg=False):
    nchunk = S // 512
    chunks = my_chunks(par, nchunk)
    NCH = len(chunks)
    MY = NCH * 512
    NT = S // 512
    SK = S // 128

    ctx = Ctx("l0a1")
    x_in = ctx.dram_in("x_in", (S, D))
    waT_in = ctx.dram_in("waT", (D, 2 * D))
    baP_in = ctx.dram_in("baP", (128, 16))
    maskC_in = ctx.dram_in("maskC", (128, 2048))
    ones_in = ctx.dram_in("ones_f", (1, 128))
    idbf_in = ctx.dram_in("id_f", (128, 128))
    yar_out = ctx.dram_out("yar_dm", (128, 8 * MY))

    c_xln = 0
    c_vall = c_xln + 8 * S
    c_qk = c_vall + SK * 1040
    c_end = c_qk + 2 * MY + 2 * S
    arena = ctx.sb("arena", (128, c_end), BF16)
    xlnfm = arena[:, c_xln:c_xln + 8 * S]
    v_all = arena[:, c_vall:c_vall + SK * 1040]
    q_sb = arena[0:64, c_qk:c_qk + 2 * MY]
    k_sb = arena[0:64, c_qk + 2 * MY:c_qk + 2 * MY + 2 * S]

    wstage = ctx.sb("wstage", (128, 2048))
    xn_t = ctx.sb("xn_t", (128, D), BF16)
    qkw_sb = ctx.sb("qkw_sb", (128, 2048), BF16)
    maskC_sb = ctx.sb("maskC_sb", (128, 2048), BF16)
    ones_sb = ctx.sb("ones_sb", (1, 128), BF16)
    ones32_sb = ctx.sb("ones32_sb", (1, 128))
    onescol = ctx.sb("onescol", (128, 1), BF16)
    idbf_sb = ctx.sb("idbf_sb", (128, 128), BF16)
    baP_sb = ctx.sb("baP_sb", (128, 16))
    xt_sb = ctx.sb("xt_sb", (128, D))
    p_sb = ctx.sb("p_sb", (128, 2 * 512), BF16)
    p2_sb = ctx.sb("p2_sb", (128, 512), BF16)
    sqt_sb = ctx.sb("sqt_sb", (64, 512), BF16)
    nrm_sb = ctx.sb("nrm_sb", (128, 8))
    rc_sb = ctx.sb("rc_sb", (1, 512))
    rcb_sb = ctx.sb("rcb_sb", (64, 512))
    yt_sb = ctx.sb("yt_sb", (64, 512))
    negb_sb = ctx.sb("negb_sb", (128, 1))
    ln_scratch(ctx)

    psA = ctx.ps("psA", (128, 512))
    psB = ctx.ps("psB", (128, 512))
    psCtx = ctx.ps("psCtx", (128, 512))
    psD = ctx.ps("psD", (128, 512))
    psT = ctx.ps("psT", (128, 1024), BF16)

    for sname in ("dma", "pe", "dve", "act", "st", "gp"):
        ctx.sem(sname)
    dma, pe, dve, act, st = (ctx.sems[sn] for sn in ("dma", "pe", "dve", "act", "st"))
    ctx.start_block(consts=[EPS, -0.5, 0.125, 0.5])
    blk = ctx.block

    def f_s0(e):
        e.dma_start(out=wstage[0:1, 0:128], in_=ones_in).then_inc(dma, 16)
        e.dma_start(out=wstage[:, 128:256], in_=idbf_in).then_inc(dma, 16)
        e.dma_start(out=baP_sb[:], in_=baP_in).then_inc(dma, 16)
        e.dma_start(out=wstage[:, 256:256 + 2048 - 256], in_=maskC_in[:, 0:2048 - 256]).then_inc(dma, 16)
        ctx.cnt["dma"] += 16 * 4
    blk.sync(f_s0)
    d0 = ctx.n("dma")

    def f_v0(e):
        e.wait_ge(dma, d0)
        e.tensor_copy(out=ones_sb[:], in_=wstage[0:1, 0:128])
        e.tensor_copy(out=ones32_sb[:], in_=wstage[0:1, 0:128])
        e.tensor_copy(out=idbf_sb[:], in_=wstage[:, 128:256])
        e.tensor_copy(out=maskC_sb[:, 0:2048 - 256], in_=wstage[:, 256:2048])
        e.memset(onescol[:], 1.0)
    ctx.vec(f_v0)
    v0 = ctx.n("dve")

    def f_s0b(e):
        e.wait_ge(dve, v0)
        e.dma_start(out=wstage[:, 0:256], in_=maskC_in[:, 2048 - 256:2048]).then_inc(dma, 16)
        ctx.cnt["dma"] += 16
    blk.sync(f_s0b)
    d0b = ctx.n("dma")

    def f_v0b(e):
        e.wait_ge(dma, d0b)
        e.tensor_copy(out=maskC_sb[:, 2048 - 256:2048], in_=wstage[:, 0:256])
    ctx.vec(f_v0b)

    def f_vm(e):
        e.memset(v_all, 1.0)
    ctx.vec(f_vm)
    for kt in range(SK):
        def f_s(e, kt=kt):
            ctx.wait(e, "dve")
            ctx.wait(e, "act")
            e.dma_start(out=xt_sb[:], in_=x_in[kt * 128:(kt + 1) * 128, :]).then_inc(dma, 16)
            ctx.cnt["dma"] += 16
        blk.sync(f_s)
        dn = ctx.n("dma")
        pe_prev = ctx.n("pe")

        def f_fence(e, dn=dn, pe_prev=pe_prev):
            e.wait_ge(dma, dn)
            e.wait_ge(pe, pe_prev)
        ctx.scal(f_fence)
        ctx.vec(f_fence)
        emit_ln(ctx, xt_sb[:], xn_t[:])
        a_ln = ctx.n("act")
        v_before = ctx.n("dve")

        def f_t(e, a_ln=a_ln, v_before=v_before):
            e.wait_ge(act, a_ln)
            e.wait_ge(dve, v_before)
            for k in range(8):
                ins = e.transpose(psT[:, k * 128:(k + 1) * 128],
                                  xn_t[:, k * 128:(k + 1) * 128], idbf_sb[:])
            ctx.inc(ins, "pe")
        blk.tensor(f_t)
        p_tr = ctx.n("pe")

        def f_v(e, kt=kt, p_tr=p_tr):
            e.wait_ge(pe, p_tr)
            e.tensor_copy(
                out=xlnfm.rearrange("p (k t) -> p k t", t=S)[:, :, kt * 128:(kt + 1) * 128],
                in_=psT[:].rearrange("p (k t) -> p k t", t=128))
            e.tensor_copy(
                out=v_all.rearrange("p (a b) -> p a b", b=65)[:, kt * 16:(kt + 1) * 16, 0:64],
                in_=xn_t[:].rearrange("p (a b) -> p a b", b=64))
        ctx.vec(f_v)

    for hp in range(8):
        for dc in range(8):
            emit_load_cast(ctx, qkw_sb[:, dc * 128:(dc + 1) * 128],
                           waT_in[dc * 128:(dc + 1) * 128, hp * 128:(hp + 1) * 128],
                           wstage[:, 0:128])
            emit_load_cast(ctx, qkw_sb[:, 1024 + dc * 128: 1024 + (dc + 1) * 128],
                           waT_in[dc * 128:(dc + 1) * 128, 1024 + hp * 128: 1024 + (hp + 1) * 128],
                           wstage[:, 0:128])
        v_w = ctx.n("dve")

        for qc in range(NCH):
            g = chunks[qc]
            ps = psA if qc % 2 == 0 else psB

            def f_t(e, g=g, ps=ps, v_w=v_w):
                e.wait_ge(dve, v_w)
                ctx.psum_guard(e)
                for dc in range(8):
                    ins = e.matmul(ps[:], qkw_sb[:, dc * 128:(dc + 1) * 128],
                                   xlnfm[:, dc * S + g * 512: dc * S + (g + 1) * 512],
                                   start=(dc == 0), stop=(dc == 7))
                ctx.inc(ins, "pe")
            blk.tensor(f_t)
            p_q = ctx.n("pe")

            def f_v(e, hp=hp, qc=qc, ps=ps, p_q=p_q):
                e.wait_ge(pe, p_q)
                e.tensor_scalar(out=q_sb[:, qc * 512:(qc + 1) * 512], in0=ps[0:64, :],
                                scalar1=baP_sb[0:64, hp:hp + 1], scalar2=None, op0=ALU.add)
                e.tensor_scalar(out=q_sb[:, MY + qc * 512: MY + (qc + 1) * 512],
                                in0=ps[64:128, :],
                                scalar1=baP_sb[64:128, hp:hp + 1], scalar2=None, op0=ALU.add)
            ctx.vec(f_v)
        for tc in range(NT):
            ps = psA if tc % 2 == 0 else psB

            def f_t(e, tc=tc, ps=ps, v_w=v_w):
                e.wait_ge(dve, v_w)
                ctx.psum_guard(e)
                for dc in range(8):
                    ins = e.matmul(ps[:], qkw_sb[:, 1024 + dc * 128: 1024 + (dc + 1) * 128],
                                   xlnfm[:, dc * S + tc * 512: dc * S + (tc + 1) * 512],
                                   start=(dc == 0), stop=(dc == 7))
                ctx.inc(ins, "pe")
            blk.tensor(f_t)
            p_k = ctx.n("pe")

            def f_v(e, hp=hp, tc=tc, ps=ps, p_k=p_k):
                e.wait_ge(pe, p_k)
                e.tensor_scalar(out=k_sb[:, tc * 512:(tc + 1) * 512], in0=ps[0:64, :],
                                scalar1=baP_sb[0:64, 8 + hp:8 + hp + 1], scalar2=None, op0=ALU.add)
                e.tensor_scalar(out=k_sb[:, S + tc * 512: S + (tc + 1) * 512], in0=ps[64:128, :],
                                scalar1=baP_sb[64:128, 8 + hp:8 + hp + 1], scalar2=None, op0=ALU.add)
            ctx.vec(f_v)

        for hh in range(2):
            qoff = hh * MY
            koff = hh * S
            h_glob = hp * 2 + hh

            def f_nrm_init(e):
                e.memset(nrm_sb[0:1, 0:2], 0.0)
            ctx.vec(f_nrm_init)
            for (src, ncols, col, off) in ((q_sb, NCH, 0, qoff), (k_sb, NT, 1, koff)):
                for cc in range(ncols):
                    def f_sc(e, src=src, cc=cc, off=off):
                        ctx.wait(e, "dve")
                        ctx.wait(e, "pe")
                        e.activation(out=sqt_sb[:], in_=src[:, off + cc * 512: off + (cc + 1) * 512],
                                     func=AF.Square)
                    ctx.scal(f_sc)
                    a_sq = ctx.n("act")

                    def f_t(e, a_sq=a_sq):
                        e.wait_ge(act, a_sq)
                        ctx.psum_guard(e)
                        ins = e.matmul(psD[0:1, :], onescol[0:64, :], sqt_sb[:],
                                       start=True, stop=True)
                        ctx.inc(ins, "pe")
                    blk.tensor(f_t)
                    p_n = ctx.n("pe")

                    def f_v(e, col=col, p_n=p_n):
                        e.wait_ge(pe, p_n)
                        e.reduce_max(out=nrm_sb[0:1, 4:5], in_=psD[0:1, :], axis=AX.X)
                        e.tensor_max(out=nrm_sb[0:1, col:col + 1], in0=nrm_sb[0:1, col:col + 1],
                                     in1=nrm_sb[0:1, 4:5])
                    ctx.vec(f_v)

            def f_v(e):
                e.tensor_mul(out=nrm_sb[0:1, 5:6], in0=nrm_sb[0:1, 0:1], in1=nrm_sb[0:1, 1:2])
            ctx.vec(f_v)
            v_n = ctx.n("dve")

            def f_sc(e, v_n=v_n):
                e.wait_ge(dve, v_n)
                e.activation(out=nrm_sb[0:1, 6:7], in_=nrm_sb[0:1, 5:6], func=AF.Ln, bias=EPS)
                e.activation(out=nrm_sb[0:1, 7:8], in_=nrm_sb[0:1, 6:7], func=AF.Exp, scale=0.5)
            ctx.scal(f_sc)
            a_b = ctx.n("act")

            def f_v(e, a_b=a_b):
                e.wait_ge(act, a_b)
                e.tensor_scalar(out=nrm_sb[0:1, 3:4], in0=nrm_sb[0:1, 7:8], scalar1=-0.125,
                                scalar2=None, op0=ALU.mult)
            ctx.vec(f_v)
            v_nb = ctx.n("dve")

            def f_t(e, v_nb=v_nb):
                e.wait_ge(dve, v_nb)
                ctx.psum_guard(e)
                ins = e.matmul(psD[:, 0:1], ones32_sb[:], nrm_sb[0:1, 3:4], start=True, stop=True)
                ctx.inc(ins, "pe")
            blk.tensor(f_t)
            p_nb = ctx.n("pe")

            def f_v(e, p_nb=p_nb):
                e.wait_ge(pe, p_nb)
                e.tensor_copy(out=negb_sb[:], in_=psD[:, 0:1])
            ctx.vec(f_v)
            v_negb = ctx.n("dve")

            for qc in range(NCH):
                g = chunks[qc]
                ktmax = g * 4 + 4
                for kt in range(ktmax):
                    psS = psA if kt % 2 == 0 else psB
                    slot = kt % 2
                    diag_r = kt - g * 4

                    def f_t1(e, qoff=qoff, koff=koff, kt=kt, qc=qc, psS=psS):
                        ctx.psum_guard(e)
                        ins = e.matmul(psS[:], k_sb[:, koff + kt * 128: koff + (kt + 1) * 128],
                                       q_sb[:, qoff + qc * 512: qoff + (qc + 1) * 512],
                                       start=True, stop=True)
                        ctx.inc(ins, "pe")
                    blk.tensor(f_t1)
                    p_s = ctx.n("pe")

                    def f_sc(e, psS=psS, slot=slot, p_s=p_s, v_negb=v_negb):
                        e.wait_ge(pe, p_s)
                        e.wait_ge(dve, v_negb)
                        e.activation(out=p_sb[:, slot * 512:(slot + 1) * 512], in_=psS[:],
                                     func=AF.Exp, bias=negb_sb[:], scale=0.125)
                    ctx.scal(f_sc)
                    a_exp = ctx.n("act")

                    rhs_off = None
                    if diag_r >= 0:
                        def f_vmask(e, slot=slot, diag_r=diag_r, a_exp=a_exp):
                            e.wait_ge(act, a_exp)
                            e.tensor_mul(out=p2_sb[:], in0=p_sb[:, slot * 512:(slot + 1) * 512],
                                         in1=maskC_sb[:, diag_r * 512:(diag_r + 1) * 512])
                        ctx.vec(f_vmask)
                    v_mask = ctx.n("dve")
                    a_exp2 = ctx.n("act")

                    def f_t2(e, kt=kt, slot=slot, h_glob=h_glob, a_exp2=a_exp2,
                             diag_r=diag_r, v_mask=v_mask, ktmax=ktmax):
                        e.wait_ge(act, a_exp2)
                        if diag_r >= 0:
                            e.wait_ge(dve, v_mask)
                        if kt == 0:
                            ctx.wait(e, "dve")
                        rhs = p2_sb[:] if diag_r >= 0 else p_sb[:, slot * 512:(slot + 1) * 512]
                        ins = e.matmul(psCtx[0:65, :],
                                       v_all[:, kt * 1040 + h_glob * 65: kt * 1040 + (h_glob + 1) * 65],
                                       rhs, start=(kt == 0), stop=(kt == ktmax - 1))
                        ctx.inc(ins, "pe")
                    blk.tensor(f_t2)
                p_ctx = ctx.n("pe")

                def f_v(e, p_ctx=p_ctx):
                    e.wait_ge(pe, p_ctx)
                    e.reciprocal(out=rc_sb[:], in_=psCtx[64:65, :])
                ctx.vec(f_v)
                v_rc = ctx.n("dve")

                def f_t3(e, v_rc=v_rc):
                    e.wait_ge(dve, v_rc)
                    ctx.wait(e, "dve")
                    ins = e.matmul(psD[0:64, :], ones32_sb[:, 0:64], rc_sb[:], start=True, stop=True)
                    ctx.inc(ins, "pe")
                blk.tensor(f_t3)
                p_rcb = ctx.n("pe")

                def f_v2(e, p_rcb=p_rcb):
                    e.wait_ge(pe, p_rcb)
                    e.wait_ge(ctx.sems["st"], ctx.cnt["st"])
                    e.tensor_copy(out=rcb_sb[:], in_=psD[0:64, :])
                    e.tensor_mul(out=yt_sb[:], in0=psCtx[0:64, :], in1=rcb_sb[:])
                ctx.vec(f_v2)
                v_y = ctx.n("dve")

                def f_st(e, hp=hp, hh=hh, qc=qc, v_y=v_y):
                    e.wait_ge(dve, v_y)
                    e.dma_start(out=yar_out[hh * 64: hh * 64 + 64,
                                            hp * MY + qc * 512: hp * MY + (qc + 1) * 512],
                                in_=yt_sb[:]).then_inc(st, 16)
                    ctx.cnt["st"] += 16
                    e.wait_ge(st, ctx.cnt["st"])
                blk.sync(f_st)

    return ctx


def l0a1_inmaps(x_b, inputs, par, S):
    wa = inputs["L0_attn_w_attn"]
    ba = inputs["L0_attn_b_attn"]
    maskC = np.zeros((128, 2048), np.float32)
    for r in range(4):
        for p in range(128):
            f0 = r * 128 + p
            if f0 < 512:
                maskC[p, r * 512 + f0:(r + 1) * 512] = 1.0
    return {
        "x_in": np.ascontiguousarray(x_b),
        "waT": np.ascontiguousarray(wa.T),
        "baP": np.ascontiguousarray(ba.reshape(16, 128).T),
        "maskC": maskC,
        "ones_f": np.ones((1, 128), dtype=np.float32),
        "id_f": np.eye(128, dtype=np.float32),
    }




def build_l0a2(par, S, dbg=False):
    nchunk = S // 512
    chunks = my_chunks(par, nchunk)
    NCH = len(chunks)
    MY = NCH * 512
    NT = S // 512
    SK = S // 128

    ctx = Ctx("l0a2")
    x_in = ctx.dram_in("x_in", (S, D))
    yar_in = ctx.dram_in("yar", (S, D))
    yarT_in = ctx.dram_in("yarT_my", (D, MY))
    waT_in = ctx.dram_in("waT", (D, 2 * D))
    baP_in = ctx.dram_in("baP", (128, 16))
    wk2T_in = ctx.dram_in("wk2T", (D, D))
    bk2P_in = ctx.dram_in("bk2P", (64, 16))
    wpT_in = ctx.dram_in("wpT", (D, D))
    bp_in = ctx.dram_in("bp_r", (1, D))
    maskS_in = ctx.dram_in("maskS", (128, 2048))
    ones_in = ctx.dram_in("ones_f", (1, 128))
    idbf_in = ctx.dram_in("id_f", (128, 128))
    x_out = ctx.dram_out("x_out", (MY, D))
    ydm_hbm = ctx.dram_tmp("ydm_hbm", (128, 8 * MY), BF16)
    if dbg:
        dbg_out = ctx.dram_out("dbg", (128, 8 * 4096))

    c_xln = 0
    c_e = c_xln + 8 * S
    c_qa = c_e + SK * 1024
    c_ka = c_qa + 2 * MY
    c_end = c_ka + 2 * S
    arena = ctx.sb("arena", (128, c_end), BF16)
    xlnfm = arena[:, c_xln:c_xln + 8 * S]
    e_all = arena[:, c_e:c_e + SK * 1024]
    qa_sb = arena[0:64, c_qa:c_qa + 2 * MY]
    ka_sb = arena[0:64, c_ka:c_ka + 2 * S]

    wstage = ctx.sb("wstage", (128, 1024))
    xn_a = ctx.sb("xn_a", (128, D), BF16)
    xn_b = ctx.sb("xn_b", (128, D), BF16)
    yar_a = ctx.sb("yar_a", (128, D))
    yar_b = ctx.sb("yar_b", (128, D))
    esh_sb = ctx.sb("esh_sb", (128, D), BF16)
    qkw_sb = ctx.sb("qkw_sb", (128, 2048), BF16)
    wpT_sb = arena[:, 0:8 * D]
    maskS_sb = ctx.sb("maskS_sb", (128, 2048), BF16)
    ones_sb = ctx.sb("ones_sb", (1, 128), BF16)
    onescol = ctx.sb("onescol", (128, 1), BF16)
    idbf_sb = ctx.sb("idbf_sb", (128, 128), BF16)
    baP_sb = ctx.sb("baP_sb", (128, 16))
    bk2n_sb = ctx.sb("bk2n_sb", (64, 16))
    bp_sb = ctx.sb("bp_sb", (1, D), BF16)
    xt_sb = ctx.sb("xt_sb", (128, D))
    ma_sb = ctx.sb("ma_sb", (128, 2 * 512), BF16)
    yart_sb = ctx.sb("yart_sb", (64, 512))
    ydm_t = ctx.sb("ydm_t", (128, 1024), BF16)
    yt_sb = ctx.sb("yt_sb", (64, 512), BF16)
    attn_sb = ctx.sb("attn_sb", (128, D))
    outt = ctx.sb("outt", (128, D))
    ln_scratch(ctx)
    t1_sb = ctx.ln_sq[:, 0:512]

    psA = ctx.ps("psA", (128, 512))
    psB = ctx.ps("psB", (128, 512))
    psCtx = ctx.ps("psCtx", (128, 512))
    psD = ctx.ps("psD", (128, 512))
    psT = ctx.ps("psT", (128, 1024), BF16)

    for sname in ("dma", "pe", "dve", "act", "st", "gp"):
        ctx.sem(sname)
    dma, pe, dve, act, st = (ctx.sems[sn] for sn in ("dma", "pe", "dve", "act", "st"))
    ctx.start_block(consts=[EPS, -0.5, -0.0025])
    blk = ctx.block

    def dump(col, ap, w):
        if not dbg:
            return
        def f(e):
            ctx.wait(e, "dve"); ctx.wait(e, "act"); ctx.wait(e, "pe")
            e.dma_start(out=dbg_out[0:ap.shape[0], int(col * 4096): int(col * 4096) + w],
                        in_=ap).then_inc(ctx.sems["gp"], 16)
            ctx.cnt["gp"] += 16
            e.wait_ge(ctx.sems["gp"], ctx.cnt["gp"])
        blk.gpsimd(f)

        def fw(e):
            e.wait_ge(ctx.sems["gp"], ctx.cnt["gp"])
        blk.sync(fw)
        blk.tensor(fw)
        ctx.vec(fw)
        ctx.scal(fw)

    def f_s0(e):
        e.dma_start(out=wstage[0:1, 0:128], in_=ones_in).then_inc(dma, 16)
        e.dma_start(out=wstage[:, 128:256], in_=idbf_in).then_inc(dma, 16)
        e.dma_start(out=baP_sb[:], in_=baP_in).then_inc(dma, 16)
        e.dma_start(out=bk2n_sb[:], in_=bk2P_in).then_inc(dma, 16)
        ctx.cnt["dma"] += 16 * 4
    blk.sync(f_s0)
    d0 = ctx.n("dma")

    def f_v0(e):
        e.wait_ge(dma, d0)
        e.tensor_copy(out=ones_sb[:], in_=wstage[0:1, 0:128])
        e.tensor_copy(out=idbf_sb[:], in_=wstage[:, 128:256])
        e.memset(onescol[:], 1.0)
        e.tensor_scalar(out=bk2n_sb[:], in0=bk2n_sb[:], scalar1=-0.0025,
                        scalar2=None, op0=ALU.mult)
    ctx.vec(f_v0)
    v0 = ctx.n("dve")

    def f_s0b(e):
        e.wait_ge(dve, v0)
        e.dma_start(out=wstage[:, 0:1024], in_=maskS_in[:, 0:1024]).then_inc(dma, 16)
        ctx.cnt["dma"] += 16
    blk.sync(f_s0b)
    d0b = ctx.n("dma")

    def f_v0b(e):
        e.wait_ge(dma, d0b)
        e.tensor_copy(out=maskS_sb[:, 0:1024], in_=wstage[:])
    ctx.vec(f_v0b)
    v0c = ctx.n("dve")

    def f_s0c(e):
        e.wait_ge(dve, v0c)
        e.dma_start(out=wstage[:, 0:1024], in_=maskS_in[:, 1024:2048]).then_inc(dma, 16)
        ctx.cnt["dma"] += 16
    blk.sync(f_s0c)
    d0c = ctx.n("dma")

    def f_v0c(e):
        e.wait_ge(dma, d0c)
        e.tensor_copy(out=maskS_sb[:, 1024:2048], in_=wstage[:])
    ctx.vec(f_v0c)
    v0d = ctx.n("dve")

    def f_s0d(e):
        e.wait_ge(dve, v0d)
        e.dma_start(out=wstage[0:1, 0:1024], in_=bp_in).then_inc(dma, 16)
        ctx.cnt["dma"] += 16
    blk.sync(f_s0d)
    d0d = ctx.n("dma")

    def f_v0d(e):
        e.wait_ge(dma, d0d)
        e.tensor_copy(out=bp_sb[:], in_=wstage[0:1, 0:1024])
    ctx.vec(f_v0d)

    for kt in range(SK):
        xn_cur = xn_a if kt % 2 == 0 else xn_b
        xn_prev = xn_b if kt % 2 == 0 else xn_a
        yar_cur = yar_a if kt % 2 == 0 else yar_b
        yar_prev = yar_b if kt % 2 == 0 else yar_a

        def f_s(e, kt=kt, yar_cur=yar_cur):
            ctx.wait(e, "dve")
            ctx.wait(e, "act")
            e.dma_start(out=xt_sb[:], in_=x_in[kt * 128:(kt + 1) * 128, :]).then_inc(dma, 16)
            e.dma_start(out=yar_cur[:], in_=yar_in[kt * 128:(kt + 1) * 128, :]).then_inc(dma, 16)
            ctx.cnt["dma"] += 32
        blk.sync(f_s)
        dn = ctx.n("dma")
        pe_prev = ctx.n("pe")

        def f_fence(e, dn=dn, pe_prev=pe_prev):
            e.wait_ge(dma, dn)
            e.wait_ge(pe, pe_prev)
        ctx.scal(f_fence)
        ctx.vec(f_fence)
        emit_ln(ctx, xt_sb[:], xn_cur[:])
        a_ln = ctx.n("act")
        v_ln = ctx.n("dve")

        def f_t(e, xn_cur=xn_cur, a_ln=a_ln, v_ln=v_ln):
            e.wait_ge(act, a_ln)
            e.wait_ge(dve, v_ln)
            for k in range(8):
                ins = e.transpose(psT[:, k * 128:(k + 1) * 128],
                                  xn_cur[:, k * 128:(k + 1) * 128], idbf_sb[:])
            ctx.inc(ins, "pe")
        blk.tensor(f_t)
        p_tr = ctx.n("pe")

        def f_v(e, kt=kt, p_tr=p_tr):
            e.wait_ge(pe, p_tr)
            e.tensor_copy(
                out=xlnfm.rearrange("p (k t) -> p k t", t=S)[:, :, kt * 128:(kt + 1) * 128],
                in_=psT[:].rearrange("p (k t) -> p k t", t=128))
        ctx.vec(f_v)
        v_fm = ctx.n("dve")

        if kt > 0:
            def f_g(e, kt=kt, xn_prev=xn_prev, xn_cur=xn_cur, v_fm=v_fm, v_ln=v_ln):
                e.wait_ge(dve, ctx.cnt["dve"])
                e.dma_start(out=esh_sb[0:127, :], in_=xn_prev[1:128, :]).then_inc(ctx.sems["gp"], 16)
                e.dma_start(out=esh_sb[127:128, :], in_=xn_cur[0:1, :]).then_inc(ctx.sems["gp"], 16)
                ctx.cnt["gp"] += 32
            blk.gpsimd(f_g)
            g_n = ctx.n("gp")

            def f_ve(e, kt=kt, yar_prev=yar_prev, g_n=g_n):
                e.wait_ge(ctx.sems["gp"], g_n)
                e.tensor_sub(out=e_all[:, (kt - 1) * 1024: kt * 1024],
                             in0=esh_sb[:], in1=yar_prev[:])
            ctx.vec(f_ve)
        if kt == SK - 1:
            def f_g2(e, xn_cur=xn_cur):
                e.wait_ge(ctx.sems["dve"], ctx.cnt["dve"])
                e.dma_start(out=esh_sb[0:127, :], in_=xn_cur[1:128, :]).then_inc(ctx.sems["gp"], 16)
                ctx.cnt["gp"] += 16
            blk.gpsimd(f_g2)
            g_n = ctx.n("gp")

            def f_ve2(e, kt=kt, yar_cur=yar_cur, g_n=g_n):
                e.wait_ge(ctx.sems["gp"], g_n)
                e.memset(esh_sb[:].rearrange("p m -> p m")[127:128, :], 0.0) if False else None
                e.tensor_sub(out=e_all[:, kt * 1024:(kt + 1) * 1024],
                             in0=esh_sb[:], in1=yar_cur[:])
            ctx.vec(f_ve2)

    if dbg:
        dump(0, e_all[:, 0:4096], 4096)
    for hp in range(8):
        for dc in range(8):
            emit_load_cast(ctx, qkw_sb[:, dc * 128:(dc + 1) * 128],
                           waT_in[dc * 128:(dc + 1) * 128, hp * 128:(hp + 1) * 128],
                           wstage[:, 0:128])
            emit_load_cast(ctx, qkw_sb[:, 1024 + dc * 128: 1024 + (dc + 1) * 128],
                           wk2T_in[dc * 128:(dc + 1) * 128, hp * 128:(hp + 1) * 128],
                           wstage[:, 0:128])
        v_w = ctx.n("dve")

        for qc in range(NCH):
            g = chunks[qc]
            ps = psA if qc % 2 == 0 else psB

            def f_t(e, g=g, ps=ps, v_w=v_w):
                e.wait_ge(dve, v_w)
                ctx.psum_guard(e)
                for dc in range(8):
                    ins = e.matmul(ps[:], qkw_sb[:, dc * 128:(dc + 1) * 128],
                                   xlnfm[:, dc * S + g * 512: dc * S + (g + 1) * 512],
                                   start=(dc == 0), stop=(dc == 7))
                ctx.inc(ins, "pe")
            blk.tensor(f_t)
            p_q = ctx.n("pe")

            def f_v(e, hp=hp, qc=qc, ps=ps, p_q=p_q):
                e.wait_ge(pe, p_q)
                for half in range(2):
                    src = ps[0:64, :] if half == 0 else ps[64:128, :]
                    bia = baP_sb[0:64, hp:hp + 1] if half == 0 else baP_sb[64:128, hp:hp + 1]
                    e.tensor_scalar(out=t1_sb[0:64, :], in0=src, scalar1=bia,
                                    scalar2=None, op0=ALU.add)
                    e.scalar_tensor_tensor(out=qa_sb[:, half * MY + qc * 512: half * MY + (qc + 1) * 512],
                                           in0=t1_sb[0:64, :], scalar=0.02, in1=t1_sb[0:64, :],
                                           op0=ALU.mult, op1=ALU.min)
            ctx.vec(f_v)
        for tc in range(NT):
            ps = psA if tc % 2 == 0 else psB

            def f_t(e, tc=tc, ps=ps, v_w=v_w):
                e.wait_ge(dve, v_w)
                ctx.psum_guard(e)
                for dc in range(8):
                    ins = e.matmul(ps[:], qkw_sb[:, 1024 + dc * 128: 1024 + (dc + 1) * 128],
                                   xlnfm[:, dc * S + tc * 512: dc * S + (tc + 1) * 512],
                                   start=(dc == 0), stop=(dc == 7))
                ctx.inc(ins, "pe")
            blk.tensor(f_t)
            p_k = ctx.n("pe")

            for half in range(2):
                def f_vc(e, ps=ps, p_k=p_k, half=half):
                    e.wait_ge(pe, p_k)
                    src = ps[0:64, :] if half == 0 else ps[64:128, :]
                    e.tensor_copy(out=t1_sb[0:64, 0:512], in_=src)
                ctx.vec(f_vc)
                v_c = ctx.n("dve")

                def f_sc(e, hp=hp, half=half, v_c=v_c):
                    e.wait_ge(dve, v_c)
                    e.activation(out=t1_sb[64:128, 0:512], in_=t1_sb[0:64, 0:512], func=AF.Exp,
                                 bias=bk2n_sb[:, 2 * hp + half: 2 * hp + half + 1], scale=-0.0025)
                ctx.scal(f_sc)
                a_e = ctx.n("act")

                def f_v(e, tc=tc, half=half, a_e=a_e):
                    e.wait_ge(act, a_e)
                    e.tensor_scalar(out=t1_sb[0:64, 0:512], in0=t1_sb[64:128, 0:512],
                                    scalar1=1.0, scalar2=None, op0=ALU.add)
                    e.reciprocal(out=t1_sb[64:128, 0:512], in_=t1_sb[0:64, 0:512])
                    e.tensor_scalar(out=ka_sb[:, half * S + tc * 512: half * S + (tc + 1) * 512],
                                    in0=t1_sb[64:128, 0:512], scalar1=0.125, scalar2=None, op0=ALU.mult)
                ctx.vec(f_v)

        if dbg and hp == 0:
            dump(1, qa_sb[:, 0:2 * MY], 2 * MY)
            dump(2, ka_sb[:, 0:2 * S], 2 * S)
        for hh in range(2):
            qoff = hh * MY
            koff = hh * S
            h_glob = hp * 2 + hh
            for qc in range(NCH):
                g = chunks[qc]
                ktmax = g * 4 + 4
                for kt in range(ktmax):
                    psS = psA if kt % 2 == 0 else psB
                    slot = kt % 2
                    diag_r = kt - g * 4

                    def f_t1(e, qoff=qoff, koff=koff, kt=kt, qc=qc, psS=psS):
                        ctx.psum_guard(e)
                        ins = e.matmul(psS[:], ka_sb[:, koff + kt * 128: koff + (kt + 1) * 128],
                                       qa_sb[:, qoff + qc * 512: qoff + (qc + 1) * 512],
                                       start=True, stop=True)
                        ctx.inc(ins, "pe")
                    blk.tensor(f_t1)
                    p_s = ctx.n("pe")

                    def f_v(e, psS=psS, slot=slot, p_s=p_s, diag_r=diag_r):
                        e.wait_ge(pe, p_s)
                        if diag_r >= 0:
                            e.tensor_mul(out=ma_sb[:, slot * 512:(slot + 1) * 512], in0=psS[:],
                                         in1=maskS_sb[:, diag_r * 512:(diag_r + 1) * 512])
                        else:
                            e.tensor_copy(out=ma_sb[:, slot * 512:(slot + 1) * 512], in_=psS[:])
                    ctx.vec(f_v)
                    v_ma = ctx.n("dve")

                    def f_t2(e, kt=kt, slot=slot, h_glob=h_glob, v_ma=v_ma, ktmax=ktmax):
                        e.wait_ge(dve, v_ma)
                        ins = e.matmul(psCtx[0:64, :],
                                       e_all[:, kt * 1024 + h_glob * 64: kt * 1024 + (h_glob + 1) * 64],
                                       ma_sb[:, slot * 512:(slot + 1) * 512],
                                       start=(kt == 0), stop=(kt == ktmax - 1))
                        ctx.inc(ins, "pe")
                    blk.tensor(f_t2)
                p_ctx = ctx.n("pe")

                def f_s(e, h_glob=h_glob, qc=qc):
                    ctx.wait(e, "dve")
                    e.dma_start(out=yart_sb[:],
                                in_=yarT_in[h_glob * 64:(h_glob + 1) * 64,
                                            qc * 512:(qc + 1) * 512]).then_inc(dma, 16)
                    ctx.cnt["dma"] += 16
                blk.sync(f_s)
                d_y = ctx.n("dma")

                def f_v2(e, p_ctx=p_ctx, d_y=d_y):
                    e.wait_ge(pe, p_ctx)
                    e.wait_ge(dma, d_y)
                    e.wait_ge(ctx.sems["st"], ctx.cnt["st"])
                    e.tensor_add(out=yt_sb[:], in0=psCtx[0:64, :], in1=yart_sb[:])
                ctx.vec(f_v2)
                v_y = ctx.n("dve")

                def f_st(e, hp=hp, hh=hh, qc=qc, v_y=v_y):
                    e.wait_ge(dve, v_y)
                    e.dma_start(out=ydm_hbm[hh * 64: hh * 64 + 64,
                                            hp * MY + qc * 512: hp * MY + (qc + 1) * 512],
                                in_=yt_sb[:]).then_inc(st, 16)
                    ctx.cnt["st"] += 16
                    e.wait_ge(st, ctx.cnt["st"])
                blk.sync(f_st)

    for rc in range(8):
        emit_load_cast(ctx, wpT_sb[:, rc * D:(rc + 1) * D],
                       wpT_in[rc * 128:(rc + 1) * 128, :], wstage[:, 0:D])
    for ci in range(NCH):
        g = chunks[ci]
        for tt in range(4):
            def f_s(e, ci=ci, g=g, tt=tt):
                ctx.wait(e, "dve")
                ctx.wait(e, "act")
                for hp in range(8):
                    e.dma_start(out=ydm_t[:, hp * 128:(hp + 1) * 128],
                                in_=ydm_hbm[:, hp * MY + ci * 512 + tt * 128:
                                            hp * MY + ci * 512 + (tt + 1) * 128]).then_inc(dma, 16)
                e.dma_start(out=xt_sb[:],
                            in_=x_in[g * 512 + tt * 128: g * 512 + (tt + 1) * 128, :]).then_inc(dma, 16)
                ctx.cnt["dma"] += 16 * 9
            blk.sync(f_s)
            dn = ctx.n("dma")

            for oc in range(2):
                ps = psA if oc == 0 else psB

                def f_t(e, oc=oc, ps=ps, dn=dn):
                    e.wait_ge(dma, dn)
                    ctx.psum_guard(e)
                    e.matmul(ps[:], ones_sb[:], bp_sb[:, oc * 512:(oc + 1) * 512],
                             start=True, stop=False)
                    e.matmul(ps[:], ones_sb[:], bp_sb[:, oc * 512:(oc + 1) * 512],
                             start=False, stop=False)
                    for hp in range(8):
                        ins = e.matmul(ps[:], ydm_t[:, hp * 128:(hp + 1) * 128],
                                       wpT_sb[:, hp * 1024 + oc * 512: hp * 1024 + (oc + 1) * 512],
                                       start=False, stop=(hp == 7))
                    ctx.inc(ins, "pe")
                blk.tensor(f_t)
            p_o = ctx.n("pe")

            def f_v(e, p_o=p_o):
                e.wait_ge(pe, p_o)
                e.wait_ge(ctx.sems["st"], ctx.cnt["st"])
                e.tensor_copy(out=attn_sb[:, 0:512], in_=psA[:])
                e.tensor_copy(out=attn_sb[:, 512:1024], in_=psB[:])
            ctx.vec(f_v)

            emit_fence(ctx, ("scalar", "vector"))
            emit_ln(ctx, attn_sb[:], ctx.ln_sq[:, 0:D])

            def f_v2(e):
                e.tensor_add(out=outt[:], in0=xt_sb[:], in1=ctx.ln_sq[:, 0:D])
            ctx.vec(f_v2)
            v_o = ctx.n("dve")

            def f_st(e, ci=ci, tt=tt, v_o=v_o):
                e.wait_ge(dve, v_o)
                e.dma_start(out=x_out[ci * 512 + tt * 128: ci * 512 + (tt + 1) * 128, :],
                            in_=outt[:]).then_inc(st, 16)
                ctx.cnt["st"] += 16
                e.wait_ge(st, ctx.cnt["st"])
            blk.sync(f_st)

    return ctx


def l0a2_inmaps(x_b, yar_b, inputs, par, S):
    nchunk = S // 512
    chunks = my_chunks(par, nchunk)
    myrows = np.concatenate([yar_b[g * 512:(g + 1) * 512] for g in chunks], axis=0)
    maskS = np.zeros((128, 2048), np.float32)
    for r in range(4):
        for p in range(128):
            f0 = r * 128 + p + 1
            if f0 < 512:
                maskS[p, r * 512 + f0:(r + 1) * 512] = 1.0
            elif f0 == 512:
                pass
    return {
        "x_in": np.ascontiguousarray(x_b),
        "yar": np.ascontiguousarray(yar_b),
        "yarT_my": np.ascontiguousarray(myrows.T),
        "waT": np.ascontiguousarray(inputs["L0_attn_w_attn"].T),
        "baP": np.ascontiguousarray(inputs["L0_attn_b_attn"].reshape(16, 128).T),
        "wk2T": np.ascontiguousarray(inputs["L0_attn_w_k2"].T),
        "bk2P": np.ascontiguousarray(inputs["L0_attn_b_k2"].reshape(16, 64).T),
        "wpT": np.ascontiguousarray(inputs["L0_attn_w_proj"].T),
        "bp_r": np.ascontiguousarray(inputs["L0_attn_b_proj"].reshape(1, D)),
        "maskS": maskS,
        "ones_f": np.ones((1, 128), dtype=np.float32),
        "id_f": np.eye(128, dtype=np.float32),
    }


def yar_assemble(results_by_par, S):
    nchunk = S // 512
    out = [np.zeros((S, D), np.float32) for _ in range(B)]
    for par, rs in results_by_par.items():
        chunks = my_chunks(par, nchunk)
        MY = len(chunks) * 512
        for b in range(B):
            ydm = rs[b]
            for hp in range(8):
                for hh in range(2):
                    h = hp * 2 + hh
                    blkm = ydm[hh * 64:(hh + 1) * 64, hp * MY:(hp + 1) * MY]
                    for ci, g in enumerate(chunks):
                        out[b][g * 512:(g + 1) * 512, h * 64:(h + 1) * 64] = \
                            blkm[:, ci * 512:(ci + 1) * 512].T
    return out




def build_l1a(par, S, dbg=False):
    nchunk = S // 512
    chunks = my_chunks(par, nchunk)
    NCH = len(chunks)
    MY = NCH * 512
    NT = S // 512
    SK = S // 128

    ctx = Ctx("l1a")
    x_in = ctx.dram_in("x_in", (S, D))
    wq_in = ctx.dram_in("Wq", (D, D))
    wk_in = ctx.dram_in("Wk", (D, D))
    wv_in = ctx.dram_in("Wv", (D, D))
    wpos_in = ctx.dram_in("Wpos", (D, D))
    wf_in = ctx.dram_in("Wf", (D, D))
    bqP_in = ctx.dram_in("bqP", (64, 16))
    bkP_in = ctx.dram_in("bkP", (64, 16))
    bv_in = ctx.dram_in("bv_r", (1, D))
    bf_in = ctx.dram_in("bf_r", (1, D))
    uP_in = ctx.dram_in("uP", (64, 16))
    vP_in = ctx.dram_in("vP", (64, 16))
    pef_in = ctx.dram_in("pef", (D, S))
    gt_in = ctx.dram_in("gt", (128, D))
    bt_in = ctx.dram_in("bt", (128, D))
    ones_in = ctx.dram_in("ones_f", (1, 128))
    idbf_in = ctx.dram_in("id_f", (128, 128))
    x_out = ctx.dram_out("x_out", (MY, D))
    F_hbm = ctx.dram_tmp("F_hbm", (S, S + 1), BF16)
    F_flat = F_hbm.rearrange("a b -> (a b)")
    ydm_hbm = ctx.dram_tmp("ydm_hbm", (128, 8 * MY), BF16)
    if dbg:
        dbg_out = ctx.dram_out("dbg", (128, 10 * 4096))

    c_xln = 0
    c_val = c_xln + 8 * S
    c_qu = c_val + SK * 130
    c_qv = c_qu + 2 * MY
    c_k = c_qv + 2 * S
    c_p = c_k + 2 * S
    c_bd = c_p + 2 * S
    c_end = c_bd + 4 * 1024 + 64
    arena = ctx.sb("arena", (128, c_end), BF16)
    xlnfm = arena[:, c_xln:c_xln + 8 * S]
    val_pair = arena[:, c_val:c_val + SK * 130]
    qu_sb = arena[0:64, c_qu:c_qu + 2 * MY]
    qv_sb = arena[0:64, c_qv:c_qv + 2 * S]
    k_sb = arena[0:64, c_k:c_k + 2 * S]
    p_sb = arena[0:64, c_p:c_p + 2 * S]
    bd_sb = arena[:, c_bd:c_bd + 4 * 1024]
    urow_sb = arena[:, c_bd:c_bd + S + 1]

    wstage = ctx.sb("wstage", (128, 1024))
    xn_t = ctx.sb("xn_t", (128, D), BF16)
    w4_sb = ctx.sb("w4_sb", (128, 4 * 1024), BF16)
    wf_sb = arena[:, 0:8 * D]
    pe_st = ctx.sb("pe_st", (128, 512), BF16)
    arena32 = ctx.sb("arena32", (128, 2048))
    bqP_sb = ctx.sb("bqP_sb", (64, 16))
    bkP_sb = ctx.sb("bkP_sb", (64, 16))
    uP_sb = ctx.sb("uP_sb", (64, 16))
    vP_sb = ctx.sb("vP_sb", (64, 16))
    bv_sb = ctx.sb("bv_sb", (1, D), BF16)
    bf_sb = ctx.sb("bf_sb", (1, D), BF16)
    gt_sb = arena32[:, 0:D]
    bt_sb = arena32[:, D:2 * D]
    ones_sb = ctx.sb("ones_sb", (1, 128), BF16)
    ones32_sb = ctx.sb("ones32_sb", (1, 128))
    onescol = ctx.sb("onescol", (128, 1), BF16)
    idbf_sb = ctx.sb("idbf_sb", (128, 128), BF16)
    id32_sb = ctx.sb("id32_sb", (128, 128))
    xt_sb = ctx.sb("xt_sb", (128, D))
    bdf_sb = ctx.sb("bdf_sb", (128, 512))
    pexp_sb = ctx.sb("pexp_sb", (128, 2 * 512), BF16)
    sqt_sb = ctx.sb("sqt_sb", (64, 512), BF16)
    nrm_sb = ctx.sb("nrm_sb", (128, 12))
    rc_sb = ctx.sb("rc_sb", (1, 512))
    yt_sb = ctx.sb("yt_sb", (64, 512), BF16)
    ydm_t = ctx.sb("ydm_t", (128, 1024), BF16)
    negb_sb = ctx.sb("negb_sb", (128, 1))
    ln_scratch(ctx, affine=True)
    xtmp = ctx.ln_sq2[:, 0:D]
    rcb_sb = ctx.ln_sq[0:64, 0:512]
    attn_sb = arena32[:, 0:D]
    outt = arena32[:, D:2 * D]

    psA = ctx.ps("psA", (128, 512))
    psB = ctx.ps("psB", (128, 512))
    psCtx = ctx.ps("psCtx", (128, 512))
    psD = ctx.ps("psD", (128, 512))
    psT = ctx.ps("psT", (128, 1024), BF16)

    for sname in ("dma", "pe", "dve", "act", "st", "gp"):
        ctx.sem(sname)
    dma, pe, dve, act, st = (ctx.sems[sn] for sn in ("dma", "pe", "dve", "act", "st"))
    ctx.start_block(consts=[EPS, -0.5, 0.5, 0.03125, -0.03125])
    blk = ctx.block

    def dump(col, ap, w):
        if not dbg:
            return
        def f(e):
            ctx.wait(e, "dve"); ctx.wait(e, "act"); ctx.wait(e, "pe")
            e.dma_start(out=dbg_out[0:ap.shape[0], int(col * 4096): int(col * 4096) + w],
                        in_=ap).then_inc(ctx.sems["gp"], 16)
            ctx.cnt["gp"] += 16
            e.wait_ge(ctx.sems["gp"], ctx.cnt["gp"])
        blk.gpsimd(f)

        def fw(e):
            e.wait_ge(ctx.sems["gp"], ctx.cnt["gp"])
        blk.sync(fw)
        blk.tensor(fw)
        ctx.vec(fw)
        ctx.scal(fw)

    def f_s0(e):
        e.dma_start(out=wstage[:, 0:128], in_=idbf_in).then_inc(dma, 16)
        e.dma_start(out=wstage[0:1, 128:256], in_=ones_in).then_inc(dma, 16)
        e.dma_start(out=bqP_sb[:], in_=bqP_in).then_inc(dma, 16)
        e.dma_start(out=bkP_sb[:], in_=bkP_in).then_inc(dma, 16)
        e.dma_start(out=uP_sb[:], in_=uP_in).then_inc(dma, 16)
        e.dma_start(out=vP_sb[:], in_=vP_in).then_inc(dma, 16)
        e.dma_start(out=gt_sb[:], in_=gt_in).then_inc(dma, 16)
        e.dma_start(out=bt_sb[:], in_=bt_in).then_inc(dma, 16)
        ctx.cnt["dma"] += 16 * 8
    blk.sync(f_s0)
    d0 = ctx.n("dma")

    def f_v0(e):
        e.wait_ge(dma, d0)
        e.tensor_copy(out=idbf_sb[:], in_=wstage[:, 0:128])
        e.tensor_copy(out=id32_sb[:], in_=wstage[:, 0:128])
        e.tensor_copy(out=ones_sb[:], in_=wstage[0:1, 128:256])
        e.tensor_copy(out=ones32_sb[:], in_=wstage[0:1, 128:256])
        e.memset(onescol[:], 1.0)
        e.memset(urow_sb[:, 0:1], 0.0)
    ctx.vec(f_v0)
    v00 = ctx.n("dve")

    def f_s0x(e):
        e.wait_ge(dve, v00)
        e.dma_start(out=wstage[0:1, 0:1024], in_=bv_in).then_inc(dma, 16)
        e.dma_start(out=wstage[32:33, 0:1024], in_=bf_in).then_inc(dma, 16)
        ctx.cnt["dma"] += 32
    blk.sync(f_s0x)
    d0x = ctx.n("dma")

    def f_v0x(e):
        e.wait_ge(dma, d0x)
        e.tensor_copy(out=bv_sb[:], in_=wstage[0:1, 0:1024])
        e.tensor_copy(out=bf_sb[:], in_=wstage[32:33, 0:1024])
    ctx.vec(f_v0x)

    for kt in range(SK):
        def f_s(e, kt=kt):
            ctx.wait(e, "dve")
            ctx.wait(e, "act")
            e.dma_start(out=xt_sb[:], in_=x_in[kt * 128:(kt + 1) * 128, :]).then_inc(dma, 16)
            ctx.cnt["dma"] += 16
        blk.sync(f_s)
        dn = ctx.n("dma")
        pe_prev = ctx.n("pe")

        def f_fence(e, dn=dn, pe_prev=pe_prev):
            e.wait_ge(dma, dn)
            e.wait_ge(pe, pe_prev)
        ctx.scal(f_fence)
        ctx.vec(f_fence)
        emit_ln(ctx, xt_sb[:], xtmp[:])
        emit_fence(ctx, ("scalar", "vector"))
        emit_ln(ctx, xtmp[:], xn_t[:], gamma_beta=(gt_sb[:], bt_sb[:]))
        a_ln = ctx.n("act")
        v_ln = ctx.n("dve")

        def f_t(e, a_ln=a_ln, v_ln=v_ln):
            e.wait_ge(act, a_ln)
            e.wait_ge(dve, v_ln)
            for k in range(8):
                ins = e.transpose(psT[:, k * 128:(k + 1) * 128],
                                  xn_t[:, k * 128:(k + 1) * 128], idbf_sb[:])
            ctx.inc(ins, "pe")
        blk.tensor(f_t)
        p_tr = ctx.n("pe")

        def f_v(e, kt=kt, p_tr=p_tr):
            e.wait_ge(pe, p_tr)
            e.tensor_copy(
                out=xlnfm.rearrange("p (k t) -> p k t", t=S)[:, :, kt * 128:(kt + 1) * 128],
                in_=psT[:].rearrange("p (k t) -> p k t", t=128))
        ctx.vec(f_v)

    for hp in range(8):
        for dc in range(8):
            emit_load_cast(ctx, w4_sb[:, dc * 128:(dc + 1) * 128],
                           wq_in[dc * 128:(dc + 1) * 128, hp * 128:(hp + 1) * 128],
                           wstage[:, 0:128])
            emit_load_cast(ctx, w4_sb[:, 1024 + dc * 128: 1024 + (dc + 1) * 128],
                           wk_in[dc * 128:(dc + 1) * 128, hp * 128:(hp + 1) * 128],
                           wstage[:, 0:128])
            emit_load_cast(ctx, w4_sb[:, 2048 + dc * 128: 2048 + (dc + 1) * 128],
                           wpos_in[dc * 128:(dc + 1) * 128, hp * 128:(hp + 1) * 128],
                           wstage[:, 0:128])
            emit_load_cast(ctx, w4_sb[:, 3072 + dc * 128: 3072 + (dc + 1) * 128],
                           wv_in[dc * 128:(dc + 1) * 128, hp * 128:(hp + 1) * 128],
                           wstage[:, 0:128])
        v_w = ctx.n("dve")

        def f_vm(e):
            ctx.wait(e, "pe")
            e.memset(val_pair, 1.0)
        ctx.vec(f_vm)
        for kt in range(SK):
            ps = psA if kt % 2 == 0 else psB

            def f_t(e, hp=hp, kt=kt, ps=ps, v_w=v_w):
                e.wait_ge(dve, v_w)
                ctx.psum_guard(e)
                e.matmul(ps[:, 0:128], ones_sb[:], bv_sb[:, hp * 128:(hp + 1) * 128],
                         start=True, stop=False)
                for dc in range(8):
                    ins = e.matmul(ps[:, 0:128], xlnfm[:, dc * S + kt * 128: dc * S + (kt + 1) * 128],
                                   w4_sb[:, 3072 + dc * 128: 3072 + (dc + 1) * 128],
                                   start=False, stop=(dc == 7))
                ctx.inc(ins, "pe")
            blk.tensor(f_t)
            p_v = ctx.n("pe")

            def f_v(e, kt=kt, ps=ps, p_v=p_v):
                e.wait_ge(pe, p_v)
                e.tensor_copy(
                    out=val_pair.rearrange("p (a b) -> p a b", b=65)[:, kt * 2:(kt + 1) * 2, 0:64],
                    in_=ps[:, 0:128].rearrange("p (a b) -> p a b", b=64))
            ctx.vec(f_v)

        for tc in range(NT):
            ps = psA if tc % 2 == 0 else psB

            def f_t(e, tc=tc, ps=ps, v_w=v_w):
                e.wait_ge(dve, v_w)
                ctx.psum_guard(e)
                for dc in range(8):
                    ins = e.matmul(ps[:], w4_sb[:, dc * 128:(dc + 1) * 128],
                                   xlnfm[:, dc * S + tc * 512: dc * S + (tc + 1) * 512],
                                   start=(dc == 0), stop=(dc == 7))
                ctx.inc(ins, "pe")
            blk.tensor(f_t)
            p_q = ctx.n("pe")

            def f_v(e, hp=hp, tc=tc, ps=ps, p_q=p_q):
                e.wait_ge(pe, p_q)
                for half in range(2):
                    src = ps[0:64, :] if half == 0 else ps[64:128, :]
                    hcol = 2 * hp + half
                    e.tensor_scalar(out=qv_sb[:, half * S + tc * 512: half * S + (tc + 1) * 512],
                                    in0=src, scalar1=bqP_sb[:, hcol:hcol + 1],
                                    scalar2=vP_sb[:, hcol:hcol + 1], op0=ALU.add, op1=ALU.add)
                    if tc * 512 in [g * 512 for g in chunks]:
                        qc = [g for g in chunks].index(tc)
                        e.tensor_scalar(out=qu_sb[:, half * MY + qc * 512: half * MY + (qc + 1) * 512],
                                        in0=src, scalar1=bqP_sb[:, hcol:hcol + 1],
                                        scalar2=uP_sb[:, hcol:hcol + 1], op0=ALU.add, op1=ALU.add)
            ctx.vec(f_v)
        for tc in range(NT):
            ps = psA if tc % 2 == 0 else psB

            def f_t(e, tc=tc, ps=ps, v_w=v_w):
                e.wait_ge(dve, v_w)
                ctx.psum_guard(e)
                for dc in range(8):
                    ins = e.matmul(ps[:], w4_sb[:, 1024 + dc * 128: 1024 + (dc + 1) * 128],
                                   xlnfm[:, dc * S + tc * 512: dc * S + (tc + 1) * 512],
                                   start=(dc == 0), stop=(dc == 7))
                ctx.inc(ins, "pe")
            blk.tensor(f_t)
            p_k = ctx.n("pe")

            def f_v(e, hp=hp, tc=tc, ps=ps, p_k=p_k):
                e.wait_ge(pe, p_k)
                for half in range(2):
                    src = ps[0:64, :] if half == 0 else ps[64:128, :]
                    hcol = 2 * hp + half
                    e.tensor_scalar(out=k_sb[:, half * S + tc * 512: half * S + (tc + 1) * 512],
                                    in0=src, scalar1=bkP_sb[:, hcol:hcol + 1],
                                    scalar2=None, op0=ALU.add)
            ctx.vec(f_v)
        for tc in range(NT):
            for dc in range(8):
                emit_load_cast(ctx, pe_st[:],
                               pef_in[dc * 128:(dc + 1) * 128, tc * 512:(tc + 1) * 512],
                               wstage[:, 0:512])
                v_pe = ctx.n("dve")

                def f_t(e, tc=tc, dc=dc, v_pe=v_pe):
                    e.wait_ge(dve, v_pe)
                    if dc == 0:
                        ctx.psum_guard(e)
                    ins = e.matmul(psD[:], w4_sb[:, 2048 + dc * 128: 2048 + (dc + 1) * 128],
                                   pe_st[:], start=(dc == 0), stop=(dc == 7))
                    ctx.inc(ins, "pe")
                blk.tensor(f_t)
            p_p = ctx.n("pe")

            def f_v(e, tc=tc, p_p=p_p):
                e.wait_ge(pe, p_p)
                e.tensor_copy(out=p_sb[:, tc * 512:(tc + 1) * 512], in_=psD[0:64, :])
                e.tensor_copy(out=p_sb[:, S + tc * 512: S + (tc + 1) * 512], in_=psD[64:128, :])
            ctx.vec(f_v)

        for hh in range(2):
            qoff_u = hh * MY
            off = hh * S
            h_glob = hp * 2 + hh

            def f_vz(e):
                ctx.wait(e, "pe")
                e.memset(urow_sb[:, 0:1], 0.0)
            ctx.vec(f_vz)
            rows_done = set()
            for g in chunks:
                tiles = [g * 512 + t * 128 for t in range(4)]
                if (g + 1) * 512 < S and (g + 1) not in chunks:
                    tiles.append((g + 1) * 512)
                elif (g + 1) * 512 < S and (g + 1) in chunks:
                    pass
                for r0 in tiles:
                    if r0 in rows_done:
                        continue
                    rows_done.add(r0)
                    only1 = (r0 % 512 == 0 and r0 // 512 not in chunks)
                    for mc in range(NT):
                        ps = psA if mc % 2 == 0 else psB

                        def f_t(e, r0=r0, mc=mc, ps=ps, off=off):
                            ctx.psum_guard(e)
                            ins = e.matmul(ps[:], qv_sb[:, off + r0: off + r0 + 128],
                                           p_sb[:, off + mc * 512: off + (mc + 1) * 512],
                                           start=True, stop=True)
                            ctx.inc(ins, "pe")
                        blk.tensor(f_t)
                        p_u = ctx.n("pe")

                        def f_v(e, mc=mc, ps=ps, p_u=p_u):
                            e.wait_ge(pe, p_u)
                            e.wait_ge(ctx.sems["st"], ctx.cnt["st"])
                            e.tensor_copy(out=urow_sb[:, 1 + mc * 512: 1 + (mc + 1) * 512],
                                          in_=ps[:])
                        ctx.vec(f_v)
                    v_u = ctx.n("dve")

                    def f_st(e, r0=r0, only1=only1, v_u=v_u):
                        e.wait_ge(dve, v_u)
                        nr = 1 if only1 else 128
                        e.dma_start(out=F_hbm[r0:r0 + nr, :], in_=urow_sb[0:nr, :]).then_inc(st, 16)
                        ctx.cnt["st"] += 16
                        e.wait_ge(st, ctx.cnt["st"])
                    blk.sync(f_st)

            def f_nrm_init(e):
                e.memset(nrm_sb[0:1, 0:4], 0.0)
            ctx.vec(f_nrm_init)
            for (src, ncols, col, soff) in ((qu_sb, NCH, 0, qoff_u), (k_sb, NT, 1, off),
                                            (qv_sb, NT, 2, off), (p_sb, NT, 3, off)):
                for cc in range(ncols):
                    def f_sc(e, src=src, cc=cc, soff=soff):
                        ctx.wait(e, "dve")
                        ctx.wait(e, "pe")
                        e.activation(out=sqt_sb[:], in_=src[:, soff + cc * 512: soff + (cc + 1) * 512],
                                     func=AF.Square)
                    ctx.scal(f_sc)
                    a_sq = ctx.n("act")

                    def f_t(e, a_sq=a_sq):
                        e.wait_ge(act, a_sq)
                        ctx.psum_guard(e)
                        ins = e.matmul(psD[0:1, :], onescol[0:64, :], sqt_sb[:],
                                       start=True, stop=True)
                        ctx.inc(ins, "pe")
                    blk.tensor(f_t)
                    p_n = ctx.n("pe")

                    def f_v(e, col=col, p_n=p_n):
                        e.wait_ge(pe, p_n)
                        e.reduce_max(out=nrm_sb[0:1, 4:5], in_=psD[0:1, :], axis=AX.X)
                        e.tensor_max(out=nrm_sb[0:1, col:col + 1], in0=nrm_sb[0:1, col:col + 1],
                                     in1=nrm_sb[0:1, 4:5])
                    ctx.vec(f_v)

            def f_v(e):
                e.tensor_mul(out=nrm_sb[0:1, 5:6], in0=nrm_sb[0:1, 0:1], in1=nrm_sb[0:1, 1:2])
                e.tensor_mul(out=nrm_sb[0:1, 6:7], in0=nrm_sb[0:1, 2:3], in1=nrm_sb[0:1, 3:4])
            ctx.vec(f_v)
            v_n = ctx.n("dve")

            def f_sc(e, v_n=v_n):
                e.wait_ge(dve, v_n)
                e.activation(out=nrm_sb[0:1, 7:8], in_=nrm_sb[0:1, 5:6], func=AF.Ln, bias=EPS)
                e.activation(out=nrm_sb[0:1, 8:9], in_=nrm_sb[0:1, 7:8], func=AF.Exp, scale=0.5)
                e.activation(out=nrm_sb[0:1, 9:10], in_=nrm_sb[0:1, 6:7], func=AF.Ln, bias=EPS)
                e.activation(out=nrm_sb[0:1, 10:11], in_=nrm_sb[0:1, 9:10], func=AF.Exp, scale=0.5)
            ctx.scal(f_sc)
            a_b = ctx.n("act")

            def f_v(e, a_b=a_b):
                e.wait_ge(act, a_b)
                e.tensor_add(out=nrm_sb[0:1, 11:12], in0=nrm_sb[0:1, 8:9], in1=nrm_sb[0:1, 10:11])
                e.tensor_scalar(out=nrm_sb[0:1, 4:5], in0=nrm_sb[0:1, 11:12], scalar1=-0.03125,
                                scalar2=None, op0=ALU.mult)
            ctx.vec(f_v)
            v_nb = ctx.n("dve")

            def f_t(e, v_nb=v_nb):
                e.wait_ge(dve, v_nb)
                ctx.psum_guard(e)
                ins = e.matmul(psD[:, 0:1], ones32_sb[:], nrm_sb[0:1, 4:5], start=True, stop=True)
                ctx.inc(ins, "pe")
            blk.tensor(f_t)
            p_nb = ctx.n("pe")

            def f_v(e, p_nb=p_nb):
                e.wait_ge(pe, p_nb)
                e.tensor_copy(out=negb_sb[:], in_=psD[:, 0:1])
            ctx.vec(f_v)
            v_negb = ctx.n("dve")

            for qc in range(NCH):
                g = chunks[qc]
                i0 = g * 512
                for ktg in range(SK // 8):
                    def f_sbd(e, i0=i0, ktg=ktg):
                        ctx.wait(e, "dve")
                        ctx.wait(e, "pe")
                        Fv = F_flat.rearrange("(a b) -> a b", b=S)
                        for t in range(4):
                            r = i0 + 1 + t * 128
                            e.dma_start(
                                out=bd_sb[:, t * 1024:(t + 1) * 1024],
                                in_=Fv[r:r + 128, ktg * 1024:(ktg + 1) * 1024]
                            ).then_inc(dma, 16)
                        ctx.cnt["dma"] += 64
                    blk.sync(f_sbd)
                    d_bd = ctx.n("dma")

                    for ktl in range(8):
                        kt = ktg * 8 + ktl
                        psS = psA if kt % 2 == 0 else psB
                        slot = kt % 2

                        def f_vc(e, ktl=ktl, d_bd=d_bd):
                            e.wait_ge(dma, d_bd)
                            ctx.wait(e, "pe")
                            e.tensor_copy(out=bdf_sb[:],
                                          in_=bd_sb[:].rearrange("p (t m) -> p t m", m=1024)[:, :, ktl * 128:(ktl + 1) * 128])
                        ctx.vec(f_vc)
                        v_bdf = ctx.n("dve")

                        def f_t1(e, off=off, qoff_u=qoff_u, kt=kt, qc=qc, psS=psS, v_bdf=v_bdf):
                            e.wait_ge(dve, v_bdf)
                            ctx.psum_guard(e)
                            for t in range(4):
                                e.matmul(psS[:, t * 128:(t + 1) * 128],
                                         k_sb[:, off + kt * 128: off + (kt + 1) * 128],
                                         qu_sb[:, qoff_u + qc * 512 + t * 128:
                                                qoff_u + qc * 512 + (t + 1) * 128],
                                         start=True, stop=False)
                                ins = e.matmul(psS[:, t * 128:(t + 1) * 128],
                                               bdf_sb[:, t * 128:(t + 1) * 128], id32_sb[:],
                                               is_transpose=True, start=False, stop=True)
                            ctx.inc(ins, "pe")
                        blk.tensor(f_t1)
                        p_s = ctx.n("pe")

                        def f_sc(e, psS=psS, slot=slot, p_s=p_s, v_negb=v_negb):
                            e.wait_ge(pe, p_s)
                            e.wait_ge(dve, v_negb)
                            e.activation(out=pexp_sb[:, slot * 512:(slot + 1) * 512], in_=psS[:],
                                         func=AF.Exp, bias=negb_sb[:], scale=0.03125)
                        ctx.scal(f_sc)
                        a_exp = ctx.n("act")

                        def f_t2(e, kt=kt, slot=slot, h_glob=h_glob, a_exp=a_exp):
                            e.wait_ge(act, a_exp)
                            if kt == 0:
                                ctx.wait(e, "dve")
                            ins = e.matmul(psCtx[0:65, :],
                                           val_pair[:, kt * 130 + hh * 65: kt * 130 + (hh + 1) * 65],
                                           pexp_sb[:, slot * 512:(slot + 1) * 512],
                                           start=(kt == 0), stop=(kt == SK - 1))
                            ctx.inc(ins, "pe")
                        blk.tensor(f_t2)
                p_ctx = ctx.n("pe")

                def f_v(e, p_ctx=p_ctx):
                    e.wait_ge(pe, p_ctx)
                    e.reciprocal(out=rc_sb[:], in_=psCtx[64:65, :])
                ctx.vec(f_v)
                v_rc = ctx.n("dve")

                def f_t3(e, v_rc=v_rc):
                    e.wait_ge(dve, v_rc)
                    ctx.wait(e, "dve")
                    ins = e.matmul(psD[0:64, :], ones32_sb[:, 0:64], rc_sb[:], start=True, stop=True)
                    ctx.inc(ins, "pe")
                blk.tensor(f_t3)
                p_rcb = ctx.n("pe")

                def f_v2(e, p_rcb=p_rcb):
                    e.wait_ge(pe, p_rcb)
                    e.wait_ge(ctx.sems["st"], ctx.cnt["st"])
                    e.tensor_copy(out=rcb_sb[:], in_=psD[0:64, :])
                    e.tensor_mul(out=yt_sb[:], in0=psCtx[0:64, :], in1=rcb_sb[:])
                ctx.vec(f_v2)
                v_y = ctx.n("dve")

                def f_st(e, hp=hp, hh=hh, qc=qc, v_y=v_y):
                    e.wait_ge(dve, v_y)
                    e.dma_start(out=ydm_hbm[hh * 64: hh * 64 + 64,
                                            hp * MY + qc * 512: hp * MY + (qc + 1) * 512],
                                in_=yt_sb[:]).then_inc(st, 16)
                    ctx.cnt["st"] += 16
                    e.wait_ge(st, ctx.cnt["st"])
                blk.sync(f_st)

    for rc in range(8):
        emit_load_cast(ctx, wf_sb[:, rc * D:(rc + 1) * D],
                       wf_in[rc * 128:(rc + 1) * 128, :], wstage[:, 0:D])
    for ci in range(NCH):
        g = chunks[ci]
        for tt in range(4):
            def f_s(e, ci=ci, g=g, tt=tt):
                ctx.wait(e, "dve")
                ctx.wait(e, "act")
                for hp in range(8):
                    e.dma_start(out=ydm_t[:, hp * 128:(hp + 1) * 128],
                                in_=ydm_hbm[:, hp * MY + ci * 512 + tt * 128:
                                            hp * MY + ci * 512 + (tt + 1) * 128]).then_inc(dma, 16)
                e.dma_start(out=xt_sb[:],
                            in_=x_in[g * 512 + tt * 128: g * 512 + (tt + 1) * 128, :]).then_inc(dma, 16)
                ctx.cnt["dma"] += 16 * 9
            blk.sync(f_s)
            dn = ctx.n("dma")

            for oc in range(2):
                ps = psA if oc == 0 else psB

                def f_t(e, oc=oc, ps=ps, dn=dn):
                    e.wait_ge(dma, dn)
                    ctx.psum_guard(e)
                    e.matmul(ps[:], ones_sb[:], bf_sb[:, oc * 512:(oc + 1) * 512],
                             start=True, stop=False)
                    for hp in range(8):
                        ins = e.matmul(ps[:], ydm_t[:, hp * 128:(hp + 1) * 128],
                                       wf_sb[:, hp * 1024 + oc * 512: hp * 1024 + (oc + 1) * 512],
                                       start=False, stop=(hp == 7))
                    ctx.inc(ins, "pe")
                blk.tensor(f_t)
            p_o = ctx.n("pe")

            def f_v(e, p_o=p_o):
                e.wait_ge(pe, p_o)
                e.wait_ge(ctx.sems["st"], ctx.cnt["st"])
                e.tensor_copy(out=attn_sb[:, 0:512], in_=psA[:])
                e.tensor_copy(out=attn_sb[:, 512:1024], in_=psB[:])
            ctx.vec(f_v)

            emit_fence(ctx, ("scalar", "vector"))
            emit_ln(ctx, attn_sb[:], ctx.ln_sq[:, 0:D])

            def f_v2(e):
                e.tensor_add(out=outt[:], in0=xt_sb[:], in1=ctx.ln_sq[:, 0:D])
            ctx.vec(f_v2)
            v_o = ctx.n("dve")

            def f_st(e, ci=ci, tt=tt, v_o=v_o):
                e.wait_ge(dve, v_o)
                e.dma_start(out=x_out[ci * 512 + tt * 128: ci * 512 + (tt + 1) * 128, :],
                            in_=outt[:]).then_inc(st, 16)
                ctx.cnt["st"] += 16
                e.wait_ge(st, ctx.cnt["st"])
            blk.sync(f_st)

    return ctx


def pe_table(S):
    pos = np.arange(S, dtype=np.float32)
    inv_freq = (1.0 / 10000.0 ** (np.arange(0.0, D, 2.0, dtype=np.float32) / D)).astype(np.float32)
    ang = pos[:, None] * inv_freq[None, :]
    enc = np.zeros((S, D), np.float32)
    enc[:, 0::2] = np.sin(ang)
    enc[:, 1::2] = np.cos(ang)
    return enc


def l1a_inmaps(x_b, inputs, par, S):
    return {
        "x_in": np.ascontiguousarray(x_b),
        "Wq": np.ascontiguousarray(inputs["L1_attn_Wq"]),
        "Wk": np.ascontiguousarray(inputs["L1_attn_Wk"]),
        "Wv": np.ascontiguousarray(inputs["L1_attn_Wv"]),
        "Wpos": np.ascontiguousarray(inputs["L1_attn_Wpos"]),
        "Wf": np.ascontiguousarray(inputs["L1_attn_Wf"]),
        "bqP": np.ascontiguousarray(inputs["L1_attn_bq"].reshape(16, 64).T),
        "bkP": np.ascontiguousarray(inputs["L1_attn_bk"].reshape(16, 64).T),
        "bv_r": np.ascontiguousarray(inputs["L1_attn_bv"].reshape(1, D)),
        "bf_r": np.ascontiguousarray(inputs["L1_attn_bf"].reshape(1, D)),
        "uP": np.ascontiguousarray(inputs["L1_attn_u"].reshape(16, 64).T),
        "vP": np.ascontiguousarray(inputs["L1_attn_v"].reshape(16, 64).T),
        "pef": np.ascontiguousarray(pe_table(S).T),
        "gt": np.ascontiguousarray(np.tile(inputs["L1_attn_gamma"].reshape(1, D), (128, 1))),
        "bt": np.ascontiguousarray(np.tile(inputs["L1_attn_beta"].reshape(1, D), (128, 1))),
        "ones_f": np.ones((1, 128), dtype=np.float32),
        "id_f": np.eye(128, dtype=np.float32),
    }


def l0fa_inmaps(x_b, inputs, par, S):
    cw = inputs["L0_ffn_conv_w"]
    wqkv = inputs["L0_ffn_wqkv"]
    bqkv = inputs["L0_ffn_bqkv"]
    return {
        "x_in": np.ascontiguousarray(x_b),
        "cwT": np.ascontiguousarray(cw.transpose(2, 1, 0).reshape(3 * D, D)),
        "cbP": np.ascontiguousarray(inputs["L0_ffn_conv_b"].reshape(8, 128).T),
        "wqkvT": np.ascontiguousarray(wqkv.T),
        "bqkvP": np.ascontiguousarray(bqkv.reshape(24, 128).T),
        "bv_r": np.ascontiguousarray(bqkv[2048:3072].reshape(1, D)),
        "woT": np.ascontiguousarray(inputs["L0_ffn_wo"].T),
        "bo_r": np.ascontiguousarray(inputs["L0_ffn_bo"].reshape(1, D)),
        "g1t": np.ascontiguousarray(np.tile(inputs["L0_ffn_ln1_g"].reshape(1, D), (128, 1))),
        "b1t": np.ascontiguousarray(np.tile(inputs["L0_ffn_ln1_b"].reshape(1, D), (128, 1))),
        "ones_f": np.ones((1, 128), dtype=np.float32),
        "id_f": np.eye(128, dtype=np.float32),
    }


def l1f_inmaps(x_b, inputs, par, S):
    return {
        "x_in": np.ascontiguousarray(x_b),
        "m1": np.ascontiguousarray(inputs["L1_ffn_m1"]),
        "m2f": np.ascontiguousarray(inputs["L1_ffn_m2"].reshape(64, 4096)),
        "mbf": np.ascontiguousarray(inputs["L1_ffn_mb"].reshape(1, 4096)),
        "w1f": np.ascontiguousarray(inputs["L1_ffn_w1"].reshape(D, 4096)),
        "w2pf": np.ascontiguousarray(inputs["L1_ffn_w2"].transpose(1, 0, 2).reshape(4096, D)),
        "b2r": np.ascontiguousarray(inputs["L1_ffn_b2"].reshape(1, D)),
        "ones_f": np.ones((1, 128), dtype=np.float32),
        "id_f": np.eye(128, dtype=np.float32),
    }


def run_block_pair(build_fn, inmap_fn, x, inputs, S):
    x = np.ascontiguousarray(x, dtype=np.float32)
    out = x.copy()
    for par in (0, 1):
        ctx = build_fn(par, S)
        chunks = my_chunks(par, S // 512)
        in_maps = [inmap_fn(x[b], inputs, par, S) for b in range(B)]
        res = run_bass_kernel_spmd(ctx.nc, in_maps, core_ids=list(range(B)))
        ctx.close()
        for b in range(B):
            xo = res.results[b]["x_out"]
            for ci, g in enumerate(chunks):
                out[b, g * 512:(g + 1) * 512, :] = xo[ci * 512:(ci + 1) * 512, :]
    return out


def run_l0a(x, inputs, S):
    x = np.ascontiguousarray(x, dtype=np.float32)
    yar_res = {}
    for par in (0, 1):
        ctx = build_l0a1(par, S)
        in_maps = [l0a1_inmaps(x[b], inputs, par, S) for b in range(B)]
        res = run_bass_kernel_spmd(ctx.nc, in_maps, core_ids=list(range(B)))
        ctx.close()
        yar_res[par] = [res.results[b]["yar_dm"] for b in range(B)]
    yar = yar_assemble(yar_res, S)
    out = x.copy()
    for par in (0, 1):
        ctx = build_l0a2(par, S)
        chunks = my_chunks(par, S // 512)
        in_maps = [l0a2_inmaps(x[b], yar[b], inputs, par, S) for b in range(B)]
        res = run_bass_kernel_spmd(ctx.nc, in_maps, core_ids=list(range(B)))
        ctx.close()
        for b in range(B):
            xo = res.results[b]["x_out"]
            for ci, g in enumerate(chunks):
                out[b, g * 512:(g + 1) * 512, :] = xo[ci * 512:(ci + 1) * 512, :]
    return out


def run_l0f(x1, inputs, S):
    x1 = np.ascontiguousarray(x1, dtype=np.float32)
    out = x1.copy()
    for par in (0, 1):
        chunks = my_chunks(par, S // 512)
        ctx = build_l0fa(par, S)
        in_maps = [l0fa_inmaps(x1[b], inputs, par, S) for b in range(B)]
        res = run_bass_kernel_spmd(ctx.nc, in_maps, core_ids=list(range(B)))
        ctx.close()
        x2p = [res.results[b]["x_out"] for b in range(B)]
        ctx = build_l0fb(par, S)
        in_maps = []
        for b in range(B):
            x1rows = np.concatenate([x1[b, g * 512:(g + 1) * 512] for g in chunks], axis=0)
            in_maps.append(l0fb_inmaps(x2p[b], x1rows, inputs, par, S))
        res = run_bass_kernel_spmd(ctx.nc, in_maps, core_ids=list(range(B)))
        ctx.close()
        for b in range(B):
            xo = res.results[b]["x_out"]
            for ci, g in enumerate(chunks):
                out[b, g * 512:(g + 1) * 512, :] = xo[ci * 512:(ci + 1) * 512, :]
    return out


def kernel(**inputs):
    x = np.asarray(inputs["L0_attn_x"], dtype=np.float32)
    S = x.shape[1]
    x = run_l0a(x, inputs, S)
    x = run_l0f(x, inputs, S)
    x = run_block_pair(build_l1a, l1a_inmaps, x, inputs, S)
    x = run_block_pair(build_l1f, l1f_inmaps, x, inputs, S)
    return x


# version 03
# speedup vs baseline: 2.3734x; 2.3734x over previous
import numpy as np
from contextlib import ExitStack
import concourse.bass as bass
import concourse.mybir as mybir
from concourse.bass_utils import run_bass_kernel_spmd
import ml_dtypes

F32 = mybir.dt.float32
BF16 = mybir.dt.bfloat16
AF = mybir.ActivationFunctionType
ALU = mybir.AluOpType
AX = mybir.AxisListType

B = 4
D = 1024
H = 16
HD = 64
EPS = 1e-5


def my_chunks(par, nchunk):
    return [i for i in range(nchunk) if (i % 4 in ((0, 3) if par == 0 else (1, 2)))]


def np_bf16(x):
    return np.ascontiguousarray(x).astype(ml_dtypes.bfloat16)


class _Auto:

    def __init__(self, ctx, e, sem):
        object.__setattr__(self, "_ctx", ctx)
        object.__setattr__(self, "_e", e)
        object.__setattr__(self, "_sem", sem)

    def __getattr__(self, name):
        attr = getattr(self._e, name)
        if name.startswith("wait") or not callable(attr):
            return attr

        def wrapped(*a, **k):
            c = self._ctx
            if c.cnt[self._sem] > 0:
                self._e.wait_ge(c.sems[self._sem], c.cnt[self._sem])
            ins = attr(*a, **k)
            c.inc(ins, self._sem)
            return ins
        return wrapped


class _BlkShim:

    def __init__(self, nc):
        self._nc = nc

    def sync(self, f):
        f(self._nc.sync)

    def tensor(self, f):
        f(self._nc.tensor)

    def vector(self, f):
        f(self._nc.vector)

    def scalar(self, f):
        f(self._nc.scalar)

    def gpsimd(self, f):
        f(self._nc.gpsimd)


class Ctx:
    def __init__(self, name, fused=False):
        self.nc = bass.Bass("TRN2", target_bir_lowering=False, debug=False,
                            detect_race_conditions=False)
        self.es = ExitStack()
        self.cnt = {}
        self.sems = {}
        self.fused = fused
        self._uniq = 0
        self._drams = {}
        self._sb_base = (self.nc.sbuf_base + 31) & ~31
        self._sb_bump = self._sb_base
        self._ps_pool = {}
        self._ps_used = {}
        self._consts_registered = {}

    def dram_in(self, name, shape, dtype=F32):
        if name in self._drams:
            return self._drams[name]
        ap = self.nc.dram_tensor(name, list(shape), dtype, kind="ExternalInput").ap()
        self._drams[name] = ap
        return ap

    def dram_out(self, name, shape, dtype=F32):
        if name in self._drams:
            return self._drams[name]
        ap = self.nc.dram_tensor(name, list(shape), dtype, kind="ExternalOutput").ap()
        self._drams[name] = ap
        return ap

    def dram_tmp(self, name, shape, dtype=F32):
        if name in self._drams:
            return self._drams[name]
        ap = self.nc.dram_tensor(name, list(shape), dtype, kind="Internal").ap()
        self._drams[name] = ap
        return ap

    def sb(self, name, shape, dtype=F32):
        if not self.fused:
            return self.es.enter_context(self.nc.sbuf_tensor(name, list(shape), dtype))
        esz = 2 if dtype == BF16 else 4
        per_part = 1
        for d in list(shape)[1:]:
            per_part *= d
        nbytes = per_part * esz
        off = (self._sb_bump + 31) & ~31
        self._uniq += 1
        t = self.nc.alloc_sbuf_tensor_at(f"{name}_f{self._uniq}", list(shape), dtype, offset=off)
        self._sb_bump = off + nbytes
        assert self._sb_bump <= self.nc.sbuf_top, (
            f"fused sbuf pool overflow at {name}: {self._sb_bump} > {self.nc.sbuf_top}")
        return t

    def ps(self, name, shape, dtype=F32):
        if not self.fused:
            return self.es.enter_context(self.nc.psum_tensor(name, list(shape), dtype))
        key = (tuple(shape), dtype)
        lst = self._ps_pool.setdefault(key, [])
        idx = self._ps_used.get(key, 0)
        if idx < len(lst):
            t = lst[idx]
        else:
            self._uniq += 1
            t = self.es.enter_context(
                self.nc.psum_tensor(f"{name}_p{self._uniq}", list(shape), dtype))
            lst.append(t)
        self._ps_used[key] = idx + 1
        return t

    def stage_reset(self):
        self._sb_bump = self._sb_base
        self._ps_used = {}

    def snapshot_cnt(self):
        return dict(self.cnt)

    def restore_cnt(self, snap):
        self.cnt = dict(snap)

    def sem(self, name):
        if name in self.sems:
            return self.sems[name]
        h = self.es.enter_context(self.nc.semaphore(name))
        self.cnt[name] = 0
        self.sems[name] = h
        return h

    def start_block(self, consts=()):
        if self.fused:
            self.block = _BlkShim(self.nc)
        else:
            self.block = self.es.enter_context(self.nc.Block())
        if consts:
            consts = [v for v in dict.fromkeys(consts) if v not in self._consts_registered]
            if consts:
                cs = self.sb("const_sb", (128, len(consts)))
                for k, v in enumerate(consts):
                    self.nc.const_aps.register(F32, float(v), cs[:, k:k + 1])
                    self._consts_registered[v] = cs[:, k:k + 1]

                def f_init(e):
                    for k, v in enumerate(consts):
                        e.memset(cs[:, k:k + 1], float(v))
                self.vec(f_init)

    def inc(self, instr, name, k=1):
        instr.then_inc(self.sems[name], k)
        self.cnt[name] += k
        return self.cnt[name]

    def n(self, name):
        return self.cnt[name]

    def close(self):
        self.es.close()

    def wait(self, e, name, val=None):
        e.wait_ge(self.sems[name], self.cnt[name] if val is None else val)

    def vec(self, f):
        self.block.vector(lambda e: f(_Auto(self, e, "dve")))

    def scal(self, f):
        self.block.scalar(lambda e: f(_Auto(self, e, "act")))

    def psum_guard(self, e):
        self.wait(e, "dve")
        self.wait(e, "act")


def emit_ln(ctx, x_ap, out_ap, width=D, gamma_beta=None):
    blk = ctx.block
    sm = ctx.ln_sm
    sq = ctx.ln_sq
    inv = 1.0 / width

    def f_sc1(e):
        e.activation(out=sq[:, 0:width], in_=x_ap, func=AF.Square)
    ctx.scal(f_sc1)
    a_sq = ctx.n("act")

    def f_v1(e):
        e.wait_ge(ctx.sems["act"], a_sq)
        e.reduce_sum(out=sm[:, 0:1], in_=x_ap, axis=AX.X)
        e.tensor_scalar(out=sm[:, 1:2], in0=sm[:, 0:1], scalar1=inv, scalar2=None, op0=ALU.mult)
        e.reduce_sum(out=sm[:, 2:3], in_=sq[:, 0:width], axis=AX.X)
        e.tensor_scalar(out=sm[:, 3:4], in0=sm[:, 2:3], scalar1=inv, scalar2=None, op0=ALU.mult)
        e.tensor_mul(out=sm[:, 4:5], in0=sm[:, 1:2], in1=sm[:, 1:2])
        e.tensor_sub(out=sm[:, 5:6], in0=sm[:, 3:4], in1=sm[:, 4:5])
    ctx.vec(f_v1)
    v_var = ctx.n("dve")

    def f_sc2(e):
        e.wait_ge(ctx.sems["dve"], v_var)
        e.activation(out=sm[:, 2:3], in_=sm[:, 5:6], func=AF.Ln, bias=EPS)
        e.activation(out=sm[:, 6:7], in_=sm[:, 2:3], func=AF.Exp, scale=-0.5)
    ctx.scal(f_sc2)
    a_rstd = ctx.n("act")

    def f_v2(e):
        e.wait_ge(ctx.sems["act"], a_rstd)
        e.scalar_tensor_tensor(out=sm[:, 7:8], in0=sm[:, 1:2], scalar=-1.0,
                                       in1=sm[:, 6:7], op0=ALU.mult, op1=ALU.mult)
    ctx.vec(f_v2)
    v_nmr = ctx.n("dve")

    if gamma_beta is None:
        def f_v3(e):
            e.tensor_scalar(out=out_ap, in0=x_ap, scalar1=sm[:, 6:7],
                                    scalar2=sm[:, 7:8], op0=ALU.mult, op1=ALU.add)
        ctx.vec(f_v3)
    else:
        g_ap, b_ap = gamma_beta

        def f_v3(e):
            e.tensor_scalar(out=sq[:, 0:width], in0=x_ap, scalar1=sm[:, 6:7],
                            scalar2=sm[:, 7:8], op0=ALU.mult, op1=ALU.add)
            e.tensor_mul(out=ctx.ln_sq2[:, 0:width], in0=sq[:, 0:width], in1=g_ap)
            e.tensor_add(out=out_ap, in0=ctx.ln_sq2[:, 0:width], in1=b_ap)
        ctx.vec(f_v3)


def ln_scratch(ctx, affine=False):
    ctx.ln_sq = ctx.sb("ln_sq", (128, D))
    ctx.ln_sm = ctx.sb("ln_sm", (128, 8))
    if affine:
        ctx.ln_sq2 = ctx.sb("ln_sq2", (128, D))


def emit_fence(ctx, engines=("scalar", "vector", "tensor")):
    blk = ctx.block
    tgts = [("dma", ctx.n("dma")), ("pe", ctx.n("pe")),
            ("dve", ctx.n("dve")), ("act", ctx.n("act"))]

    def f(e):
        for s, v in tgts:
            if v > 0:
                e.wait_ge(ctx.sems[s], v)
    for eng in engines:
        if eng == "vector":
            ctx.vec(f)
        elif eng == "scalar":
            ctx.scal(f)
        else:
            getattr(blk, eng)(f)


def emit_load_cast(ctx, dst_bf_ap, src_dram_ap, stage_ap):
    blk = ctx.block
    v0 = ctx.n("dve")

    def f_s(e):
        e.wait_ge(ctx.sems["dve"], v0)
        ctx.wait(e, "pe")
        e.dma_start(out=stage_ap, in_=src_dram_ap).then_inc(ctx.sems["dma"], 16)
        ctx.cnt["dma"] += 16
    blk.sync(f_s)
    dn = ctx.n("dma")

    def f_v(e):
        e.wait_ge(ctx.sems["dma"], dn)
        e.tensor_copy(out=dst_bf_ap, in_=stage_ap)
    ctx.vec(f_v)



def build_l1f(par, S, dbg=False):
    ctx = Ctx("l1f")
    emit_l1f(ctx, par, S, dbg=dbg, standalone=True)
    return ctx


def emit_l1f(ctx, par, S, dbg=False, standalone=False, io=None):
    nchunk = S // 512
    chunks = my_chunks(par, nchunk)
    NCH = len(chunks)
    io = io or {}

    x_in = None if "xrow" in io else ctx.dram_in("x_in", (S, D))
    xrow = io.get("xrow") or (lambda a: x_in[a:a + 128, :])
    m1_in = ctx.dram_in("m1", (D, 64))
    m2_in = ctx.dram_in("m2f", (64, 4096))
    mb_in = ctx.dram_in("mbf", (1, 4096))
    w1_in = ctx.dram_in("w1f", (D, 4096))
    w2_in = ctx.dram_in("w2pf", (4096, D))
    b2_in = ctx.dram_in("b2r", (1, D))
    ones_in = ctx.dram_in("ones_f", (1, 128))
    idbf_in = ctx.dram_in("id_f", (128, 128))
    x_out = io.get("x_out")
    if x_out is None:
        x_out = ctx.dram_out("x_out", (NCH * 512, D))
    if dbg:
        dbg_out = ctx.dram_out("dbg", (128, 8 * 4096))

    wstage = ctx.sb("wstage", (128, 2048))
    w1_sb = ctx.sb("w1_sb", (128, 8 * 4096), BF16)
    m1_sb = ctx.sb("m1_sb", (128, 8 * 64), BF16)
    m2_sb = ctx.sb("m2_sb", (64, 4096), BF16)
    mb_sb = ctx.sb("mb_sb", (1, 4096), BF16)
    b2_sb = ctx.sb("b2_sb", (1, D), BF16)
    ones_sb = ctx.sb("ones_sb", (1, 128), BF16)
    idbf_sb = ctx.sb("idbf_sb", (128, 128), BF16)
    xc_sb = ctx.sb("xc_sb", (128, 4 * D))
    xnfm = ctx.sb("xnfm", (128, 8 * 512), BF16)
    xn_t = ctx.sb("xn_t", (128, D), BF16)
    lr_sb = ctx.sb("lr_sb", (64, 512), BF16)
    logit_sb = ctx.sb("logit_sb", (128, 512))
    rmax_sb = ctx.sb("rmax_sb", (128, 8))
    oh_sb = ctx.sb("oh_sb", (128, 4 * 512), BF16)
    midr_sb = ctx.sb("midr_sb", (128, 512), BF16)
    s_sb = ctx.sb("s_sb", (128, 32 * 512), BF16)
    w2bf = ctx.sb("w2bf", (128, 4 * D), BF16)
    resA = ctx.sb("resA", (128, 4 * D))
    resB = ctx.sb("resB", (128, 4 * D))
    outt = ctx.sb("outt", (128, D))
    ln_scratch(ctx)

    psA = ctx.ps("psA", (128, 512))
    psB = ctx.ps("psB", (128, 512))
    psT = ctx.ps("psT", (128, 1024), BF16)
    psO = ctx.ps("psO", (128, 512), BF16)

    for sname in ("dma", "pe", "dve", "act", "st", "gp"):
        ctx.sem(sname)
    dma, pe, dve, act, st = (ctx.sems[sn] for sn in ("dma", "pe", "dve", "act", "st"))
    if standalone:
        ctx.start_block(consts=[EPS, -0.5])
    blk = ctx.block

    def dump(col, ap, w):
        if not dbg:
            return
        def f(e):
            ctx.wait(e, "dve"); ctx.wait(e, "act"); ctx.wait(e, "pe")
            e.dma_start(out=dbg_out[0:ap.shape[0], int(col * 4096): int(col * 4096) + w], in_=ap).then_inc(
                ctx.sems["gp"], 16)
            ctx.cnt["gp"] += 16
            e.wait_ge(ctx.sems["gp"], ctx.cnt["gp"])
        blk.gpsimd(f)

        def fw(e):
            e.wait_ge(ctx.sems["gp"], ctx.cnt["gp"])
        blk.sync(fw)
        blk.tensor(fw)
        ctx.vec(fw)
        ctx.scal(fw)

    def f_s0(e):
        e.dma_start(out=wstage[0:1, 0:128], in_=ones_in).then_inc(dma, 16)
        e.dma_start(out=wstage[0:1, 128:1152], in_=b2_in).then_inc(dma, 16)
        e.dma_start(out=wstage[:, 1280:1408], in_=idbf_in).then_inc(dma, 16)
        e.dma_start(out=wstage[64:65, 0:1024], in_=mb_in[:, 0:1024]).then_inc(dma, 16)
        e.dma_start(out=wstage[32:33, 0:1024], in_=mb_in[:, 1024:2048]).then_inc(dma, 16)
        ctx.cnt["dma"] += 16 * 5
    blk.sync(f_s0)
    d0 = ctx.n("dma")

    def f_v0(e):
        e.wait_ge(dma, d0)
        e.tensor_copy(out=ones_sb[:], in_=wstage[0:1, 0:128])
        e.tensor_copy(out=b2_sb[:], in_=wstage[0:1, 128:1152])
        e.tensor_copy(out=idbf_sb[:], in_=wstage[:, 1280:1408])
        e.tensor_copy(out=mb_sb[:, 0:1024], in_=wstage[64:65, 0:1024])
        e.tensor_copy(out=mb_sb[:, 1024:2048], in_=wstage[32:33, 0:1024])
    ctx.vec(f_v0)
    v0 = ctx.n("dve")

    def f_s0b(e):
        e.wait_ge(dve, v0)
        e.dma_start(out=wstage[64:65, 0:1024], in_=mb_in[:, 2048:3072]).then_inc(dma, 16)
        e.dma_start(out=wstage[32:33, 0:1024], in_=mb_in[:, 3072:4096]).then_inc(dma, 16)
        ctx.cnt["dma"] += 32
    blk.sync(f_s0b)
    d0b = ctx.n("dma")

    def f_v0b(e):
        e.wait_ge(dma, d0b)
        e.tensor_copy(out=mb_sb[:, 2048:3072], in_=wstage[64:65, 0:1024])
        e.tensor_copy(out=mb_sb[:, 3072:4096], in_=wstage[32:33, 0:1024])
    ctx.vec(f_v0b)
    v0 = ctx.n("dve")

    def f_s1(e):
        e.wait_ge(dve, v0)
        for dc in range(8):
            e.dma_start(out=wstage[:, dc * 64:(dc + 1) * 64],
                        in_=m1_in[dc * 128:(dc + 1) * 128, :]).then_inc(dma, 16)
        e.dma_start(out=wstage[0:64, 512:512 + 1536], in_=m2_in[:, 0:1536]).then_inc(dma, 16)
        e.dma_start(out=wstage[64:128, 512:512 + 1536], in_=m2_in[:, 1536:3072]).then_inc(dma, 16)
        ctx.cnt["dma"] += 16 * 10
    blk.sync(f_s1)
    d1 = ctx.n("dma")

    def f_v1(e):
        e.wait_ge(dma, d1)
        e.tensor_copy(out=m1_sb[:], in_=wstage[:, 0:512])
        e.tensor_copy(out=m2_sb[:, 0:1536], in_=wstage[0:64, 512:2048])
        e.tensor_copy(out=m2_sb[:, 1536:3072], in_=wstage[64:128, 512:2048])
    ctx.vec(f_v1)
    v1b = ctx.n("dve")

    def f_s1b(e):
        e.wait_ge(dve, v1b)
        e.dma_start(out=wstage[0:64, 0:1024], in_=m2_in[:, 3072:4096]).then_inc(dma, 16)
        ctx.cnt["dma"] += 16
    blk.sync(f_s1b)
    d1b = ctx.n("dma")

    def f_v1b(e):
        e.wait_ge(dma, d1b)
        e.tensor_copy(out=m2_sb[:, 3072:4096], in_=wstage[0:64, 0:1024])
    ctx.vec(f_v1b)

    for dc in range(8):
        for half in range(2):
            emit_load_cast(ctx, w1_sb[:, dc * 4096 + half * 2048: dc * 4096 + (half + 1) * 2048],
                           w1_in[dc * 128:(dc + 1) * 128, half * 2048:(half + 1) * 2048],
                           wstage[:])

    for ci, g in enumerate(chunks):
        for tt in range(4):
            pe_prev = ctx.n("pe")

            def f_s(e, g=g, tt=tt):
                ctx.wait(e, "dve")
                ctx.wait(e, "act")
                e.dma_start(out=xc_sb[:, tt * D:(tt + 1) * D],
                            in_=xrow(g * 512 + tt * 128)).then_inc(dma, 16)
                ctx.cnt["dma"] += 16
            blk.sync(f_s)
            dn = ctx.n("dma")

            def f_fence(e, dn=dn, pe_prev=pe_prev):
                e.wait_ge(dma, dn)
                e.wait_ge(pe, pe_prev)
            ctx.scal(f_fence)
            ctx.vec(f_fence)

            emit_ln(ctx, xc_sb[:, tt * D:(tt + 1) * D], xn_t[:])
            a_ln = ctx.n("act")
            v_before = ctx.n("dve")
            if ci == 0 and tt == 0:
                dump(7, xn_t[:], 1024)
                dump(7.25, ctx.ln_sm[:], 8)
                dump(7.5, xc_sb[:, 0:D], 1024)
                dump(7.75, ctx.ln_sq[:, 0:D], 1024)

            def f_t(e, tt=tt, a_ln=a_ln):
                e.wait_ge(act, a_ln)
                e.wait_ge(dve, v_before)
                for k in range(8):
                    ins = e.transpose(psT[:, k * 128:(k + 1) * 128],
                                      xn_t[:, k * 128:(k + 1) * 128], idbf_sb[:])
                ctx.inc(ins, "pe")
            blk.tensor(f_t)
            p_tr = ctx.n("pe")

            def f_v(e, tt=tt, p_tr=p_tr):
                e.wait_ge(pe, p_tr)
                e.tensor_copy(
                    out=xnfm[:].rearrange("p (k t) -> p k t", t=512)[:, :, tt * 128:(tt + 1) * 128],
                    in_=psT[:].rearrange("p (k t) -> p k t", t=128))
            ctx.vec(f_v)

        v_fm = ctx.n("dve")
        if ci == 0:
            dump(0, xnfm[:], 4096)

        def f_t(e):
            ctx.psum_guard(e)
            for dc in range(8):
                ins = e.matmul(psA[0:64, :], m1_sb[:, dc * 64:(dc + 1) * 64],
                               xnfm[:, dc * 512:(dc + 1) * 512],
                               start=(dc == 0), stop=(dc == 7))
            ctx.inc(ins, "pe")
        blk.tensor(f_t)
        p_lr = ctx.n("pe")

        def f_v(e, p_lr=p_lr):
            e.wait_ge(pe, p_lr)
            e.tensor_copy(out=lr_sb[:], in_=psA[0:64, :])
        ctx.vec(f_v)
        v_lr = ctx.n("dve")
        if ci == 0:
            dump(1, lr_sb[:], 512)

        for jmc in range(8):
            for tt in range(4):
                ps_l = psA if tt % 2 == 0 else psB

                def f_t(e, jmc=jmc, tt=tt, ps_l=ps_l):
                    ctx.psum_guard(e)
                    e.matmul(ps_l[:], lr_sb[:, tt * 128:(tt + 1) * 128],
                             m2_sb[:, jmc * 512:(jmc + 1) * 512], start=True, stop=False)
                    ins = e.matmul(ps_l[:], ones_sb[:], mb_sb[:, jmc * 512:(jmc + 1) * 512],
                                   start=False, stop=True)
                    ctx.inc(ins, "pe")
                blk.tensor(f_t)
                p_l = ctx.n("pe")

                def f_v(e, tt=tt, ps_l=ps_l, p_l=p_l):
                    e.wait_ge(pe, p_l)
                    e.tensor_copy(out=logit_sb[:], in_=ps_l[:])
                    e.tensor_reduce(out=rmax_sb[:],
                                    in_=logit_sb[:].rearrange("p (j m) -> p j m", m=64),
                                    axis=AX.X, op=ALU.max)
                    for j in range(8):
                        ins = e.tensor_single_scalar(
                            out=oh_sb[:, tt * 512 + j * 64: tt * 512 + (j + 1) * 64],
                            in_=logit_sb[:, j * 64:(j + 1) * 64],
                            scalar=rmax_sb[:, j:j + 1], op=ALU.is_equal)
                ctx.vec(f_v)
            v_oh = ctx.n("dve")
            if ci == 0 and jmc == 0:
                dump(2, oh_sb[:], 2048)
                dump(3, logit_sb[:], 512)

            for jj in range(4):
                jc = jmc * 4 + jj

                def f_t1(e, jc=jc):
                    ctx.psum_guard(e)
                    for dc in range(8):
                        ins = e.matmul(psA[:],
                                       w1_sb[:, dc * 4096 + jc * 128: dc * 4096 + (jc + 1) * 128],
                                       xnfm[:, dc * 512:(dc + 1) * 512],
                                       start=(dc == 0), stop=(dc == 7))
                    ctx.inc(ins, "pe")
                blk.tensor(f_t1)
                p_mid = ctx.n("pe")

                def f_sc(e, p_mid=p_mid):
                    e.wait_ge(pe, p_mid)
                    e.activation(out=midr_sb[:], in_=psA[:], func=AF.Relu)
                ctx.scal(f_sc)
                a_midr = ctx.n("act")

                def f_t2(e, jj=jj, v_oh=v_oh):
                    e.wait_ge(dve, max(v_oh, ctx.n("dve")))
                    for tt in range(4):
                        ins = e.transpose(psO[:, tt * 128:(tt + 1) * 128],
                                          oh_sb[:, tt * 512 + jj * 128: tt * 512 + (jj + 1) * 128],
                                          idbf_sb[:])
                    ctx.inc(ins, "pe")
                blk.tensor(f_t2)
                p_oht = ctx.n("pe")

                def f_v(e, jc=jc, p_oht=p_oht, a_midr=a_midr):
                    e.wait_ge(pe, p_oht)
                    e.wait_ge(act, a_midr)
                    e.tensor_mul(out=s_sb[:, jc * 512:(jc + 1) * 512],
                                         in0=psO[:], in1=midr_sb[:])
                ctx.vec(f_v)

        v_s = ctx.n("dve")
        if ci == 0:
            dump(4, s_sb[:, 0:4096], 4096)
            dump(5, midr_sb[:], 512)

        for jcb in range(8):
            res_dst = resA if jcb % 2 == 0 else resB
            res_src = resB if jcb % 2 == 0 else resA
            for jj in range(4):
                jc = jcb * 4 + jj
                emit_load_cast(ctx, w2bf[:, jj * D:(jj + 1) * D],
                               w2_in[jc * 128:(jc + 1) * 128, :], wstage[:, 0:D])
            v_w2 = ctx.n("dve")

            for u in range(8):
                tt, dh = u // 2, u % 2
                ps_r = psA if u % 2 == 0 else psB

                def f_t(e, jcb=jcb, tt=tt, dh=dh, ps_r=ps_r, v_w2=v_w2):
                    e.wait_ge(dve, v_w2)
                    ctx.psum_guard(e)
                    first = True
                    if jcb == 0:
                        e.matmul(ps_r[:], ones_sb[:], b2_sb[:, dh * 512:(dh + 1) * 512],
                                 start=True, stop=False)
                        first = False
                    for jj in range(4):
                        jc = jcb * 4 + jj
                        ins = e.matmul(ps_r[:],
                                       s_sb[:, jc * 512 + tt * 128: jc * 512 + (tt + 1) * 128],
                                       w2bf[:, jj * D + dh * 512: jj * D + (dh + 1) * 512],
                                       start=first, stop=(jj == 3))
                        first = False
                    ctx.inc(ins, "pe")
                blk.tensor(f_t)
                p_r = ctx.n("pe")

                def f_v(e, jcb=jcb, tt=tt, dh=dh, ps_r=ps_r, p_r=p_r,
                        res_dst=res_dst, res_src=res_src):
                    e.wait_ge(pe, p_r)
                    dst = res_dst[:, tt * D + dh * 512: tt * D + (dh + 1) * 512]
                    if jcb == 0:
                        e.tensor_copy(out=dst, in_=ps_r[:])
                    else:
                        src = res_src[:, tt * D + dh * 512: tt * D + (dh + 1) * 512]
                        e.tensor_add(out=dst, in0=ps_r[:], in1=src)
                ctx.vec(f_v)

        if ci == 0:
            dump(6, resB[:, 0:4096], 4096)
        for tt in range(4):
            emit_fence(ctx, ("scalar", "vector"))
            emit_ln(ctx, resB[:, tt * D:(tt + 1) * D], ctx.ln_sq[:, 0:D])
            a_l = ctx.n("act")

            def f_v(e, tt=tt, a_l=a_l):
                e.wait_ge(act, a_l)
                e.wait_ge(st, ctx.cnt["st"])
                e.tensor_add(out=outt[:], in0=xc_sb[:, tt * D:(tt + 1) * D],
                                     in1=ctx.ln_sq[:, 0:D])
            ctx.vec(f_v)
            v_o = ctx.n("dve")

            def f_s(e, ci=ci, tt=tt, v_o=v_o):
                e.wait_ge(dve, v_o)
                e.dma_start(out=x_out[ci * 512 + tt * 128: ci * 512 + (tt + 1) * 128, :],
                            in_=outt[:]).then_inc(st, 16)
                ctx.cnt["st"] += 16
                if not (ci == len(chunks) - 1 and tt == 3):
                    pass
                e.wait_ge(st, ctx.cnt["st"])
            blk.sync(f_s)




def build_l0fa(par, S, upto=9, dbg=False):
    ctx = Ctx("l0fa")
    emit_l0fa(ctx, par, S, upto=upto, dbg=dbg, standalone=True)
    return ctx


def emit_l0fa(ctx, par, S, upto=9, dbg=False, standalone=False, io=None):
    nchunk = S // 512
    chunks = my_chunks(par, nchunk)
    NCH = len(chunks)
    NT = S // 512
    SK = S // 128
    io = io or {}

    x_in = None if "xrow" in io else ctx.dram_in("x1f_in", (S, D))
    xrow = io.get("xrow") or (lambda a: x_in[a:a + 128, :])
    cwT_in = ctx.dram_in("cwT", (3 * D, D))
    cbP_in = ctx.dram_in("cbP", (128, 8))
    wqkvT_in = ctx.dram_in("wqkvT", (D, 3 * D))
    bqkvP_in = ctx.dram_in("bqkvP", (128, 24))
    woT_in = ctx.dram_in("woT", (D, D))
    bo_in = ctx.dram_in("bo_r", (1, D))
    bv_in = ctx.dram_in("bv_r", (1, D))
    g1_in = ctx.dram_in("g1t", (128, D))
    b1_in = ctx.dram_in("b1t", (128, D))
    ones_in = ctx.dram_in("ones_f", (1, 128))
    idbf_in = ctx.dram_in("id_f", (128, 128))
    x2p_out = io.get("x_out")
    if x2p_out is None:
        x2p_out = ctx.dram_out("x_out", (NCH * 512, D))
    ctm_hbm = ctx.dram_tmp("ctm_hbm", (NCH * 512, D))
    if dbg:
        dbg_out = ctx.dram_out("dbg", (128, 10 * 4096))

    c_xln = 0
    c_vall = 0
    c_conv = max(8 * S, SK * 1040)
    c_qk = c_conv + 8 * S
    c_end = c_qk + max(2 * NCH * 512 + 2 * S, 8 * D)
    arena = ctx.sb("arena", (128, c_end), BF16)
    xlnfm = arena[:, c_xln:c_xln + 8 * S]
    v_all = arena[:, c_vall:c_vall + SK * 1040]
    conv_fm = arena[:, c_conv:c_conv + 8 * S]
    q_sb = arena[0:64, c_qk:c_qk + 2 * NCH * 512]
    k_sb = arena[0:64, c_qk + 2 * NCH * 512:c_qk + 2 * NCH * 512 + 2 * S]
    ydm_hbm = ctx.dram_tmp("ydm_hbm", (128, 8 * NCH * 512), BF16)

    wstage = ctx.sb("wstage", (128, 1280))
    xn_t = ctx.sb("xn_t", (128, D), BF16)
    vw_sb = arena[:, c_qk:c_qk + 8 * D]
    qkw_sb = ctx.sb("qkw_sb", (128, 2048), BF16)
    cw_sb = ctx.sb("cw_sb", (128, 256), BF16)
    woT_sb = arena[:, c_vall:c_vall + 8 * D]
    ones_sb = ctx.sb("ones_sb", (1, 128), BF16)
    ones32_sb = ctx.sb("ones32_sb", (1, 128))
    onescol = ctx.sb("onescol", (128, 1), BF16)
    idbf_sb = ctx.sb("idbf_sb", (128, 128), BF16)
    cbP_sb = ctx.sb("cbP_sb", (128, 8))
    bqkvP_sb = ctx.sb("bqkvP_sb", (128, 24))
    g1_sb = ctx.sb("g1_sb", (128, D))
    b1_sb = ctx.sb("b1_sb", (128, D))
    bo_sb = ctx.sb("bo_sb", (1, D), BF16)
    bv_sb = ctx.sb("bv_sb", (1, D), BF16)
    xt_sb = ctx.sb("xt_sb", (128, D))
    p_sb = ctx.sb("p_sb", (128, 2 * 512), BF16)
    sqt_sb = ctx.sb("sqt_sb", (64, 512), BF16)
    nrm_sb = ctx.sb("nrm_sb", (128, 8))
    rc_sb = ctx.sb("rc_sb", (1, 512))
    attn_sb = ctx.sb("attn_sb", (128, D))
    outt = ctx.sb("outt", (128, D))
    negb_sb = ctx.sb("negb_sb", (128, 1))
    yt_sb = ctx.sb("yt_sb", (64, 512), BF16)
    ydm_t = ctx.sb("ydm_t", (128, 1024), BF16)
    ln_scratch(ctx, affine=True)
    rcb_sb = ctx.ln_sq[0:64, 0:512]
    ctm_sb = ctx.ln_sq2[:, 0:D]

    psA = ctx.ps("psA", (128, 512))
    psB = ctx.ps("psB", (128, 512))
    psC = ctx.ps("psC", (128, 512))
    psD = ctx.ps("psD", (128, 512))
    psCtx = ctx.ps("psCtx", (128, 512))
    psT = ctx.ps("psT", (128, 1024), BF16)

    for sname in ("dma", "pe", "dve", "act", "st", "gp"):
        ctx.sem(sname)
    dma, pe, dve, act, st = (ctx.sems[sn] for sn in ("dma", "pe", "dve", "act", "st"))
    if standalone:
        ctx.start_block(consts=[EPS, -0.5, 0.125, 0.5])
    blk = ctx.block

    def dump(col, ap, w):
        if not dbg:
            return
        def f(e):
            ctx.wait(e, "dve"); ctx.wait(e, "act"); ctx.wait(e, "pe")
            e.dma_start(out=dbg_out[0:ap.shape[0], int(col * 4096): int(col * 4096) + w],
                        in_=ap).then_inc(ctx.sems["gp"], 16)
            ctx.cnt["gp"] += 16
            e.wait_ge(ctx.sems["gp"], ctx.cnt["gp"])
        blk.gpsimd(f)

        def fw(e):
            e.wait_ge(ctx.sems["gp"], ctx.cnt["gp"])
        blk.sync(fw)
        blk.tensor(fw)
        ctx.vec(fw)
        ctx.scal(fw)

    def f_s0(e):
        e.dma_start(out=wstage[0:1, 0:128], in_=ones_in).then_inc(dma, 16)
        e.dma_start(out=wstage[0:1, 128:1152], in_=bo_in).then_inc(dma, 16)
        e.dma_start(out=wstage[:, 1152:1280], in_=idbf_in).then_inc(dma, 16)
        e.dma_start(out=cbP_sb[:], in_=cbP_in).then_inc(dma, 16)
        e.dma_start(out=bqkvP_sb[:], in_=bqkvP_in).then_inc(dma, 16)
        e.dma_start(out=g1_sb[:], in_=g1_in).then_inc(dma, 16)
        e.dma_start(out=b1_sb[:], in_=b1_in).then_inc(dma, 16)
        e.dma_start(out=wstage[64:65, 0:1024], in_=bv_in).then_inc(dma, 16)
        ctx.cnt["dma"] += 16 * 8
    blk.sync(f_s0)
    d0 = ctx.n("dma")

    def f_v0(e):
        e.wait_ge(dma, d0)
        e.tensor_copy(out=ones_sb[:], in_=wstage[0:1, 0:128])
        e.tensor_copy(out=ones32_sb[:], in_=wstage[0:1, 0:128])
        e.tensor_copy(out=bo_sb[:], in_=wstage[0:1, 128:1152])
        e.tensor_copy(out=idbf_sb[:], in_=wstage[:, 1152:1280])
        e.tensor_copy(out=bv_sb[:], in_=wstage[64:65, 0:1024])
        e.memset(onescol[:], 1.0)
    ctx.vec(f_v0)

    for dc in range(8):
        emit_load_cast(ctx, vw_sb[:, dc * D:(dc + 1) * D],
                       wqkvT_in[dc * 128:(dc + 1) * 128, 2048:3072], wstage[:, 0:D])

    if upto < 1:
        return
    for tc in range(NT):
        for tt in range(4):
            def f_s(e, tc=tc, tt=tt):
                ctx.wait(e, "dve")
                ctx.wait(e, "act")
                e.dma_start(out=xt_sb[:],
                            in_=xrow(tc * 512 + tt * 128)).then_inc(dma, 16)
                ctx.cnt["dma"] += 16
            blk.sync(f_s)
            dn = ctx.n("dma")
            pe_prev = ctx.n("pe")

            def f_fence(e, dn=dn, pe_prev=pe_prev):
                e.wait_ge(dma, dn)
                e.wait_ge(pe, pe_prev)
            ctx.scal(f_fence)
            ctx.vec(f_fence)
            emit_ln(ctx, xt_sb[:], xn_t[:])
            a_ln = ctx.n("act")
            v_before = ctx.n("dve")

            def f_t(e, a_ln=a_ln, v_before=v_before):
                e.wait_ge(act, a_ln)
                e.wait_ge(dve, v_before)
                for k in range(8):
                    ins = e.transpose(psT[:, k * 128:(k + 1) * 128],
                                      xn_t[:, k * 128:(k + 1) * 128], idbf_sb[:])
                ctx.inc(ins, "pe")
            blk.tensor(f_t)
            p_tr = ctx.n("pe")

            def f_v(e, tc=tc, tt=tt, p_tr=p_tr):
                e.wait_ge(pe, p_tr)
                e.tensor_copy(
                    out=xlnfm.rearrange("p (k t) -> p k t", t=S)[:, :, tc * 512 + tt * 128: tc * 512 + (tt + 1) * 128],
                    in_=psT[:].rearrange("p (k t) -> p k t", t=128))
            ctx.vec(f_v)

    dump(0, xlnfm[:, 0:4096], 4096)
    dump(1, xlnfm[:, 4096:8192], 4096)
    if upto < 2:
        return
    order = [(0, 1)] + [(ic, tau) for ic in range(8) for tau in (0, 1, 2) if not (ic == 0 and tau == 1)]
    NGRP = max(1, NT // 2)
    pslist = [psA, psB, psC, psD]
    for oc in range(8):
        for grp in range(0, NT, 4):
            tcs = list(range(grp, min(grp + 4, NT)))
            for idx, (ic, tau) in enumerate(order):
                slot = idx % 2
                emit_load_cast(ctx, cw_sb[:, slot * 128:(slot + 1) * 128],
                               cwT_in[tau * D + ic * 128: tau * D + (ic + 1) * 128,
                                      oc * 128:(oc + 1) * 128], wstage[:, 0:128])
                v_w = ctx.n("dve")

                def f_t(e, idx=idx, ic=ic, tau=tau, slot=slot, tcs=tcs, v_w=v_w):
                    e.wait_ge(dve, v_w)
                    if idx == 0:
                        ctx.psum_guard(e)
                    for ti, tc in enumerate(tcs):
                        ps = pslist[ti]
                        sh = tau - 1
                        lo = tc * 512 + sh
                        hi = lo + 512
                        po, qo = 0, 512
                        if lo < 0:
                            po = -lo; lo = 0
                        if hi > S:
                            qo = 512 - (hi - S); hi = S
                        ins = e.matmul(ps[:, po:qo],
                                       cw_sb[:, slot * 128:(slot + 1) * 128],
                                       xlnfm[:, ic * S + lo: ic * S + hi],
                                       start=(idx == 0), stop=(idx == len(order) - 1))
                    ctx.inc(ins, "pe")
                blk.tensor(f_t)
            p_c = ctx.n("pe")

            def f_v(e, oc=oc, tcs=tcs, p_c=p_c):
                e.wait_ge(pe, p_c)
                for ti, tc in enumerate(tcs):
                    e.tensor_scalar(out=conv_fm[:, oc * S + tc * 512: oc * S + (tc + 1) * 512],
                                    in0=pslist[ti][:], scalar1=cbP_sb[:, oc:oc + 1],
                                    scalar2=None, op0=ALU.add)
            ctx.vec(f_v)

    for ci, g in enumerate(chunks):
        for tt in range(4):
            v_prev = ctx.n("dve")

            def f_t(e, g=g, tt=tt, v_prev=v_prev):
                e.wait_ge(dve, v_prev)
                ctx.psum_guard(e)
                for k in range(8):
                    ins = e.transpose(psT[:, k * 128:(k + 1) * 128],
                                      conv_fm[:, k * S + g * 512 + tt * 128: k * S + g * 512 + (tt + 1) * 128],
                                      idbf_sb[:])
                ctx.inc(ins, "pe")
            blk.tensor(f_t)
            p_tr = ctx.n("pe")

            def f_v(e, p_tr=p_tr):
                e.wait_ge(pe, p_tr)
                e.wait_ge(ctx.sems["st"], ctx.cnt["st"])
                e.tensor_copy(out=ctm_sb[:], in_=psT[:])
            ctx.vec(f_v)
            v_ct = ctx.n("dve")

            def f_s(e, ci=ci, tt=tt, v_ct=v_ct):
                e.wait_ge(dve, v_ct)
                e.dma_start(out=ctm_hbm[ci * 512 + tt * 128: ci * 512 + (tt + 1) * 128, :],
                            in_=ctm_sb[:]).then_inc(st, 16)
                ctx.cnt["st"] += 16
                e.wait_ge(st, ctx.cnt["st"])
            blk.sync(f_s)

    dump(2, conv_fm[:, 0:4096], 4096)
    dump(3, conv_fm[:, 4096:8192], 4096)
    if upto < 3:
        return
    def f_vm(e):
        ctx.wait(e, "pe")
        e.memset(v_all, 1.0)
    ctx.vec(f_vm)
    for kt in range(SK):
        for vh in range(2):
            ps = psA if (kt * 2 + vh) % 2 == 0 else psB

            def f_t(e, kt=kt, vh=vh, ps=ps):
                ctx.psum_guard(e)
                e.matmul(ps[:], ones_sb[:], bv_sb[:, vh * 512:(vh + 1) * 512],
                         start=True, stop=False)
                for dc in range(8):
                    ins = e.matmul(ps[:], conv_fm[:, dc * S + kt * 128: dc * S + (kt + 1) * 128],
                                   vw_sb[:, dc * D + vh * 512: dc * D + (vh + 1) * 512],
                                   start=False, stop=(dc == 7))
                ctx.inc(ins, "pe")
            blk.tensor(f_t)
            p_v = ctx.n("pe")

            def f_v(e, kt=kt, vh=vh, ps=ps, p_v=p_v):
                e.wait_ge(pe, p_v)
                e.tensor_copy(
                    out=v_all.rearrange("p (a b) -> p a b", b=65)[:, kt * 16 + vh * 8: kt * 16 + (vh + 1) * 8, 0:64],
                    in_=ps[:].rearrange("p (a b) -> p a b", b=64))
            ctx.vec(f_v)

    dump(4, v_all[:, 0:4096], 4096)
    if upto < 4:
        return
    for hp in range(8):
        for dc in range(8):
            emit_load_cast(ctx, qkw_sb[:, dc * 128:(dc + 1) * 128],
                           wqkvT_in[dc * 128:(dc + 1) * 128, hp * 128:(hp + 1) * 128],
                           wstage[:, 0:128])
            emit_load_cast(ctx, qkw_sb[:, 1024 + dc * 128: 1024 + (dc + 1) * 128],
                           wqkvT_in[dc * 128:(dc + 1) * 128, 1024 + hp * 128: 1024 + (hp + 1) * 128],
                           wstage[:, 0:128])
        v_w = ctx.n("dve")

        for qc in range(NCH):
            g = chunks[qc]
            ps = psA if qc % 2 == 0 else psB

            def f_t(e, g=g, ps=ps, v_w=v_w):
                e.wait_ge(dve, v_w)
                ctx.psum_guard(e)
                for dc in range(8):
                    ins = e.matmul(ps[:], qkw_sb[:, dc * 128:(dc + 1) * 128],
                                   conv_fm[:, dc * S + g * 512: dc * S + (g + 1) * 512],
                                   start=(dc == 0), stop=(dc == 7))
                ctx.inc(ins, "pe")
            blk.tensor(f_t)
            p_q = ctx.n("pe")

            def f_v(e, hp=hp, qc=qc, ps=ps, p_q=p_q):
                e.wait_ge(pe, p_q)
                e.tensor_scalar(out=q_sb[:, qc * 512:(qc + 1) * 512], in0=ps[0:64, :],
                                scalar1=bqkvP_sb[0:64, hp:hp + 1], scalar2=None, op0=ALU.add)
                e.tensor_scalar(out=q_sb[:, NCH * 512 + qc * 512: NCH * 512 + (qc + 1) * 512],
                                in0=ps[64:128, :],
                                scalar1=bqkvP_sb[64:128, hp:hp + 1], scalar2=None, op0=ALU.add)
            ctx.vec(f_v)
        for tc in range(NT):
            ps = psA if tc % 2 == 0 else psB

            def f_t(e, tc=tc, ps=ps, v_w=v_w):
                e.wait_ge(dve, v_w)
                ctx.psum_guard(e)
                for dc in range(8):
                    ins = e.matmul(ps[:], qkw_sb[:, 1024 + dc * 128: 1024 + (dc + 1) * 128],
                                   conv_fm[:, dc * S + tc * 512: dc * S + (tc + 1) * 512],
                                   start=(dc == 0), stop=(dc == 7))
                ctx.inc(ins, "pe")
            blk.tensor(f_t)
            p_k = ctx.n("pe")

            def f_v(e, hp=hp, tc=tc, ps=ps, p_k=p_k):
                e.wait_ge(pe, p_k)
                e.tensor_scalar(out=k_sb[:, tc * 512:(tc + 1) * 512], in0=ps[0:64, :],
                                scalar1=bqkvP_sb[0:64, 8 + hp:8 + hp + 1], scalar2=None, op0=ALU.add)
                e.tensor_scalar(out=k_sb[:, S + tc * 512: S + (tc + 1) * 512], in0=ps[64:128, :],
                                scalar1=bqkvP_sb[64:128, 8 + hp:8 + hp + 1], scalar2=None, op0=ALU.add)
            ctx.vec(f_v)

        if hp == 0:
            dump(5, q_sb[:, 0:2 * NCH * 512], 2 * NCH * 512)
            dump(6, k_sb[:, 0:2 * S], 2 * S)
        for hh in range(2):
            qoff = hh * NCH * 512
            koff = hh * S
            h_glob = hp * 2 + hh
            def f_nrm_init(e):
                e.memset(nrm_sb[:, 0:2], 0.0)
            ctx.vec(f_nrm_init)
            for (src, ncols, col, off) in ((q_sb, NCH, 0, qoff), (k_sb, NT, 1, koff)):
                for cc in range(ncols):
                    a_prev = ctx.n("act")

                    def f_sc(e, src=src, cc=cc, off=off):
                        ctx.wait(e, "dve")
                        ctx.wait(e, "pe")
                        e.activation(out=sqt_sb[:], in_=src[:, off + cc * 512: off + (cc + 1) * 512],
                                     func=AF.Square)
                    ctx.scal(f_sc)
                    a_sq = ctx.n("act")

                    def f_t(e, a_sq=a_sq):
                        e.wait_ge(act, a_sq)
                        ctx.psum_guard(e)
                        ins = e.matmul(psD[0:1, :], onescol[0:64, :], sqt_sb[:],
                                       start=True, stop=True)
                        ctx.inc(ins, "pe")
                    blk.tensor(f_t)
                    p_n = ctx.n("pe")

                    def f_v(e, col=col, p_n=p_n):
                        e.wait_ge(pe, p_n)
                        e.reduce_max(out=nrm_sb[0:1, 4:5], in_=psD[0:1, :], axis=AX.X)
                        e.tensor_max(out=nrm_sb[0:1, col:col + 1], in0=nrm_sb[0:1, col:col + 1],
                                     in1=nrm_sb[0:1, 4:5])
                    ctx.vec(f_v)
            def f_v(e):
                e.tensor_mul(out=nrm_sb[0:1, 5:6], in0=nrm_sb[0:1, 0:1], in1=nrm_sb[0:1, 1:2])
            ctx.vec(f_v)
            v_n = ctx.n("dve")

            def f_sc(e, v_n=v_n):
                e.wait_ge(dve, v_n)
                e.activation(out=nrm_sb[0:1, 6:7], in_=nrm_sb[0:1, 5:6], func=AF.Ln, bias=EPS)
                e.activation(out=nrm_sb[0:1, 7:8], in_=nrm_sb[0:1, 6:7], func=AF.Exp, scale=0.5)
            ctx.scal(f_sc)
            a_b = ctx.n("act")

            def f_v(e, a_b=a_b):
                e.wait_ge(act, a_b)
                e.tensor_scalar(out=nrm_sb[0:1, 3:4], in0=nrm_sb[0:1, 7:8], scalar1=-0.125,
                                scalar2=None, op0=ALU.mult)
            ctx.vec(f_v)
            v_nb = ctx.n("dve")

            def f_t(e, v_nb=v_nb):
                e.wait_ge(dve, v_nb)
                ctx.psum_guard(e)
                ins = e.matmul(psD[:, 0:1], ones32_sb[:], nrm_sb[0:1, 3:4], start=True, stop=True)
                ctx.inc(ins, "pe")
            blk.tensor(f_t)
            p_nb = ctx.n("pe")

            def f_v(e, p_nb=p_nb):
                e.wait_ge(pe, p_nb)
                e.tensor_copy(out=negb_sb[:], in_=psD[:, 0:1])
            ctx.vec(f_v)
            v_negb = ctx.n("dve")

            for qc in range(NCH):
                for kt in range(SK):
                    psS = psA if kt % 2 == 0 else psB
                    slot = kt % 2
                    a_prev_exp = ctx.n("act")
                    p_prev_ctx = ctx.n("pe")

                    def f_t1(e, qoff=qoff, koff=koff, kt=kt, qc=qc, psS=psS):
                        ctx.psum_guard(e)
                        ins = e.matmul(psS[:], k_sb[:, koff + kt * 128: koff + (kt + 1) * 128],
                                       q_sb[:, qoff + qc * 512: qoff + (qc + 1) * 512],
                                       start=True, stop=True)
                        ctx.inc(ins, "pe")
                    blk.tensor(f_t1)
                    p_s = ctx.n("pe")

                    def f_sc(e, psS=psS, slot=slot, p_s=p_s, v_negb=v_negb):
                        e.wait_ge(pe, p_s)
                        e.wait_ge(dve, v_negb)
                        e.activation(out=p_sb[:, slot * 512:(slot + 1) * 512], in_=psS[:],
                                     func=AF.Exp, bias=negb_sb[:], scale=0.125)
                    ctx.scal(f_sc)
                    a_exp = ctx.n("act")

                    def f_t2(e, kt=kt, slot=slot, h_glob=h_glob, a_exp=a_exp):
                        e.wait_ge(act, a_exp)
                        if kt == 0:
                            ctx.wait(e, "dve")
                        ins = e.matmul(psCtx[0:65, :],
                                       v_all[:, kt * 1040 + h_glob * 65: kt * 1040 + (h_glob + 1) * 65],
                                       p_sb[:, slot * 512:(slot + 1) * 512],
                                       start=(kt == 0), stop=(kt == SK - 1))
                        ctx.inc(ins, "pe")
                    blk.tensor(f_t2)
                p_ctx = ctx.n("pe")

                def f_v(e, p_ctx=p_ctx):
                    e.wait_ge(pe, p_ctx)
                    e.reciprocal(out=rc_sb[:], in_=psCtx[64:65, :])
                ctx.vec(f_v)
                v_rc = ctx.n("dve")

                def f_t3(e, v_rc=v_rc):
                    e.wait_ge(dve, v_rc)
                    ctx.wait(e, "dve")
                    ins = e.matmul(psD[0:64, :], ones32_sb[:, 0:64], rc_sb[:], start=True, stop=True)
                    ctx.inc(ins, "pe")
                blk.tensor(f_t3)
                p_rcb = ctx.n("pe")

                def f_v2(e, p_rcb=p_rcb):
                    e.wait_ge(pe, p_rcb)
                    e.wait_ge(ctx.sems["st"], ctx.cnt["st"])
                    e.tensor_copy(out=rcb_sb[:], in_=psD[0:64, :])
                    e.tensor_mul(out=yt_sb[:], in0=psCtx[0:64, :], in1=rcb_sb[:])
                ctx.vec(f_v2)
                v_y = ctx.n("dve")

                def f_sty(e, hp=hp, hh=hh, qc=qc, v_y=v_y):
                    e.wait_ge(ctx.sems["dve"], v_y)
                    e.dma_start(out=ydm_hbm[hh * 64: hh * 64 + 64,
                                            hp * NCH * 512 + qc * 512: hp * NCH * 512 + (qc + 1) * 512],
                                in_=yt_sb[:]).then_inc(ctx.sems["st"], 16)
                    ctx.cnt["st"] += 16
                    e.wait_ge(ctx.sems["st"], ctx.cnt["st"])
                blk.sync(f_sty)

    if upto < 5:
        return
    for rc in range(8):
        emit_load_cast(ctx, woT_sb[:, rc * D:(rc + 1) * D],
                       woT_in[rc * 128:(rc + 1) * 128, :], wstage[:, 0:D])
    for ci in range(NCH):
        for tt in range(4):
            def f_s(e, ci=ci, tt=tt):
                ctx.wait(e, "dve")
                ctx.wait(e, "act")
                e.dma_start(out=ctm_sb[:],
                            in_=ctm_hbm[ci * 512 + tt * 128: ci * 512 + (tt + 1) * 128, :]
                            ).then_inc(dma, 16)
                for hp in range(8):
                    e.dma_start(out=ydm_t[:, hp * 128:(hp + 1) * 128],
                                in_=ydm_hbm[:, hp * NCH * 512 + ci * 512 + tt * 128:
                                            hp * NCH * 512 + ci * 512 + (tt + 1) * 128]).then_inc(dma, 16)
                ctx.cnt["dma"] += 16 * 9
            blk.sync(f_s)
            d_ct = ctx.n("dma")

            for oc in range(2):
                ps = psA if oc == 0 else psB

                def f_t(e, ci=ci, tt=tt, oc=oc, ps=ps, d_ct=d_ct):
                    e.wait_ge(dma, d_ct)
                    ctx.psum_guard(e)
                    e.matmul(ps[:], ones_sb[:], bo_sb[:, oc * 512:(oc + 1) * 512],
                             start=True, stop=False)
                    for hp in range(8):
                        ins = e.matmul(
                            ps[:],
                            ydm_t[:, hp * 128:(hp + 1) * 128],
                            woT_sb[:, hp * 1024 + oc * 512: hp * 1024 + (oc + 1) * 512],
                            start=False, stop=(hp == 7))
                    ctx.inc(ins, "pe")
                blk.tensor(f_t)
            p_o = ctx.n("pe")

            def f_v(e, p_o=p_o, d_ct=d_ct):
                e.wait_ge(pe, p_o)
                e.wait_ge(dma, d_ct)
                e.wait_ge(ctx.sems["st"], ctx.cnt["st"])
                e.tensor_copy(out=attn_sb[:, 0:512], in_=psA[:])
                e.tensor_copy(out=attn_sb[:, 512:1024], in_=psB[:])
                e.tensor_add(out=outt[:], in0=attn_sb[:], in1=ctm_sb[:])
            ctx.vec(f_v)
            if ci == 0 and tt == 0:
                dump(8, attn_sb[:], 1024)
                dump(8.25, ctm_sb[:], 1024)

            if upto >= 6:
                emit_fence(ctx, ("scalar", "vector"))
                emit_ln(ctx, outt[:], attn_sb[:], gamma_beta=(g1_sb[:], b1_sb[:]))
            else:
                def f_cp(e):
                    e.tensor_copy(out=attn_sb[:], in_=outt[:])
                ctx.vec(f_cp)
            v_ln = ctx.n("dve")

            def f_st(e, ci=ci, tt=tt, v_ln=v_ln):
                e.wait_ge(dve, v_ln)
                e.dma_start(out=x2p_out[ci * 512 + tt * 128: ci * 512 + (tt + 1) * 128, :],
                            in_=attn_sb[:]).then_inc(st, 16)
                ctx.cnt["st"] += 16
                e.wait_ge(st, ctx.cnt["st"])
            blk.sync(f_st)




def build_l0fb(par, S, dbg=False):
    ctx = Ctx("l0fb")
    emit_l0fb(ctx, par, S, dbg=dbg, standalone=True)
    return ctx


def emit_l0fb(ctx, par, S, dbg=False, standalone=False, io=None):
    nchunk = S // 512
    chunks = my_chunks(par, nchunk)
    NCH = len(chunks)
    NTL = NCH * 4
    MY = NCH * 512
    io = io or {}

    x2p_in = io.get("x2p") if "x2p" in io else ctx.dram_in("x2p", (MY, D))
    x1_in = io.get("x1") if "x1" in io else ctx.dram_in("x1", (MY, D))
    gw_in = ctx.dram_in("gate_wP", (128, 64))
    gb_in = ctx.dram_in("gate_b", (1, 8))
    w1_in = ctx.dram_in("w1r", (8 * D, 4096))
    b1P_in = ctx.dram_in("b1P", (128, 256))
    w2_in = ctx.dram_in("w2r", (8 * 4096, D))
    b2_in = ctx.dram_in("b2r8", (8, D))
    g2_in = ctx.dram_in("g2t", (128, D))
    b2t_in = ctx.dram_in("b2t", (128, D))
    ones_in = ctx.dram_in("ones_f", (1, 128))
    idbf_in = ctx.dram_in("id_f", (128, 128))
    x_out = io.get("x_out")
    if x_out is None:
        x_out = ctx.dram_out("x_out", (MY, D))
    if dbg:
        dbg_out = ctx.dram_out("dbg", (128, 8 * 4096))

    x2fm = ctx.sb("x2fm", (128, 8 * MY), BF16)
    acc = ctx.sb("acc", (128, NTL * D))
    probs_sb = ctx.sb("probs_sb", (128, NTL * 8))
    w1h_sb = ctx.sb("w1h_sb", (128, 8 * 1024), BF16)
    w2h_sb = ctx.sb("w2h_sb", (128, 8 * D), BF16)
    h_sb = ctx.sb("h_sb", (128, 8 * 512), BF16)
    wstage = ctx.sb("wstage", (128, 2048))
    gw_sb = ctx.sb("gw_sb", (128, 64), BF16)
    gb_sb = ctx.sb("gb_sb", (1, 8), BF16)
    b1P_sb = ctx.sb("b1P_sb", (128, 256))
    b2row_sb = ctx.sb("b2row_sb", (1, D), BF16)
    g2_sb = ctx.sb("g2_sb", (128, D))
    b2t_sb = ctx.sb("b2t_sb", (128, D))
    ones_sb = ctx.sb("ones_sb", (1, 128), BF16)
    idbf_sb = ctx.sb("idbf_sb", (128, 128), BF16)
    xt_sb = ctx.sb("xt_sb", (128, D))
    xn_t = ctx.sb("xn_t", (128, D), BF16)
    sm8 = ctx.sb("sm8", (128, 8))
    ffn_sb = ctx.sb("ffn_sb", (128, D))
    ln_scratch(ctx, affine=True)
    x1t_sb = wstage[:, 0:D]
    outt = wstage[:, D:2 * D]

    psA = ctx.ps("psA", (128, 512))
    psB = ctx.ps("psB", (128, 512))
    psC = ctx.ps("psC", (128, 512))
    psD = ctx.ps("psD", (128, 512))
    psT = ctx.ps("psT", (128, 1024), BF16)

    for sname in ("dma", "pe", "dve", "act", "st", "gp"):
        ctx.sem(sname)
    dma, pe, dve, act, st = (ctx.sems[sn] for sn in ("dma", "pe", "dve", "act", "st"))
    if standalone:
        ctx.start_block(consts=[EPS, -0.5])
    blk = ctx.block

    def dump(col, ap, w):
        if not dbg:
            return
        def f(e):
            ctx.wait(e, "dve"); ctx.wait(e, "act"); ctx.wait(e, "pe")
            e.dma_start(out=dbg_out[0:ap.shape[0], int(col * 4096): int(col * 4096) + w],
                        in_=ap).then_inc(ctx.sems["gp"], 16)
            ctx.cnt["gp"] += 16
            e.wait_ge(ctx.sems["gp"], ctx.cnt["gp"])
        blk.gpsimd(f)

        def fw(e):
            e.wait_ge(ctx.sems["gp"], ctx.cnt["gp"])
        blk.sync(fw)
        blk.tensor(fw)
        ctx.vec(fw)
        ctx.scal(fw)

    def f_s0(e):
        e.dma_start(out=wstage[0:1, 0:128], in_=ones_in).then_inc(dma, 16)
        e.dma_start(out=wstage[0:1, 128:136], in_=gb_in).then_inc(dma, 16)
        e.dma_start(out=wstage[:, 256:384], in_=idbf_in).then_inc(dma, 16)
        e.dma_start(out=wstage[:, 384:448], in_=gw_in).then_inc(dma, 16)
        e.dma_start(out=b1P_sb[:], in_=b1P_in).then_inc(dma, 16)
        e.dma_start(out=g2_sb[:], in_=g2_in).then_inc(dma, 16)
        e.dma_start(out=b2t_sb[:], in_=b2t_in).then_inc(dma, 16)
        ctx.cnt["dma"] += 16 * 7
    blk.sync(f_s0)
    d0 = ctx.n("dma")

    def f_v0(e):
        e.wait_ge(dma, d0)
        e.tensor_copy(out=ones_sb[:], in_=wstage[0:1, 0:128])
        e.tensor_copy(out=gb_sb[:], in_=wstage[0:1, 128:136])
        e.tensor_copy(out=idbf_sb[:], in_=wstage[:, 256:384])
        e.tensor_copy(out=gw_sb[:], in_=wstage[:, 384:448])
        e.memset(acc[:], 0.0)
    ctx.vec(f_v0)

    for mt in range(NTL):
        def f_s(e, mt=mt):
            ctx.wait(e, "dve")
            ctx.wait(e, "act")
            e.dma_start(out=xt_sb[:], in_=x2p_in[mt * 128:(mt + 1) * 128, :]).then_inc(dma, 16)
            ctx.cnt["dma"] += 16
        blk.sync(f_s)
        dn = ctx.n("dma")
        pe_prev = ctx.n("pe")

        def f_fence(e, dn=dn, pe_prev=pe_prev):
            e.wait_ge(dma, dn)
            e.wait_ge(pe, pe_prev)
        ctx.vec(f_fence)

        def f_cast(e):
            e.tensor_copy(out=xn_t[:], in_=xt_sb[:])
        ctx.vec(f_cast)
        v_c = ctx.n("dve")

        def f_t(e, mt=mt, v_c=v_c):
            e.wait_ge(dve, v_c)
            ctx.psum_guard(e)
            for k in range(8):
                ins = e.transpose(psT[:, k * 128:(k + 1) * 128],
                                  xn_t[:, k * 128:(k + 1) * 128], idbf_sb[:])
            ctx.inc(ins, "pe")
        blk.tensor(f_t)
        p_tr = ctx.n("pe")

        def f_v(e, mt=mt, p_tr=p_tr):
            e.wait_ge(pe, p_tr)
            e.tensor_copy(
                out=x2fm[:].rearrange("p (k t) -> p k t", t=MY)[:, :, mt * 128:(mt + 1) * 128],
                in_=psT[:].rearrange("p (k t) -> p k t", t=128))
        ctx.vec(f_v)
        v_fm = ctx.n("dve")

        def f_tg(e, mt=mt, v_fm=v_fm):
            e.wait_ge(dve, v_fm)
            ctx.psum_guard(e)
            e.matmul(psA[:, 0:8], ones_sb[:], gb_sb[:], start=True, stop=False)
            for dc in range(8):
                ins = e.matmul(psA[:, 0:8], x2fm[:, dc * MY + mt * 128: dc * MY + (mt + 1) * 128],
                               gw_sb[:, dc * 8:(dc + 1) * 8], start=False, stop=(dc == 7))
            ctx.inc(ins, "pe")
        blk.tensor(f_tg)
        p_g = ctx.n("pe")

        def f_vg(e, mt=mt, p_g=p_g):
            e.wait_ge(pe, p_g)
            e.tensor_copy(out=sm8[:], in_=psA[:, 0:8])
            e.reduce_max(out=ctx.ln_sm[:, 0:1], in_=sm8[:], axis=AX.X, negate=True)
        ctx.vec(f_vg)
        v_m = ctx.n("dve")

        def f_sg(e, v_m=v_m):
            e.wait_ge(dve, v_m)
            e.activation(out=ctx.ln_sq[:, 0:8], in_=sm8[:], func=AF.Exp, bias=ctx.ln_sm[:, 0:1])
        ctx.scal(f_sg)
        a_e = ctx.n("act")

        def f_vg2(e, mt=mt, a_e=a_e):
            e.wait_ge(act, a_e)
            e.reduce_sum(out=ctx.ln_sm[:, 1:2], in_=ctx.ln_sq[:, 0:8], axis=AX.X)
            e.reciprocal(out=ctx.ln_sm[:, 2:3], in_=ctx.ln_sm[:, 1:2])
            e.tensor_scalar(out=probs_sb[:, mt * 8:(mt + 1) * 8], in0=ctx.ln_sq[:, 0:8],
                            scalar1=ctx.ln_sm[:, 2:3], scalar2=None, op0=ALU.mult)
        ctx.vec(f_vg2)

    if dbg:
        dump(0, x2fm[:, 0:4096], 4096)
        dump(1, probs_sb[:], NTL * 8)

    for e_i in range(8):
        def f_sb2(e, e_i=e_i):
            ctx.wait(e, "pe")
            ctx.wait(e, "dve")
            e.dma_start(out=wstage[0:1, 0:1024], in_=b2_in[e_i:e_i + 1, :]).then_inc(dma, 16)
            ctx.cnt["dma"] += 16
        blk.sync(f_sb2)
        d_b2 = ctx.n("dma")

        def f_vb2(e, d_b2=d_b2):
            e.wait_ge(dma, d_b2)
            e.tensor_copy(out=b2row_sb[:], in_=wstage[0:1, 0:1024])
        ctx.vec(f_vb2)

        for fh in range(4):
            for dc in range(8):
                emit_load_cast(ctx, w1h_sb[:, dc * 1024:(dc + 1) * 1024],
                               w1_in[e_i * D + dc * 128: e_i * D + (dc + 1) * 128,
                                     fh * 1024:(fh + 1) * 1024], wstage[:, 0:D])
            for fc in range(8):
                emit_load_cast(ctx, w2h_sb[:, fc * D:(fc + 1) * D],
                               w2_in[e_i * 4096 + fh * 1024 + fc * 128:
                                     e_i * 4096 + fh * 1024 + (fc + 1) * 128, :],
                               wstage[:, 0:D])
            v_w = ctx.n("dve")

            for tcg in range(NCH):
                for fc in range(8):
                    ps = psA if fc % 2 == 0 else psB

                    def f_t(e, e_i=e_i, fh=fh, tcg=tcg, fc=fc, ps=ps, v_w=v_w):
                        e.wait_ge(dve, v_w)
                        ctx.psum_guard(e)
                        for dc in range(8):
                            ins = e.matmul(ps[:],
                                           w1h_sb[:, dc * 1024 + fc * 128: dc * 1024 + (fc + 1) * 128],
                                           x2fm[:, dc * MY + tcg * 512: dc * MY + (tcg + 1) * 512],
                                           start=(dc == 0), stop=(dc == 7))
                        ctx.inc(ins, "pe")
                    blk.tensor(f_t)
                    p_h = ctx.n("pe")

                    def f_sc(e, e_i=e_i, fh=fh, fc=fc, ps=ps, p_h=p_h):
                        e.wait_ge(pe, p_h)
                        e.activation(out=h_sb[:, fc * 512:(fc + 1) * 512], in_=ps[:],
                                     func=AF.Relu, bias=b1P_sb[:, e_i * 32 + fh * 8 + fc: e_i * 32 + fh * 8 + fc + 1])
                    ctx.scal(f_sc)
                a_h = ctx.n("act")

                for ts in range(4):
                    for dh in range(2):
                        ps = psC if (ts * 2 + dh) % 2 == 0 else psD

                        def f_t(e, e_i=e_i, fh=fh, tcg=tcg, ts=ts, dh=dh, ps=ps, a_h=a_h):
                            e.wait_ge(act, a_h)
                            ctx.psum_guard(e)
                            first = True
                            if fh == 0:
                                e.matmul(ps[:], ones_sb[:], b2row_sb[:, dh * 512:(dh + 1) * 512],
                                         start=True, stop=False)
                                first = False
                            for fc in range(8):
                                ins = e.matmul(ps[:],
                                               h_sb[:, fc * 512 + ts * 128: fc * 512 + (ts + 1) * 128],
                                               w2h_sb[:, fc * D + dh * 512: fc * D + (dh + 1) * 512],
                                               start=first, stop=(fc == 7))
                                first = False
                            ctx.inc(ins, "pe")
                        blk.tensor(f_t)
                        p_y = ctx.n("pe")

                        def f_v(e, e_i=e_i, tcg=tcg, ts=ts, dh=dh, ps=ps, p_y=p_y):
                            e.wait_ge(pe, p_y)
                            mt = tcg * 4 + ts
                            sl = acc[:, mt * D + dh * 512: mt * D + (dh + 1) * 512]
                            e.scalar_tensor_tensor(out=sl, in0=ps[:],
                                                   scalar=probs_sb[:, mt * 8 + e_i: mt * 8 + e_i + 1],
                                                   in1=sl, op0=ALU.mult, op1=ALU.add)
                        ctx.vec(f_v)

    if dbg:
        dump(2, acc[:, 0:4096], 4096)

    for mt in range(NTL):
        def f_s(e, mt=mt):
            ctx.wait(e, "dve")
            ctx.wait(e, "act")
            e.dma_start(out=xt_sb[:], in_=x2p_in[mt * 128:(mt + 1) * 128, :]).then_inc(dma, 16)
            e.dma_start(out=x1t_sb[:], in_=x1_in[mt * 128:(mt + 1) * 128, :]).then_inc(dma, 16)
            ctx.cnt["dma"] += 32
        blk.sync(f_s)
        dn = ctx.n("dma")

        def f_fence(e, dn=dn):
            e.wait_ge(dma, dn)
        ctx.vec(f_fence)
        ctx.scal(f_fence)

        def f_v(e, mt=mt):
            e.tensor_add(out=outt[:], in0=xt_sb[:], in1=acc[:, mt * D:(mt + 1) * D])
        ctx.vec(f_v)
        emit_fence(ctx, ("scalar", "vector"))
        emit_ln(ctx, outt[:], ffn_sb[:], gamma_beta=(g2_sb[:], b2t_sb[:]))
        emit_fence(ctx, ("scalar", "vector"))
        emit_ln(ctx, ffn_sb[:], ctx.ln_sq[:, 0:D])

        def f_v2(e, mt=mt):
            e.wait_ge(ctx.sems["st"], ctx.cnt["st"])
            e.tensor_add(out=outt[:], in0=x1t_sb[:], in1=ctx.ln_sq[:, 0:D])
        ctx.vec(f_v2)
        v_o = ctx.n("dve")

        def f_st(e, mt=mt, v_o=v_o):
            e.wait_ge(dve, v_o)
            e.dma_start(out=x_out[mt * 128:(mt + 1) * 128, :], in_=outt[:]).then_inc(st, 16)
            ctx.cnt["st"] += 16
            e.wait_ge(st, ctx.cnt["st"])
        blk.sync(f_st)


def l0fb_inmaps(x2p_rows, x1_rows, inputs, par, S):
    b1 = inputs["L0_ffn_b1"]
    return {
        "x2p": np.ascontiguousarray(x2p_rows),
        "x1": np.ascontiguousarray(x1_rows),
        "gate_wP": np.ascontiguousarray(inputs["L0_ffn_gate_w"].reshape(8, 128, 8).transpose(1, 0, 2).reshape(128, 64)),
        "gate_b": np.ascontiguousarray(inputs["L0_ffn_gate_b"].reshape(1, 8)),
        "w1r": np.ascontiguousarray(inputs["L0_ffn_w1"].reshape(8 * D, 4096)),
        "b1P": np.ascontiguousarray(b1.reshape(8, 32, 128).transpose(2, 0, 1).reshape(128, 256)),
        "w2r": np.ascontiguousarray(inputs["L0_ffn_w2"].reshape(8 * 4096, D)),
        "b2r8": np.ascontiguousarray(inputs["L0_ffn_b2"]),
        "g2t": np.ascontiguousarray(np.tile(inputs["L0_ffn_ln2_g"].reshape(1, D), (128, 1))),
        "b2t": np.ascontiguousarray(np.tile(inputs["L0_ffn_ln2_b"].reshape(1, D), (128, 1))),
        "ones_f": np.ones((1, 128), dtype=np.float32),
        "id_f": np.eye(128, dtype=np.float32),
    }




def build_l0a1(par, S, dbg=False):
    ctx = Ctx("l0a1")
    emit_l0a1(ctx, par, S, dbg=dbg, standalone=True)
    return ctx


def emit_l0a1(ctx, par, S, dbg=False, standalone=False, io=None):
    nchunk = S // 512
    chunks = my_chunks(par, nchunk)
    NCH = len(chunks)
    MY = NCH * 512
    NT = S // 512
    SK = S // 128
    io = io or {}

    x_in = ctx.dram_in("x_in", (S, D))
    xrow = io.get("xrow") or (lambda a: x_in[a:a + 128, :])
    waT_in = ctx.dram_in("waT", (D, 2 * D))
    baP_in = ctx.dram_in("baP", (128, 16))
    maskC_in = ctx.dram_in("maskC", (128, 2048))
    ones_in = ctx.dram_in("ones_f", (1, 128))
    idbf_in = ctx.dram_in("id_f", (128, 128))
    yar_out = io.get("yar_out")
    if yar_out is None:
        yar_out = ctx.dram_out("yar_dm", (128, 8 * MY))
    ccyar = io.get("ccyar")

    c_xln = 0
    c_vall = c_xln + 8 * S
    c_qk = c_vall + SK * 1040
    c_end = c_qk + 2 * MY + 2 * S
    arena = ctx.sb("arena", (128, c_end), BF16)
    xlnfm = arena[:, c_xln:c_xln + 8 * S]
    v_all = arena[:, c_vall:c_vall + SK * 1040]
    q_sb = arena[0:64, c_qk:c_qk + 2 * MY]
    k_sb = arena[0:64, c_qk + 2 * MY:c_qk + 2 * MY + 2 * S]

    wstage = ctx.sb("wstage", (128, 2048))
    xn_t = ctx.sb("xn_t", (128, D), BF16)
    qkw_sb = ctx.sb("qkw_sb", (128, 2048), BF16)
    maskC_sb = ctx.sb("maskC_sb", (128, 2048), BF16)
    ones_sb = ctx.sb("ones_sb", (1, 128), BF16)
    ones32_sb = ctx.sb("ones32_sb", (1, 128))
    onescol = ctx.sb("onescol", (128, 1), BF16)
    idbf_sb = ctx.sb("idbf_sb", (128, 128), BF16)
    baP_sb = ctx.sb("baP_sb", (128, 16))
    xt_sb = ctx.sb("xt_sb", (128, D))
    p_sb = ctx.sb("p_sb", (128, 2 * 512), BF16)
    p2_sb = ctx.sb("p2_sb", (128, 512), BF16)
    sqt_sb = ctx.sb("sqt_sb", (64, 512), BF16)
    nrm_sb = ctx.sb("nrm_sb", (128, 8))
    rc_sb = ctx.sb("rc_sb", (1, 512))
    rcb_sb = ctx.sb("rcb_sb", (64, 512))
    yt_sb = ctx.sb("yt_sb", (64, 512))
    yct_sb = ctx.sb("yct_sb", (128, 256))
    id64_sb = ctx.sb("id64_sb", (64, 64))
    negb_sb = ctx.sb("negb_sb", (128, 1))
    ln_scratch(ctx)

    psA = ctx.ps("psA", (128, 512))
    psB = ctx.ps("psB", (128, 512))
    psCtx = ctx.ps("psCtx", (128, 512))
    psD = ctx.ps("psD", (128, 512))
    psT = ctx.ps("psT", (128, 1024), BF16)

    for sname in ("dma", "pe", "dve", "act", "st", "gp"):
        ctx.sem(sname)
    dma, pe, dve, act, st = (ctx.sems[sn] for sn in ("dma", "pe", "dve", "act", "st"))
    if standalone:
        ctx.start_block(consts=[EPS, -0.5, 0.125, 0.5])
    blk = ctx.block

    def f_s0(e):
        e.dma_start(out=wstage[0:1, 0:128], in_=ones_in).then_inc(dma, 16)
        e.dma_start(out=wstage[:, 128:256], in_=idbf_in).then_inc(dma, 16)
        e.dma_start(out=baP_sb[:], in_=baP_in).then_inc(dma, 16)
        e.dma_start(out=wstage[:, 256:256 + 2048 - 256], in_=maskC_in[:, 0:2048 - 256]).then_inc(dma, 16)
        ctx.cnt["dma"] += 16 * 4
    blk.sync(f_s0)
    d0 = ctx.n("dma")

    def f_v0(e):
        e.wait_ge(dma, d0)
        e.tensor_copy(out=ones_sb[:], in_=wstage[0:1, 0:128])
        e.tensor_copy(out=ones32_sb[:], in_=wstage[0:1, 0:128])
        e.tensor_copy(out=idbf_sb[:], in_=wstage[:, 128:256])
        e.tensor_copy(out=id64_sb[:], in_=wstage[0:64, 128:192])
        e.tensor_copy(out=maskC_sb[:, 0:2048 - 256], in_=wstage[:, 256:2048])
        e.memset(onescol[:], 1.0)
    ctx.vec(f_v0)
    v0 = ctx.n("dve")

    def f_s0b(e):
        e.wait_ge(dve, v0)
        e.dma_start(out=wstage[:, 0:256], in_=maskC_in[:, 2048 - 256:2048]).then_inc(dma, 16)
        ctx.cnt["dma"] += 16
    blk.sync(f_s0b)
    d0b = ctx.n("dma")

    def f_v0b(e):
        e.wait_ge(dma, d0b)
        e.tensor_copy(out=maskC_sb[:, 2048 - 256:2048], in_=wstage[:, 0:256])
    ctx.vec(f_v0b)

    def f_vm(e):
        e.memset(v_all, 1.0)
    ctx.vec(f_vm)
    for kt in range(SK):
        def f_s(e, kt=kt):
            ctx.wait(e, "dve")
            ctx.wait(e, "act")
            e.dma_start(out=xt_sb[:], in_=xrow(kt * 128)).then_inc(dma, 16)
            ctx.cnt["dma"] += 16
        blk.sync(f_s)
        dn = ctx.n("dma")
        pe_prev = ctx.n("pe")

        def f_fence(e, dn=dn, pe_prev=pe_prev):
            e.wait_ge(dma, dn)
            e.wait_ge(pe, pe_prev)
        ctx.scal(f_fence)
        ctx.vec(f_fence)
        emit_ln(ctx, xt_sb[:], xn_t[:])
        a_ln = ctx.n("act")
        v_before = ctx.n("dve")

        def f_t(e, a_ln=a_ln, v_before=v_before):
            e.wait_ge(act, a_ln)
            e.wait_ge(dve, v_before)
            for k in range(8):
                ins = e.transpose(psT[:, k * 128:(k + 1) * 128],
                                  xn_t[:, k * 128:(k + 1) * 128], idbf_sb[:])
            ctx.inc(ins, "pe")
        blk.tensor(f_t)
        p_tr = ctx.n("pe")

        def f_v(e, kt=kt, p_tr=p_tr):
            e.wait_ge(pe, p_tr)
            e.tensor_copy(
                out=xlnfm.rearrange("p (k t) -> p k t", t=S)[:, :, kt * 128:(kt + 1) * 128],
                in_=psT[:].rearrange("p (k t) -> p k t", t=128))
            e.tensor_copy(
                out=v_all.rearrange("p (a b) -> p a b", b=65)[:, kt * 16:(kt + 1) * 16, 0:64],
                in_=xn_t[:].rearrange("p (a b) -> p a b", b=64))
        ctx.vec(f_v)

    for hp in range(8):
        for dc in range(8):
            emit_load_cast(ctx, qkw_sb[:, dc * 128:(dc + 1) * 128],
                           waT_in[dc * 128:(dc + 1) * 128, hp * 128:(hp + 1) * 128],
                           wstage[:, 0:128])
            emit_load_cast(ctx, qkw_sb[:, 1024 + dc * 128: 1024 + (dc + 1) * 128],
                           waT_in[dc * 128:(dc + 1) * 128, 1024 + hp * 128: 1024 + (hp + 1) * 128],
                           wstage[:, 0:128])
        v_w = ctx.n("dve")

        for qc in range(NCH):
            g = chunks[qc]
            ps = psA if qc % 2 == 0 else psB

            def f_t(e, g=g, ps=ps, v_w=v_w):
                e.wait_ge(dve, v_w)
                ctx.psum_guard(e)
                for dc in range(8):
                    ins = e.matmul(ps[:], qkw_sb[:, dc * 128:(dc + 1) * 128],
                                   xlnfm[:, dc * S + g * 512: dc * S + (g + 1) * 512],
                                   start=(dc == 0), stop=(dc == 7))
                ctx.inc(ins, "pe")
            blk.tensor(f_t)
            p_q = ctx.n("pe")

            def f_v(e, hp=hp, qc=qc, ps=ps, p_q=p_q):
                e.wait_ge(pe, p_q)
                e.tensor_scalar(out=q_sb[:, qc * 512:(qc + 1) * 512], in0=ps[0:64, :],
                                scalar1=baP_sb[0:64, hp:hp + 1], scalar2=None, op0=ALU.add)
                e.tensor_scalar(out=q_sb[:, MY + qc * 512: MY + (qc + 1) * 512],
                                in0=ps[64:128, :],
                                scalar1=baP_sb[64:128, hp:hp + 1], scalar2=None, op0=ALU.add)
            ctx.vec(f_v)
        for tc in range(NT):
            ps = psA if tc % 2 == 0 else psB

            def f_t(e, tc=tc, ps=ps, v_w=v_w):
                e.wait_ge(dve, v_w)
                ctx.psum_guard(e)
                for dc in range(8):
                    ins = e.matmul(ps[:], qkw_sb[:, 1024 + dc * 128: 1024 + (dc + 1) * 128],
                                   xlnfm[:, dc * S + tc * 512: dc * S + (tc + 1) * 512],
                                   start=(dc == 0), stop=(dc == 7))
                ctx.inc(ins, "pe")
            blk.tensor(f_t)
            p_k = ctx.n("pe")

            def f_v(e, hp=hp, tc=tc, ps=ps, p_k=p_k):
                e.wait_ge(pe, p_k)
                e.tensor_scalar(out=k_sb[:, tc * 512:(tc + 1) * 512], in0=ps[0:64, :],
                                scalar1=baP_sb[0:64, 8 + hp:8 + hp + 1], scalar2=None, op0=ALU.add)
                e.tensor_scalar(out=k_sb[:, S + tc * 512: S + (tc + 1) * 512], in0=ps[64:128, :],
                                scalar1=baP_sb[64:128, 8 + hp:8 + hp + 1], scalar2=None, op0=ALU.add)
            ctx.vec(f_v)

        for hh in range(2):
            qoff = hh * MY
            koff = hh * S
            h_glob = hp * 2 + hh

            def f_nrm_init(e):
                e.memset(nrm_sb[0:1, 0:2], 0.0)
            ctx.vec(f_nrm_init)
            for (src, ncols, col, off) in ((q_sb, NCH, 0, qoff), (k_sb, NT, 1, koff)):
                for cc in range(ncols):
                    def f_sc(e, src=src, cc=cc, off=off):
                        ctx.wait(e, "dve")
                        ctx.wait(e, "pe")
                        e.activation(out=sqt_sb[:], in_=src[:, off + cc * 512: off + (cc + 1) * 512],
                                     func=AF.Square)
                    ctx.scal(f_sc)
                    a_sq = ctx.n("act")

                    def f_t(e, a_sq=a_sq):
                        e.wait_ge(act, a_sq)
                        ctx.psum_guard(e)
                        ins = e.matmul(psD[0:1, :], onescol[0:64, :], sqt_sb[:],
                                       start=True, stop=True)
                        ctx.inc(ins, "pe")
                    blk.tensor(f_t)
                    p_n = ctx.n("pe")

                    def f_v(e, col=col, p_n=p_n):
                        e.wait_ge(pe, p_n)
                        e.reduce_max(out=nrm_sb[0:1, 4:5], in_=psD[0:1, :], axis=AX.X)
                        e.tensor_max(out=nrm_sb[0:1, col:col + 1], in0=nrm_sb[0:1, col:col + 1],
                                     in1=nrm_sb[0:1, 4:5])
                    ctx.vec(f_v)

            def f_v(e):
                e.tensor_mul(out=nrm_sb[0:1, 5:6], in0=nrm_sb[0:1, 0:1], in1=nrm_sb[0:1, 1:2])
            ctx.vec(f_v)
            v_n = ctx.n("dve")

            def f_sc(e, v_n=v_n):
                e.wait_ge(dve, v_n)
                e.activation(out=nrm_sb[0:1, 6:7], in_=nrm_sb[0:1, 5:6], func=AF.Ln, bias=EPS)
                e.activation(out=nrm_sb[0:1, 7:8], in_=nrm_sb[0:1, 6:7], func=AF.Exp, scale=0.5)
            ctx.scal(f_sc)
            a_b = ctx.n("act")

            def f_v(e, a_b=a_b):
                e.wait_ge(act, a_b)
                e.tensor_scalar(out=nrm_sb[0:1, 3:4], in0=nrm_sb[0:1, 7:8], scalar1=-0.125,
                                scalar2=None, op0=ALU.mult)
            ctx.vec(f_v)
            v_nb = ctx.n("dve")

            def f_t(e, v_nb=v_nb):
                e.wait_ge(dve, v_nb)
                ctx.psum_guard(e)
                ins = e.matmul(psD[:, 0:1], ones32_sb[:], nrm_sb[0:1, 3:4], start=True, stop=True)
                ctx.inc(ins, "pe")
            blk.tensor(f_t)
            p_nb = ctx.n("pe")

            def f_v(e, p_nb=p_nb):
                e.wait_ge(pe, p_nb)
                e.tensor_copy(out=negb_sb[:], in_=psD[:, 0:1])
            ctx.vec(f_v)
            v_negb = ctx.n("dve")

            for qc in range(NCH):
                g = chunks[qc]
                ktmax = g * 4 + 4
                for kt in range(ktmax):
                    psS = psA if kt % 2 == 0 else psB
                    slot = kt % 2
                    diag_r = kt - g * 4

                    def f_t1(e, qoff=qoff, koff=koff, kt=kt, qc=qc, psS=psS):
                        ctx.psum_guard(e)
                        ins = e.matmul(psS[:], k_sb[:, koff + kt * 128: koff + (kt + 1) * 128],
                                       q_sb[:, qoff + qc * 512: qoff + (qc + 1) * 512],
                                       start=True, stop=True)
                        ctx.inc(ins, "pe")
                    blk.tensor(f_t1)
                    p_s = ctx.n("pe")

                    def f_sc(e, psS=psS, slot=slot, p_s=p_s, v_negb=v_negb):
                        e.wait_ge(pe, p_s)
                        e.wait_ge(dve, v_negb)
                        e.activation(out=p_sb[:, slot * 512:(slot + 1) * 512], in_=psS[:],
                                     func=AF.Exp, bias=negb_sb[:], scale=0.125)
                    ctx.scal(f_sc)
                    a_exp = ctx.n("act")

                    rhs_off = None
                    if diag_r >= 0:
                        def f_vmask(e, slot=slot, diag_r=diag_r, a_exp=a_exp):
                            e.wait_ge(act, a_exp)
                            e.tensor_mul(out=p2_sb[:], in0=p_sb[:, slot * 512:(slot + 1) * 512],
                                         in1=maskC_sb[:, diag_r * 512:(diag_r + 1) * 512])
                        ctx.vec(f_vmask)
                    v_mask = ctx.n("dve")
                    a_exp2 = ctx.n("act")

                    def f_t2(e, kt=kt, slot=slot, h_glob=h_glob, a_exp2=a_exp2,
                             diag_r=diag_r, v_mask=v_mask, ktmax=ktmax):
                        e.wait_ge(act, a_exp2)
                        if diag_r >= 0:
                            e.wait_ge(dve, v_mask)
                        if kt == 0:
                            ctx.wait(e, "dve")
                        rhs = p2_sb[:] if diag_r >= 0 else p_sb[:, slot * 512:(slot + 1) * 512]
                        ins = e.matmul(psCtx[0:65, :],
                                       v_all[:, kt * 1040 + h_glob * 65: kt * 1040 + (h_glob + 1) * 65],
                                       rhs, start=(kt == 0), stop=(kt == ktmax - 1))
                        ctx.inc(ins, "pe")
                    blk.tensor(f_t2)
                p_ctx = ctx.n("pe")

                def f_v(e, p_ctx=p_ctx):
                    e.wait_ge(pe, p_ctx)
                    e.reciprocal(out=rc_sb[:], in_=psCtx[64:65, :])
                ctx.vec(f_v)
                v_rc = ctx.n("dve")

                def f_t3(e, v_rc=v_rc):
                    e.wait_ge(dve, v_rc)
                    ctx.wait(e, "dve")
                    ins = e.matmul(psD[0:64, :], ones32_sb[:, 0:64], rc_sb[:], start=True, stop=True)
                    ctx.inc(ins, "pe")
                blk.tensor(f_t3)
                p_rcb = ctx.n("pe")

                def f_v2(e, p_rcb=p_rcb):
                    e.wait_ge(pe, p_rcb)
                    e.wait_ge(ctx.sems["st"], ctx.cnt["st"])
                    e.tensor_copy(out=rcb_sb[:], in_=psD[0:64, :])
                    e.tensor_mul(out=yt_sb[:], in0=psCtx[0:64, :], in1=rcb_sb[:])
                ctx.vec(f_v2)
                v_y = ctx.n("dve")

                def f_st(e, hp=hp, hh=hh, qc=qc, v_y=v_y):
                    e.wait_ge(dve, v_y)
                    e.dma_start(out=yar_out[hh * 64: hh * 64 + 64,
                                            hp * MY + qc * 512: hp * MY + (qc + 1) * 512],
                                in_=yt_sb[:]).then_inc(st, 16)
                    ctx.cnt["st"] += 16
                    e.wait_ge(st, ctx.cnt["st"])
                blk.sync(f_st)

                if ccyar is not None:
                    h_g = hp * 2 + hh

                    def f_tt(e, v_y=v_y):
                        e.wait_ge(dve, v_y)
                        ctx.psum_guard(e)
                        for t in range(4):
                            ins = e.transpose(psD[:, t * 64:(t + 1) * 64],
                                              yt_sb[:, t * 128:(t + 1) * 128], id64_sb[:])
                        ctx.inc(ins, "pe")
                    blk.tensor(f_tt)
                    p_yt = ctx.n("pe")

                    def f_ve(e, p_yt=p_yt):
                        e.wait_ge(pe, p_yt)
                        e.wait_ge(st, ctx.cnt["st"])
                        e.tensor_copy(out=yct_sb[:], in_=psD[:, 0:256])
                    ctx.vec(f_ve)
                    v_ct = ctx.n("dve")

                    def f_sct(e, h_g=h_g, qc=qc, v_ct=v_ct):
                        e.wait_ge(dve, v_ct)
                        e.dma_start(
                            out=ccyar.rearrange("(c p) d -> p c d", p=128)[
                                :, qc * 4:(qc + 1) * 4, h_g * 64:(h_g + 1) * 64],
                            in_=yct_sb[:].rearrange("p (c d) -> p c d", d=64)
                        ).then_inc(st, 16)
                        ctx.cnt["st"] += 16
                        e.wait_ge(st, ctx.cnt["st"])
                    blk.sync(f_sct)


def l0a1_inmaps(x_b, inputs, par, S):
    wa = inputs["L0_attn_w_attn"]
    ba = inputs["L0_attn_b_attn"]
    maskC = np.zeros((128, 2048), np.float32)
    for r in range(4):
        for p in range(128):
            f0 = r * 128 + p
            if f0 < 512:
                maskC[p, r * 512 + f0:(r + 1) * 512] = 1.0
    return {
        "x_in": np.ascontiguousarray(x_b),
        "waT": np.ascontiguousarray(wa.T),
        "baP": np.ascontiguousarray(ba.reshape(16, 128).T),
        "maskC": maskC,
        "ones_f": np.ones((1, 128), dtype=np.float32),
        "id_f": np.eye(128, dtype=np.float32),
    }




def build_l0a2(par, S, dbg=False):
    ctx = Ctx("l0a2")
    emit_l0a2(ctx, par, S, dbg=dbg, standalone=True)
    return ctx


def emit_l0a2(ctx, par, S, dbg=False, standalone=False, io=None):
    nchunk = S // 512
    chunks = my_chunks(par, nchunk)
    NCH = len(chunks)
    MY = NCH * 512
    NT = S // 512
    SK = S // 128
    io = io or {}

    x_in = ctx.dram_in("x_in", (S, D))
    xrow = io.get("xrow") or (lambda a: x_in[a:a + 128, :])
    if "yarrow" in io:
        yarrow = io["yarrow"]
    else:
        yar_in = ctx.dram_in("yar", (S, D))
        yarrow = lambda a: yar_in[a:a + 128, :]
    yardm = io.get("yardm")
    if yardm is None:
        yardm = ctx.dram_in("yardm", (128, 8 * MY))
    waT_in = ctx.dram_in("waT", (D, 2 * D))
    baP_in = ctx.dram_in("baP", (128, 16))
    wk2T_in = ctx.dram_in("wk2T", (D, D))
    bk2P_in = ctx.dram_in("bk2P", (64, 16))
    wpT_in = ctx.dram_in("wpT", (D, D))
    bp_in = ctx.dram_in("bp_r", (1, D))
    maskS_in = ctx.dram_in("maskS", (128, 2048))
    ones_in = ctx.dram_in("ones_f", (1, 128))
    idbf_in = ctx.dram_in("id_f", (128, 128))
    x_out = io.get("x_out")
    if x_out is None:
        x_out = ctx.dram_out("x_out", (MY, D))
    ydm_hbm = ctx.dram_tmp("ydm_hbm", (128, 8 * MY), BF16)
    if dbg:
        dbg_out = ctx.dram_out("dbg", (128, 8 * 4096))

    c_xln = 0
    c_e = c_xln + 8 * S
    c_qa = c_e + SK * 1024
    c_ka = c_qa + 2 * MY
    c_end = c_ka + 2 * S
    arena = ctx.sb("arena", (128, c_end), BF16)
    xlnfm = arena[:, c_xln:c_xln + 8 * S]
    e_all = arena[:, c_e:c_e + SK * 1024]
    qa_sb = arena[0:64, c_qa:c_qa + 2 * MY]
    ka_sb = arena[0:64, c_ka:c_ka + 2 * S]

    wstage = ctx.sb("wstage", (128, 1024))
    xn_a = ctx.sb("xn_a", (128, D), BF16)
    xn_b = ctx.sb("xn_b", (128, D), BF16)
    yar_a = ctx.sb("yar_a", (128, D))
    yar_b = ctx.sb("yar_b", (128, D))
    esh_sb = ctx.sb("esh_sb", (128, D), BF16)
    qkw_sb = ctx.sb("qkw_sb", (128, 2048), BF16)
    wpT_sb = arena[:, 0:8 * D]
    maskS_sb = ctx.sb("maskS_sb", (128, 2048), BF16)
    ones_sb = ctx.sb("ones_sb", (1, 128), BF16)
    onescol = ctx.sb("onescol", (128, 1), BF16)
    idbf_sb = ctx.sb("idbf_sb", (128, 128), BF16)
    baP_sb = ctx.sb("baP_sb", (128, 16))
    bk2n_sb = ctx.sb("bk2n_sb", (64, 16))
    bp_sb = ctx.sb("bp_sb", (1, D), BF16)
    xt_sb = ctx.sb("xt_sb", (128, D))
    ma_sb = ctx.sb("ma_sb", (128, 2 * 512), BF16)
    yart_sb = ctx.sb("yart_sb", (64, 512))
    ydm_t = ctx.sb("ydm_t", (128, 1024), BF16)
    yt_sb = ctx.sb("yt_sb", (64, 512), BF16)
    attn_sb = ctx.sb("attn_sb", (128, D))
    outt = ctx.sb("outt", (128, D))
    ln_scratch(ctx)
    t1_sb = ctx.ln_sq[:, 0:512]

    psA = ctx.ps("psA", (128, 512))
    psB = ctx.ps("psB", (128, 512))
    psCtx = ctx.ps("psCtx", (128, 512))
    psD = ctx.ps("psD", (128, 512))
    psT = ctx.ps("psT", (128, 1024), BF16)

    for sname in ("dma", "pe", "dve", "act", "st", "gp"):
        ctx.sem(sname)
    dma, pe, dve, act, st = (ctx.sems[sn] for sn in ("dma", "pe", "dve", "act", "st"))
    if standalone:
        ctx.start_block(consts=[EPS, -0.5, -0.0025])
    blk = ctx.block

    def dump(col, ap, w):
        if not dbg:
            return
        def f(e):
            ctx.wait(e, "dve"); ctx.wait(e, "act"); ctx.wait(e, "pe")
            e.dma_start(out=dbg_out[0:ap.shape[0], int(col * 4096): int(col * 4096) + w],
                        in_=ap).then_inc(ctx.sems["gp"], 16)
            ctx.cnt["gp"] += 16
            e.wait_ge(ctx.sems["gp"], ctx.cnt["gp"])
        blk.gpsimd(f)

        def fw(e):
            e.wait_ge(ctx.sems["gp"], ctx.cnt["gp"])
        blk.sync(fw)
        blk.tensor(fw)
        ctx.vec(fw)
        ctx.scal(fw)

    def f_s0(e):
        e.dma_start(out=wstage[0:1, 0:128], in_=ones_in).then_inc(dma, 16)
        e.dma_start(out=wstage[:, 128:256], in_=idbf_in).then_inc(dma, 16)
        e.dma_start(out=baP_sb[:], in_=baP_in).then_inc(dma, 16)
        e.dma_start(out=bk2n_sb[:], in_=bk2P_in).then_inc(dma, 16)
        ctx.cnt["dma"] += 16 * 4
    blk.sync(f_s0)
    d0 = ctx.n("dma")

    def f_v0(e):
        e.wait_ge(dma, d0)
        e.tensor_copy(out=ones_sb[:], in_=wstage[0:1, 0:128])
        e.tensor_copy(out=idbf_sb[:], in_=wstage[:, 128:256])
        e.memset(onescol[:], 1.0)
        e.tensor_scalar(out=bk2n_sb[:], in0=bk2n_sb[:], scalar1=-0.0025,
                        scalar2=None, op0=ALU.mult)
    ctx.vec(f_v0)
    v0 = ctx.n("dve")

    def f_s0b(e):
        e.wait_ge(dve, v0)
        e.dma_start(out=wstage[:, 0:1024], in_=maskS_in[:, 0:1024]).then_inc(dma, 16)
        ctx.cnt["dma"] += 16
    blk.sync(f_s0b)
    d0b = ctx.n("dma")

    def f_v0b(e):
        e.wait_ge(dma, d0b)
        e.tensor_copy(out=maskS_sb[:, 0:1024], in_=wstage[:])
    ctx.vec(f_v0b)
    v0c = ctx.n("dve")

    def f_s0c(e):
        e.wait_ge(dve, v0c)
        e.dma_start(out=wstage[:, 0:1024], in_=maskS_in[:, 1024:2048]).then_inc(dma, 16)
        ctx.cnt["dma"] += 16
    blk.sync(f_s0c)
    d0c = ctx.n("dma")

    def f_v0c(e):
        e.wait_ge(dma, d0c)
        e.tensor_copy(out=maskS_sb[:, 1024:2048], in_=wstage[:])
    ctx.vec(f_v0c)
    v0d = ctx.n("dve")

    def f_s0d(e):
        e.wait_ge(dve, v0d)
        e.dma_start(out=wstage[0:1, 0:1024], in_=bp_in).then_inc(dma, 16)
        ctx.cnt["dma"] += 16
    blk.sync(f_s0d)
    d0d = ctx.n("dma")

    def f_v0d(e):
        e.wait_ge(dma, d0d)
        e.tensor_copy(out=bp_sb[:], in_=wstage[0:1, 0:1024])
    ctx.vec(f_v0d)

    for kt in range(SK):
        xn_cur = xn_a if kt % 2 == 0 else xn_b
        xn_prev = xn_b if kt % 2 == 0 else xn_a
        yar_cur = yar_a if kt % 2 == 0 else yar_b
        yar_prev = yar_b if kt % 2 == 0 else yar_a

        def f_s(e, kt=kt, yar_cur=yar_cur):
            ctx.wait(e, "dve")
            ctx.wait(e, "act")
            e.dma_start(out=xt_sb[:], in_=xrow(kt * 128)).then_inc(dma, 16)
            e.dma_start(out=yar_cur[:], in_=yarrow(kt * 128)).then_inc(dma, 16)
            ctx.cnt["dma"] += 32
        blk.sync(f_s)
        dn = ctx.n("dma")
        pe_prev = ctx.n("pe")

        def f_fence(e, dn=dn, pe_prev=pe_prev):
            e.wait_ge(dma, dn)
            e.wait_ge(pe, pe_prev)
        ctx.scal(f_fence)
        ctx.vec(f_fence)
        emit_ln(ctx, xt_sb[:], xn_cur[:])
        a_ln = ctx.n("act")
        v_ln = ctx.n("dve")

        def f_t(e, xn_cur=xn_cur, a_ln=a_ln, v_ln=v_ln):
            e.wait_ge(act, a_ln)
            e.wait_ge(dve, v_ln)
            for k in range(8):
                ins = e.transpose(psT[:, k * 128:(k + 1) * 128],
                                  xn_cur[:, k * 128:(k + 1) * 128], idbf_sb[:])
            ctx.inc(ins, "pe")
        blk.tensor(f_t)
        p_tr = ctx.n("pe")

        def f_v(e, kt=kt, p_tr=p_tr):
            e.wait_ge(pe, p_tr)
            e.tensor_copy(
                out=xlnfm.rearrange("p (k t) -> p k t", t=S)[:, :, kt * 128:(kt + 1) * 128],
                in_=psT[:].rearrange("p (k t) -> p k t", t=128))
        ctx.vec(f_v)
        v_fm = ctx.n("dve")

        if kt > 0:
            def f_g(e, kt=kt, xn_prev=xn_prev, xn_cur=xn_cur, v_fm=v_fm, v_ln=v_ln):
                e.wait_ge(dve, ctx.cnt["dve"])
                e.dma_start(out=esh_sb[0:127, :], in_=xn_prev[1:128, :]).then_inc(ctx.sems["gp"], 16)
                e.dma_start(out=esh_sb[127:128, :], in_=xn_cur[0:1, :]).then_inc(ctx.sems["gp"], 16)
                ctx.cnt["gp"] += 32
            blk.gpsimd(f_g)
            g_n = ctx.n("gp")

            def f_ve(e, kt=kt, yar_prev=yar_prev, g_n=g_n):
                e.wait_ge(ctx.sems["gp"], g_n)
                e.tensor_sub(out=e_all[:, (kt - 1) * 1024: kt * 1024],
                             in0=esh_sb[:], in1=yar_prev[:])
            ctx.vec(f_ve)
        if kt == SK - 1:
            def f_g2(e, xn_cur=xn_cur):
                e.wait_ge(ctx.sems["dve"], ctx.cnt["dve"])
                e.dma_start(out=esh_sb[0:127, :], in_=xn_cur[1:128, :]).then_inc(ctx.sems["gp"], 16)
                ctx.cnt["gp"] += 16
            blk.gpsimd(f_g2)
            g_n = ctx.n("gp")

            def f_ve2(e, kt=kt, yar_cur=yar_cur, g_n=g_n):
                e.wait_ge(ctx.sems["gp"], g_n)
                e.memset(esh_sb[:].rearrange("p m -> p m")[127:128, :], 0.0) if False else None
                e.tensor_sub(out=e_all[:, kt * 1024:(kt + 1) * 1024],
                             in0=esh_sb[:], in1=yar_cur[:])
            ctx.vec(f_ve2)

    if dbg:
        dump(0, e_all[:, 0:4096], 4096)
    for hp in range(8):
        for dc in range(8):
            emit_load_cast(ctx, qkw_sb[:, dc * 128:(dc + 1) * 128],
                           waT_in[dc * 128:(dc + 1) * 128, hp * 128:(hp + 1) * 128],
                           wstage[:, 0:128])
            emit_load_cast(ctx, qkw_sb[:, 1024 + dc * 128: 1024 + (dc + 1) * 128],
                           wk2T_in[dc * 128:(dc + 1) * 128, hp * 128:(hp + 1) * 128],
                           wstage[:, 0:128])
        v_w = ctx.n("dve")

        for qc in range(NCH):
            g = chunks[qc]
            ps = psA if qc % 2 == 0 else psB

            def f_t(e, g=g, ps=ps, v_w=v_w):
                e.wait_ge(dve, v_w)
                ctx.psum_guard(e)
                for dc in range(8):
                    ins = e.matmul(ps[:], qkw_sb[:, dc * 128:(dc + 1) * 128],
                                   xlnfm[:, dc * S + g * 512: dc * S + (g + 1) * 512],
                                   start=(dc == 0), stop=(dc == 7))
                ctx.inc(ins, "pe")
            blk.tensor(f_t)
            p_q = ctx.n("pe")

            def f_v(e, hp=hp, qc=qc, ps=ps, p_q=p_q):
                e.wait_ge(pe, p_q)
                for half in range(2):
                    src = ps[0:64, :] if half == 0 else ps[64:128, :]
                    bia = baP_sb[0:64, hp:hp + 1] if half == 0 else baP_sb[64:128, hp:hp + 1]
                    e.tensor_scalar(out=t1_sb[0:64, :], in0=src, scalar1=bia,
                                    scalar2=None, op0=ALU.add)
                    e.scalar_tensor_tensor(out=qa_sb[:, half * MY + qc * 512: half * MY + (qc + 1) * 512],
                                           in0=t1_sb[0:64, :], scalar=0.02, in1=t1_sb[0:64, :],
                                           op0=ALU.mult, op1=ALU.min)
            ctx.vec(f_v)
        for tc in range(NT):
            ps = psA if tc % 2 == 0 else psB

            def f_t(e, tc=tc, ps=ps, v_w=v_w):
                e.wait_ge(dve, v_w)
                ctx.psum_guard(e)
                for dc in range(8):
                    ins = e.matmul(ps[:], qkw_sb[:, 1024 + dc * 128: 1024 + (dc + 1) * 128],
                                   xlnfm[:, dc * S + tc * 512: dc * S + (tc + 1) * 512],
                                   start=(dc == 0), stop=(dc == 7))
                ctx.inc(ins, "pe")
            blk.tensor(f_t)
            p_k = ctx.n("pe")

            for half in range(2):
                def f_vc(e, ps=ps, p_k=p_k, half=half):
                    e.wait_ge(pe, p_k)
                    src = ps[0:64, :] if half == 0 else ps[64:128, :]
                    e.tensor_copy(out=t1_sb[0:64, 0:512], in_=src)
                ctx.vec(f_vc)
                v_c = ctx.n("dve")

                def f_sc(e, hp=hp, half=half, v_c=v_c):
                    e.wait_ge(dve, v_c)
                    e.activation(out=t1_sb[64:128, 0:512], in_=t1_sb[0:64, 0:512], func=AF.Exp,
                                 bias=bk2n_sb[:, 2 * hp + half: 2 * hp + half + 1], scale=-0.0025)
                ctx.scal(f_sc)
                a_e = ctx.n("act")

                def f_v(e, tc=tc, half=half, a_e=a_e):
                    e.wait_ge(act, a_e)
                    e.tensor_scalar(out=t1_sb[0:64, 0:512], in0=t1_sb[64:128, 0:512],
                                    scalar1=1.0, scalar2=None, op0=ALU.add)
                    e.reciprocal(out=t1_sb[64:128, 0:512], in_=t1_sb[0:64, 0:512])
                    e.tensor_scalar(out=ka_sb[:, half * S + tc * 512: half * S + (tc + 1) * 512],
                                    in0=t1_sb[64:128, 0:512], scalar1=0.125, scalar2=None, op0=ALU.mult)
                ctx.vec(f_v)

        if dbg and hp == 0:
            dump(1, qa_sb[:, 0:2 * MY], 2 * MY)
            dump(2, ka_sb[:, 0:2 * S], 2 * S)
        for hh in range(2):
            qoff = hh * MY
            koff = hh * S
            h_glob = hp * 2 + hh
            for qc in range(NCH):
                g = chunks[qc]
                ktmax = g * 4 + 4
                for kt in range(ktmax):
                    psS = psA if kt % 2 == 0 else psB
                    slot = kt % 2
                    diag_r = kt - g * 4

                    def f_t1(e, qoff=qoff, koff=koff, kt=kt, qc=qc, psS=psS):
                        ctx.psum_guard(e)
                        ins = e.matmul(psS[:], ka_sb[:, koff + kt * 128: koff + (kt + 1) * 128],
                                       qa_sb[:, qoff + qc * 512: qoff + (qc + 1) * 512],
                                       start=True, stop=True)
                        ctx.inc(ins, "pe")
                    blk.tensor(f_t1)
                    p_s = ctx.n("pe")

                    def f_v(e, psS=psS, slot=slot, p_s=p_s, diag_r=diag_r):
                        e.wait_ge(pe, p_s)
                        if diag_r >= 0:
                            e.tensor_mul(out=ma_sb[:, slot * 512:(slot + 1) * 512], in0=psS[:],
                                         in1=maskS_sb[:, diag_r * 512:(diag_r + 1) * 512])
                        else:
                            e.tensor_copy(out=ma_sb[:, slot * 512:(slot + 1) * 512], in_=psS[:])
                    ctx.vec(f_v)
                    v_ma = ctx.n("dve")

                    def f_t2(e, kt=kt, slot=slot, h_glob=h_glob, v_ma=v_ma, ktmax=ktmax):
                        e.wait_ge(dve, v_ma)
                        ins = e.matmul(psCtx[0:64, :],
                                       e_all[:, kt * 1024 + h_glob * 64: kt * 1024 + (h_glob + 1) * 64],
                                       ma_sb[:, slot * 512:(slot + 1) * 512],
                                       start=(kt == 0), stop=(kt == ktmax - 1))
                        ctx.inc(ins, "pe")
                    blk.tensor(f_t2)
                p_ctx = ctx.n("pe")

                def f_s(e, hp=hp, hh=hh, qc=qc):
                    ctx.wait(e, "dve")
                    e.dma_start(out=yart_sb[:],
                                in_=yardm[hh * 64: hh * 64 + 64,
                                          hp * MY + qc * 512: hp * MY + (qc + 1) * 512]
                                ).then_inc(dma, 16)
                    ctx.cnt["dma"] += 16
                blk.sync(f_s)
                d_y = ctx.n("dma")

                def f_v2(e, p_ctx=p_ctx, d_y=d_y):
                    e.wait_ge(pe, p_ctx)
                    e.wait_ge(dma, d_y)
                    e.wait_ge(ctx.sems["st"], ctx.cnt["st"])
                    e.tensor_add(out=yt_sb[:], in0=psCtx[0:64, :], in1=yart_sb[:])
                ctx.vec(f_v2)
                v_y = ctx.n("dve")

                def f_st(e, hp=hp, hh=hh, qc=qc, v_y=v_y):
                    e.wait_ge(dve, v_y)
                    e.dma_start(out=ydm_hbm[hh * 64: hh * 64 + 64,
                                            hp * MY + qc * 512: hp * MY + (qc + 1) * 512],
                                in_=yt_sb[:]).then_inc(st, 16)
                    ctx.cnt["st"] += 16
                    e.wait_ge(st, ctx.cnt["st"])
                blk.sync(f_st)

    for rc in range(8):
        emit_load_cast(ctx, wpT_sb[:, rc * D:(rc + 1) * D],
                       wpT_in[rc * 128:(rc + 1) * 128, :], wstage[:, 0:D])
    for ci in range(NCH):
        g = chunks[ci]
        for tt in range(4):
            def f_s(e, ci=ci, g=g, tt=tt):
                ctx.wait(e, "dve")
                ctx.wait(e, "act")
                for hp in range(8):
                    e.dma_start(out=ydm_t[:, hp * 128:(hp + 1) * 128],
                                in_=ydm_hbm[:, hp * MY + ci * 512 + tt * 128:
                                            hp * MY + ci * 512 + (tt + 1) * 128]).then_inc(dma, 16)
                e.dma_start(out=xt_sb[:],
                            in_=xrow(g * 512 + tt * 128)).then_inc(dma, 16)
                ctx.cnt["dma"] += 16 * 9
            blk.sync(f_s)
            dn = ctx.n("dma")

            for oc in range(2):
                ps = psA if oc == 0 else psB

                def f_t(e, oc=oc, ps=ps, dn=dn):
                    e.wait_ge(dma, dn)
                    ctx.psum_guard(e)
                    e.matmul(ps[:], ones_sb[:], bp_sb[:, oc * 512:(oc + 1) * 512],
                             start=True, stop=False)
                    e.matmul(ps[:], ones_sb[:], bp_sb[:, oc * 512:(oc + 1) * 512],
                             start=False, stop=False)
                    for hp in range(8):
                        ins = e.matmul(ps[:], ydm_t[:, hp * 128:(hp + 1) * 128],
                                       wpT_sb[:, hp * 1024 + oc * 512: hp * 1024 + (oc + 1) * 512],
                                       start=False, stop=(hp == 7))
                    ctx.inc(ins, "pe")
                blk.tensor(f_t)
            p_o = ctx.n("pe")

            def f_v(e, p_o=p_o):
                e.wait_ge(pe, p_o)
                e.wait_ge(ctx.sems["st"], ctx.cnt["st"])
                e.tensor_copy(out=attn_sb[:, 0:512], in_=psA[:])
                e.tensor_copy(out=attn_sb[:, 512:1024], in_=psB[:])
            ctx.vec(f_v)

            emit_fence(ctx, ("scalar", "vector"))
            emit_ln(ctx, attn_sb[:], ctx.ln_sq[:, 0:D])

            def f_v2(e):
                e.tensor_add(out=outt[:], in0=xt_sb[:], in1=ctx.ln_sq[:, 0:D])
            ctx.vec(f_v2)
            v_o = ctx.n("dve")

            def f_st(e, ci=ci, tt=tt, v_o=v_o):
                e.wait_ge(dve, v_o)
                e.dma_start(out=x_out[ci * 512 + tt * 128: ci * 512 + (tt + 1) * 128, :],
                            in_=outt[:]).then_inc(st, 16)
                ctx.cnt["st"] += 16
                e.wait_ge(st, ctx.cnt["st"])
            blk.sync(f_st)


def l0a2_inmaps(x_b, yar_b, yardm_self, inputs, par, S):
    nchunk = S // 512
    chunks = my_chunks(par, nchunk)
    maskS = np.zeros((128, 2048), np.float32)
    for r in range(4):
        for p in range(128):
            f0 = r * 128 + p + 1
            if f0 < 512:
                maskS[p, r * 512 + f0:(r + 1) * 512] = 1.0
            elif f0 == 512:
                pass
    return {
        "x_in": np.ascontiguousarray(x_b),
        "yar": np.ascontiguousarray(yar_b),
        "yardm": np.ascontiguousarray(yardm_self),
        "waT": np.ascontiguousarray(inputs["L0_attn_w_attn"].T),
        "baP": np.ascontiguousarray(inputs["L0_attn_b_attn"].reshape(16, 128).T),
        "wk2T": np.ascontiguousarray(inputs["L0_attn_w_k2"].T),
        "bk2P": np.ascontiguousarray(inputs["L0_attn_b_k2"].reshape(16, 64).T),
        "wpT": np.ascontiguousarray(inputs["L0_attn_w_proj"].T),
        "bp_r": np.ascontiguousarray(inputs["L0_attn_b_proj"].reshape(1, D)),
        "maskS": maskS,
        "ones_f": np.ones((1, 128), dtype=np.float32),
        "id_f": np.eye(128, dtype=np.float32),
    }


def yar_assemble(results_by_par, S):
    nchunk = S // 512
    out = [np.zeros((S, D), np.float32) for _ in range(B)]
    for par, rs in results_by_par.items():
        chunks = my_chunks(par, nchunk)
        MY = len(chunks) * 512
        for b in range(B):
            ydm = rs[b]
            for hp in range(8):
                for hh in range(2):
                    h = hp * 2 + hh
                    blkm = ydm[hh * 64:(hh + 1) * 64, hp * MY:(hp + 1) * MY]
                    for ci, g in enumerate(chunks):
                        out[b][g * 512:(g + 1) * 512, h * 64:(h + 1) * 64] = \
                            blkm[:, ci * 512:(ci + 1) * 512].T
    return out




def build_l1a(par, S, dbg=False):
    ctx = Ctx("l1a")
    emit_l1a(ctx, par, S, dbg=dbg, standalone=True)
    return ctx


def emit_l1a(ctx, par, S, dbg=False, standalone=False, io=None):
    nchunk = S // 512
    chunks = my_chunks(par, nchunk)
    NCH = len(chunks)
    MY = NCH * 512
    NT = S // 512
    SK = S // 128
    io = io or {}

    x_in = None if "xrow" in io else ctx.dram_in("x2f_in", (S, D))
    xrow = io.get("xrow") or (lambda a: x_in[a:a + 128, :])
    wq_in = ctx.dram_in("Wq", (D, D))
    wk_in = ctx.dram_in("Wk", (D, D))
    wv_in = ctx.dram_in("Wv", (D, D))
    wpos_in = ctx.dram_in("Wpos", (D, D))
    wf_in = ctx.dram_in("Wf", (D, D))
    bqP_in = ctx.dram_in("bqP", (64, 16))
    bkP_in = ctx.dram_in("bkP", (64, 16))
    bv_in = ctx.dram_in("bvL1_r", (1, D))
    bf_in = ctx.dram_in("bf_r", (1, D))
    uP_in = ctx.dram_in("uP", (64, 16))
    vP_in = ctx.dram_in("vP", (64, 16))
    pef_in = ctx.dram_in("pef", (D, S))
    gt_in = ctx.dram_in("gt", (128, D))
    bt_in = ctx.dram_in("bt", (128, D))
    ones_in = ctx.dram_in("ones_f", (1, 128))
    idbf_in = ctx.dram_in("id_f", (128, 128))
    x_out = io.get("x_out")
    if x_out is None:
        x_out = ctx.dram_out("x_out", (MY, D))
    F_hbm = ctx.dram_tmp("F_hbm", (S, S + 1), BF16)
    F_flat = F_hbm.rearrange("a b -> (a b)")
    ydm_hbm = ctx.dram_tmp("ydm_hbm", (128, 8 * MY), BF16)
    if dbg:
        dbg_out = ctx.dram_out("dbg", (128, 10 * 4096))

    c_xln = 0
    c_val = c_xln + 8 * S
    c_qu = c_val + SK * 130
    c_qv = c_qu + 2 * MY
    c_k = c_qv + 2 * S
    c_p = c_k + 2 * S
    c_bd = c_p + 2 * S
    c_end = c_bd + 4 * 1024 + 64
    arena = ctx.sb("arena", (128, c_end), BF16)
    xlnfm = arena[:, c_xln:c_xln + 8 * S]
    val_pair = arena[:, c_val:c_val + SK * 130]
    qu_sb = arena[0:64, c_qu:c_qu + 2 * MY]
    qv_sb = arena[0:64, c_qv:c_qv + 2 * S]
    k_sb = arena[0:64, c_k:c_k + 2 * S]
    p_sb = arena[0:64, c_p:c_p + 2 * S]
    bd_sb = arena[:, c_bd:c_bd + 4 * 1024]
    urow_sb = arena[:, c_bd:c_bd + S + 1]

    wstage = ctx.sb("wstage", (128, 1024))
    xn_t = ctx.sb("xn_t", (128, D), BF16)
    w4_sb = ctx.sb("w4_sb", (128, 4 * 1024), BF16)
    wf_sb = arena[:, 0:8 * D]
    pe_st = ctx.sb("pe_st", (128, 512), BF16)
    arena32 = ctx.sb("arena32", (128, 2048))
    bqP_sb = ctx.sb("bqP_sb", (64, 16))
    bkP_sb = ctx.sb("bkP_sb", (64, 16))
    uP_sb = ctx.sb("uP_sb", (64, 16))
    vP_sb = ctx.sb("vP_sb", (64, 16))
    bv_sb = ctx.sb("bv_sb", (1, D), BF16)
    bf_sb = ctx.sb("bf_sb", (1, D), BF16)
    gt_sb = arena32[:, 0:D]
    bt_sb = arena32[:, D:2 * D]
    ones_sb = ctx.sb("ones_sb", (1, 128), BF16)
    ones32_sb = ctx.sb("ones32_sb", (1, 128))
    onescol = ctx.sb("onescol", (128, 1), BF16)
    idbf_sb = ctx.sb("idbf_sb", (128, 128), BF16)
    id32_sb = ctx.sb("id32_sb", (128, 128))
    xt_sb = ctx.sb("xt_sb", (128, D))
    bdf_sb = ctx.sb("bdf_sb", (128, 512))
    pexp_sb = ctx.sb("pexp_sb", (128, 2 * 512), BF16)
    sqt_sb = ctx.sb("sqt_sb", (64, 512), BF16)
    nrm_sb = ctx.sb("nrm_sb", (128, 12))
    rc_sb = ctx.sb("rc_sb", (1, 512))
    yt_sb = ctx.sb("yt_sb", (64, 512), BF16)
    ydm_t = ctx.sb("ydm_t", (128, 1024), BF16)
    negb_sb = ctx.sb("negb_sb", (128, 1))
    ln_scratch(ctx, affine=True)
    xtmp = ctx.ln_sq2[:, 0:D]
    rcb_sb = ctx.ln_sq[0:64, 0:512]
    attn_sb = arena32[:, 0:D]
    outt = arena32[:, D:2 * D]

    psA = ctx.ps("psA", (128, 512))
    psB = ctx.ps("psB", (128, 512))
    psCtx = ctx.ps("psCtx", (128, 512))
    psD = ctx.ps("psD", (128, 512))
    psT = ctx.ps("psT", (128, 1024), BF16)

    for sname in ("dma", "pe", "dve", "act", "st", "gp"):
        ctx.sem(sname)
    dma, pe, dve, act, st = (ctx.sems[sn] for sn in ("dma", "pe", "dve", "act", "st"))
    if standalone:
        ctx.start_block(consts=[EPS, -0.5, 0.5, 0.03125, -0.03125])
    blk = ctx.block

    def dump(col, ap, w):
        if not dbg:
            return
        def f(e):
            ctx.wait(e, "dve"); ctx.wait(e, "act"); ctx.wait(e, "pe")
            e.dma_start(out=dbg_out[0:ap.shape[0], int(col * 4096): int(col * 4096) + w],
                        in_=ap).then_inc(ctx.sems["gp"], 16)
            ctx.cnt["gp"] += 16
            e.wait_ge(ctx.sems["gp"], ctx.cnt["gp"])
        blk.gpsimd(f)

        def fw(e):
            e.wait_ge(ctx.sems["gp"], ctx.cnt["gp"])
        blk.sync(fw)
        blk.tensor(fw)
        ctx.vec(fw)
        ctx.scal(fw)

    def f_s0(e):
        e.dma_start(out=wstage[:, 0:128], in_=idbf_in).then_inc(dma, 16)
        e.dma_start(out=wstage[0:1, 128:256], in_=ones_in).then_inc(dma, 16)
        e.dma_start(out=bqP_sb[:], in_=bqP_in).then_inc(dma, 16)
        e.dma_start(out=bkP_sb[:], in_=bkP_in).then_inc(dma, 16)
        e.dma_start(out=uP_sb[:], in_=uP_in).then_inc(dma, 16)
        e.dma_start(out=vP_sb[:], in_=vP_in).then_inc(dma, 16)
        e.dma_start(out=gt_sb[:], in_=gt_in).then_inc(dma, 16)
        e.dma_start(out=bt_sb[:], in_=bt_in).then_inc(dma, 16)
        ctx.cnt["dma"] += 16 * 8
    blk.sync(f_s0)
    d0 = ctx.n("dma")

    def f_v0(e):
        e.wait_ge(dma, d0)
        e.tensor_copy(out=idbf_sb[:], in_=wstage[:, 0:128])
        e.tensor_copy(out=id32_sb[:], in_=wstage[:, 0:128])
        e.tensor_copy(out=ones_sb[:], in_=wstage[0:1, 128:256])
        e.tensor_copy(out=ones32_sb[:], in_=wstage[0:1, 128:256])
        e.memset(onescol[:], 1.0)
        e.memset(urow_sb[:, 0:1], 0.0)
    ctx.vec(f_v0)
    v00 = ctx.n("dve")

    def f_s0x(e):
        e.wait_ge(dve, v00)
        e.dma_start(out=wstage[0:1, 0:1024], in_=bv_in).then_inc(dma, 16)
        e.dma_start(out=wstage[32:33, 0:1024], in_=bf_in).then_inc(dma, 16)
        ctx.cnt["dma"] += 32
    blk.sync(f_s0x)
    d0x = ctx.n("dma")

    def f_v0x(e):
        e.wait_ge(dma, d0x)
        e.tensor_copy(out=bv_sb[:], in_=wstage[0:1, 0:1024])
        e.tensor_copy(out=bf_sb[:], in_=wstage[32:33, 0:1024])
    ctx.vec(f_v0x)

    for kt in range(SK):
        def f_s(e, kt=kt):
            ctx.wait(e, "dve")
            ctx.wait(e, "act")
            e.dma_start(out=xt_sb[:], in_=xrow(kt * 128)).then_inc(dma, 16)
            ctx.cnt["dma"] += 16
        blk.sync(f_s)
        dn = ctx.n("dma")
        pe_prev = ctx.n("pe")

        def f_fence(e, dn=dn, pe_prev=pe_prev):
            e.wait_ge(dma, dn)
            e.wait_ge(pe, pe_prev)
        ctx.scal(f_fence)
        ctx.vec(f_fence)
        emit_ln(ctx, xt_sb[:], xtmp[:])
        emit_fence(ctx, ("scalar", "vector"))
        emit_ln(ctx, xtmp[:], xn_t[:], gamma_beta=(gt_sb[:], bt_sb[:]))
        a_ln = ctx.n("act")
        v_ln = ctx.n("dve")

        def f_t(e, a_ln=a_ln, v_ln=v_ln):
            e.wait_ge(act, a_ln)
            e.wait_ge(dve, v_ln)
            for k in range(8):
                ins = e.transpose(psT[:, k * 128:(k + 1) * 128],
                                  xn_t[:, k * 128:(k + 1) * 128], idbf_sb[:])
            ctx.inc(ins, "pe")
        blk.tensor(f_t)
        p_tr = ctx.n("pe")

        def f_v(e, kt=kt, p_tr=p_tr):
            e.wait_ge(pe, p_tr)
            e.tensor_copy(
                out=xlnfm.rearrange("p (k t) -> p k t", t=S)[:, :, kt * 128:(kt + 1) * 128],
                in_=psT[:].rearrange("p (k t) -> p k t", t=128))
        ctx.vec(f_v)

    for hp in range(8):
        for dc in range(8):
            emit_load_cast(ctx, w4_sb[:, dc * 128:(dc + 1) * 128],
                           wq_in[dc * 128:(dc + 1) * 128, hp * 128:(hp + 1) * 128],
                           wstage[:, 0:128])
            emit_load_cast(ctx, w4_sb[:, 1024 + dc * 128: 1024 + (dc + 1) * 128],
                           wk_in[dc * 128:(dc + 1) * 128, hp * 128:(hp + 1) * 128],
                           wstage[:, 0:128])
            emit_load_cast(ctx, w4_sb[:, 2048 + dc * 128: 2048 + (dc + 1) * 128],
                           wpos_in[dc * 128:(dc + 1) * 128, hp * 128:(hp + 1) * 128],
                           wstage[:, 0:128])
            emit_load_cast(ctx, w4_sb[:, 3072 + dc * 128: 3072 + (dc + 1) * 128],
                           wv_in[dc * 128:(dc + 1) * 128, hp * 128:(hp + 1) * 128],
                           wstage[:, 0:128])
        v_w = ctx.n("dve")

        def f_vm(e):
            ctx.wait(e, "pe")
            e.memset(val_pair, 1.0)
        ctx.vec(f_vm)
        for kt in range(SK):
            ps = psA if kt % 2 == 0 else psB

            def f_t(e, hp=hp, kt=kt, ps=ps, v_w=v_w):
                e.wait_ge(dve, v_w)
                ctx.psum_guard(e)
                e.matmul(ps[:, 0:128], ones_sb[:], bv_sb[:, hp * 128:(hp + 1) * 128],
                         start=True, stop=False)
                for dc in range(8):
                    ins = e.matmul(ps[:, 0:128], xlnfm[:, dc * S + kt * 128: dc * S + (kt + 1) * 128],
                                   w4_sb[:, 3072 + dc * 128: 3072 + (dc + 1) * 128],
                                   start=False, stop=(dc == 7))
                ctx.inc(ins, "pe")
            blk.tensor(f_t)
            p_v = ctx.n("pe")

            def f_v(e, kt=kt, ps=ps, p_v=p_v):
                e.wait_ge(pe, p_v)
                e.tensor_copy(
                    out=val_pair.rearrange("p (a b) -> p a b", b=65)[:, kt * 2:(kt + 1) * 2, 0:64],
                    in_=ps[:, 0:128].rearrange("p (a b) -> p a b", b=64))
            ctx.vec(f_v)

        for tc in range(NT):
            ps = psA if tc % 2 == 0 else psB

            def f_t(e, tc=tc, ps=ps, v_w=v_w):
                e.wait_ge(dve, v_w)
                ctx.psum_guard(e)
                for dc in range(8):
                    ins = e.matmul(ps[:], w4_sb[:, dc * 128:(dc + 1) * 128],
                                   xlnfm[:, dc * S + tc * 512: dc * S + (tc + 1) * 512],
                                   start=(dc == 0), stop=(dc == 7))
                ctx.inc(ins, "pe")
            blk.tensor(f_t)
            p_q = ctx.n("pe")

            def f_v(e, hp=hp, tc=tc, ps=ps, p_q=p_q):
                e.wait_ge(pe, p_q)
                for half in range(2):
                    src = ps[0:64, :] if half == 0 else ps[64:128, :]
                    hcol = 2 * hp + half
                    e.tensor_scalar(out=qv_sb[:, half * S + tc * 512: half * S + (tc + 1) * 512],
                                    in0=src, scalar1=bqP_sb[:, hcol:hcol + 1],
                                    scalar2=vP_sb[:, hcol:hcol + 1], op0=ALU.add, op1=ALU.add)
                    if tc * 512 in [g * 512 for g in chunks]:
                        qc = [g for g in chunks].index(tc)
                        e.tensor_scalar(out=qu_sb[:, half * MY + qc * 512: half * MY + (qc + 1) * 512],
                                        in0=src, scalar1=bqP_sb[:, hcol:hcol + 1],
                                        scalar2=uP_sb[:, hcol:hcol + 1], op0=ALU.add, op1=ALU.add)
            ctx.vec(f_v)
        for tc in range(NT):
            ps = psA if tc % 2 == 0 else psB

            def f_t(e, tc=tc, ps=ps, v_w=v_w):
                e.wait_ge(dve, v_w)
                ctx.psum_guard(e)
                for dc in range(8):
                    ins = e.matmul(ps[:], w4_sb[:, 1024 + dc * 128: 1024 + (dc + 1) * 128],
                                   xlnfm[:, dc * S + tc * 512: dc * S + (tc + 1) * 512],
                                   start=(dc == 0), stop=(dc == 7))
                ctx.inc(ins, "pe")
            blk.tensor(f_t)
            p_k = ctx.n("pe")

            def f_v(e, hp=hp, tc=tc, ps=ps, p_k=p_k):
                e.wait_ge(pe, p_k)
                for half in range(2):
                    src = ps[0:64, :] if half == 0 else ps[64:128, :]
                    hcol = 2 * hp + half
                    e.tensor_scalar(out=k_sb[:, half * S + tc * 512: half * S + (tc + 1) * 512],
                                    in0=src, scalar1=bkP_sb[:, hcol:hcol + 1],
                                    scalar2=None, op0=ALU.add)
            ctx.vec(f_v)
        for tc in range(NT):
            for dc in range(8):
                emit_load_cast(ctx, pe_st[:],
                               pef_in[dc * 128:(dc + 1) * 128, tc * 512:(tc + 1) * 512],
                               wstage[:, 0:512])
                v_pe = ctx.n("dve")

                def f_t(e, tc=tc, dc=dc, v_pe=v_pe):
                    e.wait_ge(dve, v_pe)
                    if dc == 0:
                        ctx.psum_guard(e)
                    ins = e.matmul(psD[:], w4_sb[:, 2048 + dc * 128: 2048 + (dc + 1) * 128],
                                   pe_st[:], start=(dc == 0), stop=(dc == 7))
                    ctx.inc(ins, "pe")
                blk.tensor(f_t)
            p_p = ctx.n("pe")

            def f_v(e, tc=tc, p_p=p_p):
                e.wait_ge(pe, p_p)
                e.tensor_copy(out=p_sb[:, tc * 512:(tc + 1) * 512], in_=psD[0:64, :])
                e.tensor_copy(out=p_sb[:, S + tc * 512: S + (tc + 1) * 512], in_=psD[64:128, :])
            ctx.vec(f_v)

        for hh in range(2):
            qoff_u = hh * MY
            off = hh * S
            h_glob = hp * 2 + hh

            def f_vz(e):
                ctx.wait(e, "pe")
                e.memset(urow_sb[:, 0:1], 0.0)
            ctx.vec(f_vz)
            rows_done = set()
            for g in chunks:
                tiles = [g * 512 + t * 128 for t in range(4)]
                if (g + 1) * 512 < S and (g + 1) not in chunks:
                    tiles.append((g + 1) * 512)
                elif (g + 1) * 512 < S and (g + 1) in chunks:
                    pass
                for r0 in tiles:
                    if r0 in rows_done:
                        continue
                    rows_done.add(r0)
                    only1 = (r0 % 512 == 0 and r0 // 512 not in chunks)
                    for mc in range(NT):
                        ps = psA if mc % 2 == 0 else psB

                        def f_t(e, r0=r0, mc=mc, ps=ps, off=off):
                            ctx.psum_guard(e)
                            ins = e.matmul(ps[:], qv_sb[:, off + r0: off + r0 + 128],
                                           p_sb[:, off + mc * 512: off + (mc + 1) * 512],
                                           start=True, stop=True)
                            ctx.inc(ins, "pe")
                        blk.tensor(f_t)
                        p_u = ctx.n("pe")

                        def f_v(e, mc=mc, ps=ps, p_u=p_u):
                            e.wait_ge(pe, p_u)
                            e.wait_ge(ctx.sems["st"], ctx.cnt["st"])
                            e.tensor_copy(out=urow_sb[:, 1 + mc * 512: 1 + (mc + 1) * 512],
                                          in_=ps[:])
                        ctx.vec(f_v)
                    v_u = ctx.n("dve")

                    def f_st(e, r0=r0, only1=only1, v_u=v_u):
                        e.wait_ge(dve, v_u)
                        nr = 1 if only1 else 128
                        e.dma_start(out=F_hbm[r0:r0 + nr, :], in_=urow_sb[0:nr, :]).then_inc(st, 16)
                        ctx.cnt["st"] += 16
                        e.wait_ge(st, ctx.cnt["st"])
                    blk.sync(f_st)

            def f_nrm_init(e):
                e.memset(nrm_sb[0:1, 0:4], 0.0)
            ctx.vec(f_nrm_init)
            for (src, ncols, col, soff) in ((qu_sb, NCH, 0, qoff_u), (k_sb, NT, 1, off),
                                            (qv_sb, NT, 2, off), (p_sb, NT, 3, off)):
                for cc in range(ncols):
                    def f_sc(e, src=src, cc=cc, soff=soff):
                        ctx.wait(e, "dve")
                        ctx.wait(e, "pe")
                        e.activation(out=sqt_sb[:], in_=src[:, soff + cc * 512: soff + (cc + 1) * 512],
                                     func=AF.Square)
                    ctx.scal(f_sc)
                    a_sq = ctx.n("act")

                    def f_t(e, a_sq=a_sq):
                        e.wait_ge(act, a_sq)
                        ctx.psum_guard(e)
                        ins = e.matmul(psD[0:1, :], onescol[0:64, :], sqt_sb[:],
                                       start=True, stop=True)
                        ctx.inc(ins, "pe")
                    blk.tensor(f_t)
                    p_n = ctx.n("pe")

                    def f_v(e, col=col, p_n=p_n):
                        e.wait_ge(pe, p_n)
                        e.reduce_max(out=nrm_sb[0:1, 4:5], in_=psD[0:1, :], axis=AX.X)
                        e.tensor_max(out=nrm_sb[0:1, col:col + 1], in0=nrm_sb[0:1, col:col + 1],
                                     in1=nrm_sb[0:1, 4:5])
                    ctx.vec(f_v)

            def f_v(e):
                e.tensor_mul(out=nrm_sb[0:1, 5:6], in0=nrm_sb[0:1, 0:1], in1=nrm_sb[0:1, 1:2])
                e.tensor_mul(out=nrm_sb[0:1, 6:7], in0=nrm_sb[0:1, 2:3], in1=nrm_sb[0:1, 3:4])
            ctx.vec(f_v)
            v_n = ctx.n("dve")

            def f_sc(e, v_n=v_n):
                e.wait_ge(dve, v_n)
                e.activation(out=nrm_sb[0:1, 7:8], in_=nrm_sb[0:1, 5:6], func=AF.Ln, bias=EPS)
                e.activation(out=nrm_sb[0:1, 8:9], in_=nrm_sb[0:1, 7:8], func=AF.Exp, scale=0.5)
                e.activation(out=nrm_sb[0:1, 9:10], in_=nrm_sb[0:1, 6:7], func=AF.Ln, bias=EPS)
                e.activation(out=nrm_sb[0:1, 10:11], in_=nrm_sb[0:1, 9:10], func=AF.Exp, scale=0.5)
            ctx.scal(f_sc)
            a_b = ctx.n("act")

            def f_v(e, a_b=a_b):
                e.wait_ge(act, a_b)
                e.tensor_add(out=nrm_sb[0:1, 11:12], in0=nrm_sb[0:1, 8:9], in1=nrm_sb[0:1, 10:11])
                e.tensor_scalar(out=nrm_sb[0:1, 4:5], in0=nrm_sb[0:1, 11:12], scalar1=-0.03125,
                                scalar2=None, op0=ALU.mult)
            ctx.vec(f_v)
            v_nb = ctx.n("dve")

            def f_t(e, v_nb=v_nb):
                e.wait_ge(dve, v_nb)
                ctx.psum_guard(e)
                ins = e.matmul(psD[:, 0:1], ones32_sb[:], nrm_sb[0:1, 4:5], start=True, stop=True)
                ctx.inc(ins, "pe")
            blk.tensor(f_t)
            p_nb = ctx.n("pe")

            def f_v(e, p_nb=p_nb):
                e.wait_ge(pe, p_nb)
                e.tensor_copy(out=negb_sb[:], in_=psD[:, 0:1])
            ctx.vec(f_v)
            v_negb = ctx.n("dve")

            for qc in range(NCH):
                g = chunks[qc]
                i0 = g * 512
                for ktg in range(SK // 8):
                    def f_sbd(e, i0=i0, ktg=ktg):
                        ctx.wait(e, "dve")
                        ctx.wait(e, "pe")
                        Fv = F_flat.rearrange("(a b) -> a b", b=S)
                        for t in range(4):
                            r = i0 + 1 + t * 128
                            e.dma_start(
                                out=bd_sb[:, t * 1024:(t + 1) * 1024],
                                in_=Fv[r:r + 128, ktg * 1024:(ktg + 1) * 1024]
                            ).then_inc(dma, 16)
                        ctx.cnt["dma"] += 64
                    blk.sync(f_sbd)
                    d_bd = ctx.n("dma")

                    for ktl in range(8):
                        kt = ktg * 8 + ktl
                        psS = psA if kt % 2 == 0 else psB
                        slot = kt % 2

                        def f_vc(e, ktl=ktl, d_bd=d_bd):
                            e.wait_ge(dma, d_bd)
                            ctx.wait(e, "pe")
                            e.tensor_copy(out=bdf_sb[:],
                                          in_=bd_sb[:].rearrange("p (t m) -> p t m", m=1024)[:, :, ktl * 128:(ktl + 1) * 128])
                        ctx.vec(f_vc)
                        v_bdf = ctx.n("dve")

                        def f_t1(e, off=off, qoff_u=qoff_u, kt=kt, qc=qc, psS=psS, v_bdf=v_bdf):
                            e.wait_ge(dve, v_bdf)
                            ctx.psum_guard(e)
                            for t in range(4):
                                e.matmul(psS[:, t * 128:(t + 1) * 128],
                                         k_sb[:, off + kt * 128: off + (kt + 1) * 128],
                                         qu_sb[:, qoff_u + qc * 512 + t * 128:
                                                qoff_u + qc * 512 + (t + 1) * 128],
                                         start=True, stop=False)
                                ins = e.matmul(psS[:, t * 128:(t + 1) * 128],
                                               bdf_sb[:, t * 128:(t + 1) * 128], id32_sb[:],
                                               is_transpose=True, start=False, stop=True)
                            ctx.inc(ins, "pe")
                        blk.tensor(f_t1)
                        p_s = ctx.n("pe")

                        def f_sc(e, psS=psS, slot=slot, p_s=p_s, v_negb=v_negb):
                            e.wait_ge(pe, p_s)
                            e.wait_ge(dve, v_negb)
                            e.activation(out=pexp_sb[:, slot * 512:(slot + 1) * 512], in_=psS[:],
                                         func=AF.Exp, bias=negb_sb[:], scale=0.03125)
                        ctx.scal(f_sc)
                        a_exp = ctx.n("act")

                        def f_t2(e, kt=kt, slot=slot, h_glob=h_glob, a_exp=a_exp):
                            e.wait_ge(act, a_exp)
                            if kt == 0:
                                ctx.wait(e, "dve")
                            ins = e.matmul(psCtx[0:65, :],
                                           val_pair[:, kt * 130 + hh * 65: kt * 130 + (hh + 1) * 65],
                                           pexp_sb[:, slot * 512:(slot + 1) * 512],
                                           start=(kt == 0), stop=(kt == SK - 1))
                            ctx.inc(ins, "pe")
                        blk.tensor(f_t2)
                p_ctx = ctx.n("pe")

                def f_v(e, p_ctx=p_ctx):
                    e.wait_ge(pe, p_ctx)
                    e.reciprocal(out=rc_sb[:], in_=psCtx[64:65, :])
                ctx.vec(f_v)
                v_rc = ctx.n("dve")

                def f_t3(e, v_rc=v_rc):
                    e.wait_ge(dve, v_rc)
                    ctx.wait(e, "dve")
                    ins = e.matmul(psD[0:64, :], ones32_sb[:, 0:64], rc_sb[:], start=True, stop=True)
                    ctx.inc(ins, "pe")
                blk.tensor(f_t3)
                p_rcb = ctx.n("pe")

                def f_v2(e, p_rcb=p_rcb):
                    e.wait_ge(pe, p_rcb)
                    e.wait_ge(ctx.sems["st"], ctx.cnt["st"])
                    e.tensor_copy(out=rcb_sb[:], in_=psD[0:64, :])
                    e.tensor_mul(out=yt_sb[:], in0=psCtx[0:64, :], in1=rcb_sb[:])
                ctx.vec(f_v2)
                v_y = ctx.n("dve")

                def f_st(e, hp=hp, hh=hh, qc=qc, v_y=v_y):
                    e.wait_ge(dve, v_y)
                    e.dma_start(out=ydm_hbm[hh * 64: hh * 64 + 64,
                                            hp * MY + qc * 512: hp * MY + (qc + 1) * 512],
                                in_=yt_sb[:]).then_inc(st, 16)
                    ctx.cnt["st"] += 16
                    e.wait_ge(st, ctx.cnt["st"])
                blk.sync(f_st)

    for rc in range(8):
        emit_load_cast(ctx, wf_sb[:, rc * D:(rc + 1) * D],
                       wf_in[rc * 128:(rc + 1) * 128, :], wstage[:, 0:D])
    for ci in range(NCH):
        g = chunks[ci]
        for tt in range(4):
            def f_s(e, ci=ci, g=g, tt=tt):
                ctx.wait(e, "dve")
                ctx.wait(e, "act")
                for hp in range(8):
                    e.dma_start(out=ydm_t[:, hp * 128:(hp + 1) * 128],
                                in_=ydm_hbm[:, hp * MY + ci * 512 + tt * 128:
                                            hp * MY + ci * 512 + (tt + 1) * 128]).then_inc(dma, 16)
                e.dma_start(out=xt_sb[:],
                            in_=xrow(g * 512 + tt * 128)).then_inc(dma, 16)
                ctx.cnt["dma"] += 16 * 9
            blk.sync(f_s)
            dn = ctx.n("dma")

            for oc in range(2):
                ps = psA if oc == 0 else psB

                def f_t(e, oc=oc, ps=ps, dn=dn):
                    e.wait_ge(dma, dn)
                    ctx.psum_guard(e)
                    e.matmul(ps[:], ones_sb[:], bf_sb[:, oc * 512:(oc + 1) * 512],
                             start=True, stop=False)
                    for hp in range(8):
                        ins = e.matmul(ps[:], ydm_t[:, hp * 128:(hp + 1) * 128],
                                       wf_sb[:, hp * 1024 + oc * 512: hp * 1024 + (oc + 1) * 512],
                                       start=False, stop=(hp == 7))
                    ctx.inc(ins, "pe")
                blk.tensor(f_t)
            p_o = ctx.n("pe")

            def f_v(e, p_o=p_o):
                e.wait_ge(pe, p_o)
                e.wait_ge(ctx.sems["st"], ctx.cnt["st"])
                e.tensor_copy(out=attn_sb[:, 0:512], in_=psA[:])
                e.tensor_copy(out=attn_sb[:, 512:1024], in_=psB[:])
            ctx.vec(f_v)

            emit_fence(ctx, ("scalar", "vector"))
            emit_ln(ctx, attn_sb[:], ctx.ln_sq[:, 0:D])

            def f_v2(e):
                e.tensor_add(out=outt[:], in0=xt_sb[:], in1=ctx.ln_sq[:, 0:D])
            ctx.vec(f_v2)
            v_o = ctx.n("dve")

            def f_st(e, ci=ci, tt=tt, v_o=v_o):
                e.wait_ge(dve, v_o)
                e.dma_start(out=x_out[ci * 512 + tt * 128: ci * 512 + (tt + 1) * 128, :],
                            in_=outt[:]).then_inc(st, 16)
                ctx.cnt["st"] += 16
                e.wait_ge(st, ctx.cnt["st"])
            blk.sync(f_st)


def pe_table(S):
    pos = np.arange(S, dtype=np.float32)
    inv_freq = (1.0 / 10000.0 ** (np.arange(0.0, D, 2.0, dtype=np.float32) / D)).astype(np.float32)
    ang = pos[:, None] * inv_freq[None, :]
    enc = np.zeros((S, D), np.float32)
    enc[:, 0::2] = np.sin(ang)
    enc[:, 1::2] = np.cos(ang)
    return enc


def l1a_inmaps(x_b, inputs, par, S):
    return {
        "x2f_in": np.ascontiguousarray(x_b),
        "Wq": np.ascontiguousarray(inputs["L1_attn_Wq"]),
        "Wk": np.ascontiguousarray(inputs["L1_attn_Wk"]),
        "Wv": np.ascontiguousarray(inputs["L1_attn_Wv"]),
        "Wpos": np.ascontiguousarray(inputs["L1_attn_Wpos"]),
        "Wf": np.ascontiguousarray(inputs["L1_attn_Wf"]),
        "bqP": np.ascontiguousarray(inputs["L1_attn_bq"].reshape(16, 64).T),
        "bkP": np.ascontiguousarray(inputs["L1_attn_bk"].reshape(16, 64).T),
        "bvL1_r": np.ascontiguousarray(inputs["L1_attn_bv"].reshape(1, D)),
        "bf_r": np.ascontiguousarray(inputs["L1_attn_bf"].reshape(1, D)),
        "uP": np.ascontiguousarray(inputs["L1_attn_u"].reshape(16, 64).T),
        "vP": np.ascontiguousarray(inputs["L1_attn_v"].reshape(16, 64).T),
        "pef": np.ascontiguousarray(pe_table(S).T),
        "gt": np.ascontiguousarray(np.tile(inputs["L1_attn_gamma"].reshape(1, D), (128, 1))),
        "bt": np.ascontiguousarray(np.tile(inputs["L1_attn_beta"].reshape(1, D), (128, 1))),
        "ones_f": np.ones((1, 128), dtype=np.float32),
        "id_f": np.eye(128, dtype=np.float32),
    }


def l0fa_inmaps(x_b, inputs, par, S):
    cw = inputs["L0_ffn_conv_w"]
    wqkv = inputs["L0_ffn_wqkv"]
    bqkv = inputs["L0_ffn_bqkv"]
    return {
        "x1f_in": np.ascontiguousarray(x_b),
        "cwT": np.ascontiguousarray(cw.transpose(2, 1, 0).reshape(3 * D, D)),
        "cbP": np.ascontiguousarray(inputs["L0_ffn_conv_b"].reshape(8, 128).T),
        "wqkvT": np.ascontiguousarray(wqkv.T),
        "bqkvP": np.ascontiguousarray(bqkv.reshape(24, 128).T),
        "bv_r": np.ascontiguousarray(bqkv[2048:3072].reshape(1, D)),
        "woT": np.ascontiguousarray(inputs["L0_ffn_wo"].T),
        "bo_r": np.ascontiguousarray(inputs["L0_ffn_bo"].reshape(1, D)),
        "g1t": np.ascontiguousarray(np.tile(inputs["L0_ffn_ln1_g"].reshape(1, D), (128, 1))),
        "b1t": np.ascontiguousarray(np.tile(inputs["L0_ffn_ln1_b"].reshape(1, D), (128, 1))),
        "ones_f": np.ones((1, 128), dtype=np.float32),
        "id_f": np.eye(128, dtype=np.float32),
    }


def l1f_inmaps(x_b, inputs, par, S):
    return {
        "x_in": np.ascontiguousarray(x_b),
        "m1": np.ascontiguousarray(inputs["L1_ffn_m1"]),
        "m2f": np.ascontiguousarray(inputs["L1_ffn_m2"].reshape(64, 4096)),
        "mbf": np.ascontiguousarray(inputs["L1_ffn_mb"].reshape(1, 4096)),
        "w1f": np.ascontiguousarray(inputs["L1_ffn_w1"].reshape(D, 4096)),
        "w2pf": np.ascontiguousarray(inputs["L1_ffn_w2"].transpose(1, 0, 2).reshape(4096, D)),
        "b2r": np.ascontiguousarray(inputs["L1_ffn_b2"].reshape(1, D)),
        "ones_f": np.ones((1, 128), dtype=np.float32),
        "id_f": np.eye(128, dtype=np.float32),
    }



ALL_CONSTS = [EPS, -0.5, 0.125, 0.5, -0.0025, 0.03125, -0.03125]


def emit_barrier(ctx):
    tgts = [(n, ctx.cnt[n]) for n in ("dma", "pe", "dve", "act", "st", "gp", "cc")
            if n in ctx.cnt and ctx.cnt[n] > 0]
    nc = ctx.nc
    for e in (nc.sync, nc.tensor, nc.vector, nc.scalar, nc.gpsimd):
        for n, v in tgts:
            e.wait_ge(ctx.sems[n], v)


def emit_gather(ctx, cin, cout):
    emit_barrier(ctx)
    nc = ctx.nc
    nc.gpsimd.collective_compute(
        "AllGather", ALU.bypass,
        replica_groups=[[0, 1], [2, 3], [4, 5], [6, 7]],
        ins=[cin], outs=[cout]).then_inc(ctx.sems["cc"], 1)
    ctx.cnt["cc"] += 1
    for e in (nc.sync, nc.tensor, nc.vector, nc.scalar, nc.gpsimd):
        e.wait_ge(ctx.sems["cc"], ctx.cnt["cc"])


def canon_row_map(S):
    nch = S // 512
    m = {}
    for p in (0, 1):
        for i, g in enumerate(my_chunks(p, nch)):
            m[g] = p * (S // 2) + i * 512
    return lambda a: m[a // 512] + (a % 512)


def build_fused(S):
    ctx = Ctx("fused", fused=True)
    nc = ctx.nc
    MY = S // 2
    cr = canon_row_map(S)

    ccyar_in = ctx.dram_tmp("ccyar_in", (MY, D))
    ccyar_out = ctx.dram_tmp("ccyar_out", (S, D))
    cc1_in = ctx.dram_tmp("cc1_in", (MY, D))
    cc1_out = ctx.dram_tmp("cc1_out", (S, D))
    cc2_in = ctx.dram_tmp("cc2_in", (MY, D))
    cc2_out = ctx.dram_tmp("cc2_out", (S, D))
    yardm_i = ctx.dram_tmp("yardm_i", (128, 8 * MY))
    x2p_loc = ctx.dram_tmp("x2p_loc", (MY, D))
    x3_loc = ctx.dram_tmp("x3_loc", (MY, D))
    x_out = ctx.dram_out("x_out", (MY, D))

    for sname in ("dma", "pe", "dve", "act", "st", "gp", "cc"):
        ctx.sem(sname)

    engines = [nc.sync, nc.tensor, nc.vector, nc.scalar, nc.gpsimd]
    ALLE = [e.engine for e in engines]
    with nc.body("main", valid_engines=ALLE):
        idx = []
        for e in engines:
            pid = e.partition_id()
            idx.append(e.compute_val(pid % 2))
        ctx.start_block(consts=ALL_CONSTS)
        ctx._sb_base = (ctx._sb_bump + 31) & ~31
        snap = ctx.snapshot_cnt()

        for arm in nc.Switch(engines=engines, index=idx, n=2):
            par = arm
            ctx.restore_cnt(snap)
            nch = S // 512
            chunks = my_chunks(par, nch)
            loc = {g: i * 512 for i, g in enumerate(chunks)}

            ctx.stage_reset()
            emit_l0a1(ctx, par, S, io={"yar_out": yardm_i, "ccyar": ccyar_in})
            emit_gather(ctx, ccyar_in, ccyar_out)

            ctx.stage_reset()
            emit_l0a2(ctx, par, S, io={
                "yarrow": (lambda a: ccyar_out[cr(a):cr(a) + 128, :]),
                "yardm": yardm_i,
                "x_out": cc1_in,
            })
            emit_gather(ctx, cc1_in, cc1_out)

            ctx.stage_reset()
            emit_l0fa(ctx, par, S, io={
                "xrow": (lambda a: cc1_out[cr(a):cr(a) + 128, :]),
                "x_out": x2p_loc,
            })
            emit_barrier(ctx)

            ctx.stage_reset()
            emit_l0fb(ctx, par, S, io={
                "x2p": x2p_loc,
                "x1": cc1_in,
                "x_out": cc2_in,
            })
            emit_gather(ctx, cc2_in, cc2_out)

            ctx.stage_reset()
            emit_l1a(ctx, par, S, io={
                "xrow": (lambda a: cc2_out[cr(a):cr(a) + 128, :]),
                "x_out": x3_loc,
            })
            emit_barrier(ctx)

            ctx.stage_reset()
            emit_l1f(ctx, par, S, io={
                "xrow": (lambda a, loc=loc: x3_loc[loc[a // 512] + (a % 512):
                                                   loc[a // 512] + (a % 512) + 128, :]),
                "x_out": x_out,
            })
    return ctx


def fused_inmaps(x_b, inputs, par, S):
    m = {}
    m.update(l0a1_inmaps(x_b, inputs, par, S))
    m.update(l0a2_inmaps(x_b, np.zeros((1, 1), np.float32), np.zeros((1, 1), np.float32),
                         inputs, par, S))
    m.pop("yar", None)
    m.pop("yardm", None)
    m.update({k: v for k, v in l0fa_inmaps(x_b, inputs, par, S).items() if k != "x1f_in"})
    x1d = l0fb_inmaps(np.zeros((1, 1), np.float32), np.zeros((1, 1), np.float32),
                      inputs, par, S)
    x1d.pop("x2p"); x1d.pop("x1")
    m.update(x1d)
    m.update({k: v for k, v in l1a_inmaps(x_b, inputs, par, S).items() if k != "x2f_in"})
    l1fd = l1f_inmaps(x_b, inputs, par, S)
    l1fd.pop("x_in")
    m.update(l1fd)
    return m


def kernel_fused(inputs):
    x = np.asarray(inputs["L0_attn_x"], dtype=np.float32)
    S = x.shape[1]
    ctx = build_fused(S)
    in_maps = []
    for c in range(8):
        b, par = c // 2, c % 2
        in_maps.append(fused_inmaps(x[b], inputs, par, S))
    res = run_bass_kernel_spmd(ctx.nc, in_maps, core_ids=list(range(8)))
    ctx.close()
    out = np.empty_like(x)
    for c in range(8):
        b, par = c // 2, c % 2
        chunks = my_chunks(par, S // 512)
        xo = res.results[c]["x_out"]
        for ci, g in enumerate(chunks):
            out[b, g * 512:(g + 1) * 512, :] = xo[ci * 512:(ci + 1) * 512, :]
    return out


def run_block_pair(build_fn, inmap_fn, x, inputs, S):
    x = np.ascontiguousarray(x, dtype=np.float32)
    out = x.copy()
    for par in (0, 1):
        ctx = build_fn(par, S)
        chunks = my_chunks(par, S // 512)
        in_maps = [inmap_fn(x[b], inputs, par, S) for b in range(B)]
        res = run_bass_kernel_spmd(ctx.nc, in_maps, core_ids=list(range(B)))
        ctx.close()
        for b in range(B):
            xo = res.results[b]["x_out"]
            for ci, g in enumerate(chunks):
                out[b, g * 512:(g + 1) * 512, :] = xo[ci * 512:(ci + 1) * 512, :]
    return out


def run_l0a(x, inputs, S):
    x = np.ascontiguousarray(x, dtype=np.float32)
    yar_res = {}
    for par in (0, 1):
        ctx = build_l0a1(par, S)
        in_maps = [l0a1_inmaps(x[b], inputs, par, S) for b in range(B)]
        res = run_bass_kernel_spmd(ctx.nc, in_maps, core_ids=list(range(B)))
        ctx.close()
        yar_res[par] = [res.results[b]["yar_dm"] for b in range(B)]
    yar = yar_assemble(yar_res, S)
    out = x.copy()
    for par in (0, 1):
        ctx = build_l0a2(par, S)
        chunks = my_chunks(par, S // 512)
        in_maps = [l0a2_inmaps(x[b], yar[b], yar_res[par][b], inputs, par, S) for b in range(B)]
        res = run_bass_kernel_spmd(ctx.nc, in_maps, core_ids=list(range(B)))
        ctx.close()
        for b in range(B):
            xo = res.results[b]["x_out"]
            for ci, g in enumerate(chunks):
                out[b, g * 512:(g + 1) * 512, :] = xo[ci * 512:(ci + 1) * 512, :]
    return out


def run_l0f(x1, inputs, S):
    x1 = np.ascontiguousarray(x1, dtype=np.float32)
    out = x1.copy()
    for par in (0, 1):
        chunks = my_chunks(par, S // 512)
        ctx = build_l0fa(par, S)
        in_maps = [l0fa_inmaps(x1[b], inputs, par, S) for b in range(B)]
        res = run_bass_kernel_spmd(ctx.nc, in_maps, core_ids=list(range(B)))
        ctx.close()
        x2p = [res.results[b]["x_out"] for b in range(B)]
        ctx = build_l0fb(par, S)
        in_maps = []
        for b in range(B):
            x1rows = np.concatenate([x1[b, g * 512:(g + 1) * 512] for g in chunks], axis=0)
            in_maps.append(l0fb_inmaps(x2p[b], x1rows, inputs, par, S))
        res = run_bass_kernel_spmd(ctx.nc, in_maps, core_ids=list(range(B)))
        ctx.close()
        for b in range(B):
            xo = res.results[b]["x_out"]
            for ci, g in enumerate(chunks):
                out[b, g * 512:(g + 1) * 512, :] = xo[ci * 512:(ci + 1) * 512, :]
    return out


def kernel(**inputs):
    return kernel_fused(inputs)


def kernel_unfused(**inputs):
    x = np.asarray(inputs["L0_attn_x"], dtype=np.float32)
    S = x.shape[1]
    x = run_l0a(x, inputs, S)
    x = run_l0f(x, inputs, S)
    x = run_block_pair(build_l1a, l1a_inmaps, x, inputs, S)
    x = run_block_pair(build_l1f, l1f_inmaps, x, inputs, S)
    return x
